# version 05
# baseline (speedup 1.0000x reference)
import math
from contextlib import ExitStack
import numpy as np
import ml_dtypes
import concourse.bass as bass
import concourse.mybir as mybir
from concourse.bass_utils import run_bass_kernel_spmd
from concourse.ap import AP

F32 = mybir.dt.float32
BF16 = mybir.dt.bfloat16
I32 = mybir.dt.int32
AF = mybir.ActivationFunctionType
ALU = mybir.AluOpType
AX = mybir.AxisListType.X

BF = ml_dtypes.bfloat16

HID = 2048
SEQ = 2048
B = 4
NCORE = 8
TOK = 1024
NH = 32
DK = 64
HPC = 16
INTER0 = 5504
DFF1 = 5461
DFF1P = 5504
NBUCK = 32
LN_EPS = 1e-5
GN_EPS = 1e-5 * 64.0
CHUNK = 64
NCHUNK = SEQ // CHUNK
CAP_K = 1024
DFF1P = 5504



def _rel_bucket_table():
    d = np.arange(-2047, 2048)
    rp = np.maximum(d, 0)
    max_exact = NBUCK // 2
    is_small = rp < max_exact
    with np.errstate(divide="ignore"):
        rp_large = max_exact + (
            np.log(np.maximum(rp, 1).astype(np.float32) / max_exact)
            / math.log(128 / max_exact) * (NBUCK - max_exact)
        ).astype(np.int32)
    rp_large = np.minimum(rp_large, NBUCK - 1)
    bucket = np.where(is_small, rp, rp_large)
    M_T = np.zeros((NBUCK, 4095), np.float32)
    M_T[bucket, np.arange(4095)] = 1.0
    return M_T


def _threefry2x32(k0, k1, x0, x1):
    def rotl(v, d):
        return ((v << np.uint32(d)) | (v >> np.uint32(32 - d))).astype(np.uint32)
    rot = [13, 15, 26, 6, 17, 29, 16, 24]
    ks0, ks1 = np.uint32(k0), np.uint32(k1)
    ks2 = np.uint32(ks0 ^ ks1 ^ np.uint32(0x1BD11BDA))
    x0 = (x0 + ks0).astype(np.uint32)
    x1 = (x1 + ks1).astype(np.uint32)
    keys = [(ks1, ks2), (ks2, ks0), (ks0, ks1), (ks1, ks2), (ks2, ks0)]
    for g in range(5):
        for r in (rot[0:4] if g % 2 == 0 else rot[4:8]):
            x0 = (x0 + x1).astype(np.uint32)
            x1 = rotl(x1, r)
            x1 = (x1 ^ x0).astype(np.uint32)
        a, b = keys[g]
        x0 = (x0 + a).astype(np.uint32)
        x1 = (x1 + b + np.uint32(g + 1)).astype(np.uint32)
    return x0, x1


def _gumbel_const():
    n = B * SEQ * 8
    i = np.arange(n, dtype=np.uint32)
    o0, o1 = _threefry2x32(0, 42, np.zeros(n, np.uint32), i)
    bits = (o0 ^ o1).astype(np.uint32)
    u = (((bits >> np.uint32(9)) | np.uint32(0x3F800000)).view(np.float32)
         - np.float32(1.0)).reshape(B * SEQ, 8)
    eps = np.float32(1e-10)
    return (-np.log(-np.log(u + eps) + eps)).astype(np.float32)


def _new_nc():
    return bass.Bass("TRN2", target_bir_lowering=False, debug=False,
                     detect_race_conditions=False)


class Chain:

    def __init__(self, eng, sem):
        self.eng = eng
        self.sem = sem
        self.n = 0

    def __call__(self, inst):
        self.n += 16
        inst.then_inc(self.sem, 16)
        self.eng.wait_ge(self.sem, self.n)
        return inst


class Alloc:

    def __init__(self, nc):
        self.nc = nc
        self.es = ExitStack()

    def sb(self, name, shape, dtype):
        return self.es.enter_context(self.nc.sbuf_tensor(name, shape, dtype))

    def ps(self, name, shape, dtype=F32):
        return self.es.enter_context(self.nc.psum_tensor(name, shape, dtype))

    def sem(self, name):
        return self.es.enter_context(self.nc.semaphore(name))

    def block(self):
        return self.es.enter_context(self.nc.Block())

    def close(self):
        self.es.close()



def build_A():
    nc = _new_nc()
    x0 = nc.dram_tensor("x0", [TOK, HID], F32, kind="ExternalInput").ap()
    gT_idx = nc.dram_tensor("gT_idx", [256, INTER0], I32, kind="ExternalInput").ap()
    uT_idx = nc.dram_tensor("uT_idx", [256, INTER0], I32, kind="ExternalInput").ap()
    d_idx = nc.dram_tensor("d_idx", [688, HID], I32, kind="ExternalInput").ap()
    lut = nc.dram_tensor("lut", [1, 16], F32, kind="ExternalInput").ap()

    xn_cm = nc.dram_tensor("xn_cm", [HID, TOK], BF16, kind="ExternalOutput").ap()
    WgT = nc.dram_tensor("WgT", [256, INTER0], BF16, kind="ExternalOutput").ap()
    WuT = nc.dram_tensor("WuT", [256, INTER0], BF16, kind="ExternalOutput").ap()
    Wd = nc.dram_tensor("Wd", [688, HID], BF16, kind="ExternalOutput").ap()
    dbg = nc.dram_tensor("dbg", [128, 32], F32, kind="ExternalOutput").ap()

    mat_tiles = []
    for src, dst, rows, cols in ((gT_idx, WgT, 256, INTER0),
                                 (uT_idx, WuT, 256, INTER0),
                                 (d_idx, Wd, 688, HID)):
        r = 0
        while r < rows:
            pr = min(128, rows - r)
            mat_tiles.append((src, dst, r, pr, cols))
            r += pr
    NMT = len(mat_tiles)

    al = Alloc(nc)
    xt = al.sb("xt", [128, 2, HID], F32)
    xsq = al.sb("xsq", [128, HID], F32)
    rstd = al.sb("rstd", [128, 8], F32)
    sc = al.sb("sc", [128, 6], F32)
    xnb = al.sb("xnb", [128, 8 * HID], BF16)
    xncm = al.sb("xncm", [128, 16 * TOK], BF16)
    ident = al.sb("ident", [128, 128], BF16)
    ioti = al.sb("ioti", [128, 128], I32)
    lutsb = al.sb("lutsb", [1, 16], F32)
    lutb = al.sb("lutb", [1, 16], BF16)
    lutrep = al.sb("lutrep", [128, 16], F32)
    ones1 = al.sb("ones1", [1, 128], BF16)
    idxi = al.sb("idxi", [128, 2, INTER0], I32)
    idxb = al.sb("idxb", [128, INTER0], BF16)
    wmat = al.sb("wmat", [128, INTER0], BF16)
    term = al.sb("term", [128, INTER0], BF16)
    ptr0 = al.ps("ptr0", [128, 128], BF16)
    ptr1 = al.ps("ptr1", [128, 128], BF16)
    plut = al.ps("plut", [128, 16], F32)
    warm = al.sb("warm", [1, 8], F32)
    wrm0 = al.sem("wrm0"); wrm1 = al.sem("wrm1"); wrm2 = al.sem("wrm2")
    ldl = al.sem("ldl")
    ld0 = al.sem("ld0")
    ld1 = al.sem("ld1")
    ldm0 = al.sem("ldm0")
    ldm1 = al.sem("ldm1")
    va = al.sem("va")
    sa = al.sem("sa")
    vb = al.sem("vb")
    tr = al.sem("tr")
    ev = al.sem("ev")
    mt = al.sem("mt")
    ms = al.sem("ms")
    pre = al.sem("pre")
    so = al.sem("so")
    block = al.block()
    if True:
        @block.sync
        def _(sync):
            for w_ in range(3):
                sync.dma_start(out=warm[0:1, w_:w_ + 1],
                               in_=AP(lut.tensor, 0, [[1, 1], [1, 1]])).then_inc(wrm0, 16)
                sync.wait_ge(wrm0, 16 * (w_ + 1))
            sync.dma_start(out=lutsb[:], in_=lut[:, :]).then_inc(ldl, 16)
            for i in range(8):
                if i >= 2:
                    sync.wait_ge(vb, i - 1)
                sync.dma_start(out=xt[:, i % 2, :],
                               in_=x0[i * 128:(i + 1) * 128, :]).then_inc(ld0 if i % 2 == 0 else ld1, 16)
            for dc in range(16):
                sync.wait_ge(ev, (dc + 1) * 8)
                sync.dma_start(out=xn_cm[dc * 128:(dc + 1) * 128, :],
                               in_=xncm[:, dc * TOK:(dc + 1) * TOK]).then_inc(so, 16)
            sync.wait_ge(so, 16 * 16)
            sync.dma_start(out=dbg[:, 0:8], in_=xsq[:, 0:8]).then_inc(so, 16)
            sync.wait_ge(so, 16 * 17)

        @block.gpsimd
        def _(g):
            for w_ in range(3):
                g.dma_start(out=warm[0:1, 3 + w_:4 + w_],
                            in_=AP(lut.tensor, 0, [[1, 1], [1, 1]])).then_inc(wrm1, 16)
                g.wait_ge(wrm1, 16 * (w_ + 1))
            g.iota(ioti[:], pattern=[[1, 128]], base=0, channel_multiplier=-1).then_inc(pre, 1)
            g.memset(ones1[:], 1.0).then_inc(pre, 1)
            for j, (src, dst, r0, pr, cols) in enumerate(mat_tiles):
                if j >= 2:
                    g.wait_ge(mt, j - 1)
                g.dma_start(out=idxi[0:pr, j % 2, 0:cols],
                            in_=src[r0:r0 + pr, :]).then_inc(ldm0 if j % 2 == 0 else ldm1, 16)


        @block.vector
        def _(v):
            v.wait_ge(pre, 1)
            v.tensor_single_scalar(out=ident[:], in_=ioti[:], scalar=0,
                                   op=ALU.is_equal).then_inc(pre, 1)
            v.wait_ge(ldl, 16)
            v.tensor_copy(out=lutb[:], in_=lutsb[:]).then_inc(pre, 1)
            for i in range(8):
                v.wait_ge(ld0 if i % 2 == 0 else ld1, 16 * (i // 2 + 1))
                x = xt[:, i % 2, :]
                v.tensor_tensor(out=xsq[:], in0=x, in1=x, op=ALU.mult)
                v.reduce_sum(out=sc[:, 0:1], in_=x, axis=AX)
                v.reduce_sum(out=sc[:, 1:2], in_=xsq[:], axis=AX)
                v.tensor_scalar(out=sc[:, 0:1], in0=sc[:, 0:1], scalar1=1.0 / HID,
                                scalar2=None, op0=ALU.mult)
                v.tensor_scalar(out=sc[:, 1:2], in0=sc[:, 1:2], scalar1=1.0 / HID,
                                scalar2=None, op0=ALU.mult)
                v.tensor_tensor(out=sc[:, 2:3], in0=sc[:, 0:1], in1=sc[:, 0:1], op=ALU.mult)
                v.tensor_tensor(out=sc[:, 3:4], in0=sc[:, 1:2], in1=sc[:, 2:3],
                                op=ALU.subtract)
                v.tensor_scalar(out=sc[:, 4:5], in0=sc[:, 0:1], scalar1=-1.0,
                                scalar2=None, op0=ALU.mult)
                v.tensor_scalar(out=sc[:, 3:4], in0=sc[:, 3:4], scalar1=LN_EPS,
                                scalar2=None, op0=ALU.add).then_inc(va, 1)
                v.wait_ge(sa, i + 1)
                v.tensor_scalar(out=xnb[:, i * HID:(i + 1) * HID], in0=x,
                                scalar1=sc[:, 4:5], scalar2=rstd[:, i:i + 1],
                                op0=ALU.add, op1=ALU.mult).then_inc(vb, 1)
            v.tensor_copy(out=xsq[:, 0:8], in_=rstd[:, 0:8])
            for dc in range(16):
                for i in range(8):
                    j = dc * 8 + i
                    v.wait_ge(tr, j + 1)
                    v.tensor_copy(out=xncm[:, dc * TOK + i * 128: dc * TOK + (i + 1) * 128],
                                  in_=(ptr0 if j % 2 == 0 else ptr1)[:, :]).then_inc(ev, 1)
            v.wait_ge(pre, 5)
            v.tensor_copy(out=lutrep[:], in_=plut[:])
            for j, (src, dst, r0, pr, cols) in enumerate(mat_tiles):
                v.wait_ge(ldm0 if j % 2 == 0 else ldm1, 16 * (j // 2 + 1))
                if j >= 1:
                    v.wait_ge(ms, 16 * j)
                v.tensor_copy(out=idxb[0:pr, 0:cols], in_=idxi[0:pr, j % 2, 0:cols])
                v.tensor_scalar(out=wmat[0:pr, 0:cols], in0=idxb[0:pr, 0:cols],
                                scalar1=0.0, scalar2=lutrep[0:pr, 0:1],
                                op0=ALU.is_equal, op1=ALU.mult)
                for bb in range(1, 16):
                    v.tensor_scalar(out=term[0:pr, 0:cols], in0=idxb[0:pr, 0:cols],
                                    scalar1=float(bb), scalar2=lutrep[0:pr, bb:bb + 1],
                                    op0=ALU.is_equal, op1=ALU.mult)
                    inst = v.tensor_tensor(out=wmat[0:pr, 0:cols], in0=wmat[0:pr, 0:cols],
                                           in1=term[0:pr, 0:cols], op=ALU.add)
                inst.then_inc(mt, 1)

        @block.scalar
        def _(s):
            for w_ in range(2):
                s.dma_start(out=warm[0:1, 6 + w_:7 + w_],
                            in_=AP(lut.tensor, 0, [[1, 1], [1, 1]])).then_inc(wrm2, 16)
                s.wait_ge(wrm2, 16 * (w_ + 1))
            for i in range(8):
                s.wait_ge(va, i + 1)
                s.activation(out=sc[:, 5:6], in_=sc[:, 3:4], func=AF.Ln,
                             bias=0.0, scale=1.0)
                s.activation(out=rstd[:, i:i + 1], in_=sc[:, 5:6], func=AF.Exp,
                             bias=0.0, scale=-0.5).then_inc(sa, 1)
            for j, (_src, dst, r0, pr, cols) in enumerate(mat_tiles):
                s.wait_ge(mt, j + 1)
                s.dma_start(out=dst[r0:r0 + pr, :], in_=wmat[0:pr, 0:cols]).then_inc(ms, 16)
            s.wait_ge(ms, 16 * len(mat_tiles))

        @block.tensor
        def _(t):
            t.wait_ge(pre, 4)
            t.matmul(plut[:], ones1[:], lutb[:], start=True, stop=True).then_inc(pre, 1)
            for dc in range(16):
                for i in range(8):
                    j = dc * 8 + i
                    t.wait_ge(vb, i + 1)
                    if j >= 2:
                        t.wait_ge(ev, j - 1)
                    t.transpose((ptr0 if j % 2 == 0 else ptr1)[:, :],
                                xnb[:, i * HID + dc * 128: i * HID + dc * 128 + 128],
                                ident[:]).then_inc(tr, 1)
    al.close()
    return nc




def build_B():
    nc = _new_nc()
    xn = nc.dram_tensor("xn", [HID, SEQ], BF16, kind="ExternalInput").ap()
    wq = nc.dram_tensor("wq", [HID, 1024], F32, kind="ExternalInput").ap()
    wk = nc.dram_tensor("wk", [HID, 1024], F32, kind="ExternalInput").ap()
    wv = nc.dram_tensor("wv", [HID, 1024], F32, kind="ExternalInput").ap()
    rb = nc.dram_tensor("rb", [NBUCK, HPC], F32, kind="ExternalInput").ap()
    mt_ = nc.dram_tensor("mt", [NBUCK, 4095], F32, kind="ExternalInput").ap()
    o_cm = nc.dram_tensor("o_cm", [1024, SEQ], BF16, kind="ExternalOutput").ap()

    al = Alloc(nc)
    xnsb = al.sb("xnsb", [128, 16 * SEQ], BF16)
    wvsb = al.sb("wvsb", [128, 16 * 1024], BF16)
    mb = al.sb("mb", [2 * NBUCK, 4095], BF16)
    rbs = al.sb("rbs", [2 * NBUCK, HPC], BF16)
    rbsb = al.sb("rbsb", [NBUCK, HPC], F32)
    rbh = al.sb("rbh", [NBUCK, 2 * HPC], BF16)
    rbt = al.sb("rbt", [NBUCK, HPC], F32)
    rbl = al.sb("rbl", [NBUCK, HPC], BF16)
    tbc = al.sb("tbc", [HPC, 512], F32)
    tb = al.sb("tb", [HPC, 4095], BF16)
    bmax = al.sb("bmax", [HPC, 1], F32)
    bmaxb = al.sb("bmaxb", [HPC, 1], BF16)
    bmaxr = al.sb("bmaxr", [1, HPC], F32)
    tall = al.sb("tall", [128, 2, 4095], BF16)
    ident = al.sb("ident", [128, 128], BF16)
    ioti = al.sb("ioti", [128, 128], I32)
    ones1 = al.sb("ones1", [1, 128], BF16)
    wqt = al.sb("wqt", [128, 16 * 128], BF16)
    wkt = al.sb("wkt", [128, 16 * 128], BF16)
    qh = al.sb("qh", [128, 2, SEQ], BF16)
    kh = al.sb("kh", [128, 2, SEQ], BF16)
    vhat = al.sb("vhat", [128, HPC * 16 * 65], BF16)
    asb = al.sb("asb", [128, 2, 512], BF16)
    runmax = al.sb("runmax", [128, 1], F32)
    mcol = al.sb("mcol", [128, 1], F32)
    rmb = al.sb("rmb", [128, 1], BF16)
    msc = al.sb("msc", [1, 4], F32)
    mscb = al.sb("mscb", [1, 1], BF16)
    negm = al.sb("negm", [128, 1], F32)
    rrow = al.sb("rrow", [1, 512], F32)
    rrowb = al.sb("rrowb", [1, 512], BF16)
    rrsb = al.sb("rrsb", [64, 512], F32)
    osb = al.sb("osb", [64, 512], BF16)

    pp0 = al.ps("pp0", [128, 512], F32)
    pp1 = al.ps("pp1", [128, 512], F32)
    pp2 = al.ps("pp2", [128, 512], F32)
    pp3 = al.ps("pp3", [128, 512], F32)
    pav = al.ps("pav", [128, 512], F32)
    pms = al.ps("pms", [128, 512], F32)
    pmsb = al.ps("pmsb", [128, 128], BF16)

    ldxn = al.sem("ldxn"); ldw0 = al.sem("ldw0"); ldw1 = al.sem("ldw1")
    ldv = al.sem("ldv"); ldc0 = al.sem("ldc0"); ldc1 = al.sem("ldc1")
    tgd = al.sem("tgd"); tve = al.sem("tve"); ttb = al.sem("ttb"); tha = al.sem("tha")
    prj = al.sem("prj"); vprj = al.sem("vprj"); vvp = al.sem("vvp")
    gset = al.sem("gset")
    vset = al.sem("vset")
    sA = al.sem("sA"); vA = al.sem("vA"); sB = al.sem("sB"); eB = al.sem("eB")
    sAV = al.sem("sAV"); vO = al.sem("vO")
    mmx = al.sem("mmx"); vmx = al.sem("vmx"); vneg = al.sem("vneg")
    vrr = al.sem("vrr"); mrr = al.sem("mrr"); sto = al.sem("sto")

    block = al.block()
    NHP = HPC // 2

    if True:
        @block.sync
        def _(sync):
            sync.dma_start(out=rbsb[:], in_=rb[:, :]).then_inc(ldc0, 16)
            for i in range(16):
                sync.dma_start(out=xnsb[:, i * SEQ:(i + 1) * SEQ],
                               in_=xn[i * 128:(i + 1) * 128, :]).then_inc(ldxn, 16)
            n = 0
            for h in range(HPC):
                for qb in range(4):
                    n += 1
                    sync.wait_ge(vO, n)
                    sync.dma_start(out=o_cm[h * 64:(h + 1) * 64, qb * 512:(qb + 1) * 512],
                                   in_=osb[0:64, :]).then_inc(sto, 16)
                    sync.wait_ge(sto, 16 * n)

        @block.gpsimd
        def _(g):
            g.iota(ioti[:], pattern=[[1, 128]], base=0, channel_multiplier=-1).then_inc(gset, 1)
            g.memset(ones1[:], 1.0).then_inc(gset, 1)
            g.memset(qh[:], 0.0).then_inc(gset, 1)
            g.memset(kh[:], 0.0).then_inc(gset, 1)
            g.memset(vhat[:], 1.0).then_inc(gset, 1)
            g.memset(tall[:], 0.0).then_inc(gset, 1)
            g.dma_start(out=mb[0:NBUCK, :], in_=mt_[:, :]).then_inc(ldc1, 16)
            for dc in range(16):
                src_v = AP(wv.tensor, dc * 128 * 1024, [[1024, 128], [1, 1024]])
                g.dma_start(out=wvsb[:, dc * 1024:(dc + 1) * 1024],
                            in_=src_v).then_inc(ldv, 16)
            def tall_chain(h):
                buf = h % 2
                if h >= 2:
                    g.wait_ge(sB, 64 * (h - 1))
                base = 16 * 8 * h
                g.dma_start(out=tall[0:1, buf, 0:4095],
                            in_=tb[h:h + 1, 0:4095]).then_inc(tha, 16)
                for a in range(7):
                    w = 1 << a
                    g.wait_ge(tha, base + 16 * (a + 1))
                    g.dma_start(out=tall[w:2 * w, buf, w:4095],
                                in_=tall[0:w, buf, 0:4095 - w]).then_inc(tha, 16)

            g.wait_ge(ttb, 8)
            for hp in range(NHP):
                if hp >= 1:
                    g.wait_ge(prj, 32 + 8 * hp)
                src_q = AP(wq.tensor, hp * 128, [[1024, 128], [1024 * 128, 16], [1, 128]])
                src_k = AP(wk.tensor, hp * 128, [[1024, 128], [1024 * 128, 16], [1, 128]])
                g.dma_start(out=wqt[:].rearrange("p (c f) -> p c f", c=16),
                            in_=src_q).then_inc(ldw0, 16)
                g.dma_start(out=wkt[:].rearrange("p (c f) -> p c f", c=16),
                            in_=src_k).then_inc(ldw1, 16)
                tall_chain(2 * hp)
                tall_chain(2 * hp + 1)

        @block.scalar
        def _(s):
            n = 0
            for h in range(HPC):
                for qb in range(4):
                    s.wait_ge(vneg, n + 1)
                    for kt in range(16):
                        s.wait_ge(sB, 16 * n + kt + 1)
                        if kt >= 2:
                            s.wait_ge(sAV, 16 * n + kt - 1)
                        pb = pp2 if kt % 2 == 0 else pp3
                        s.activation(out=asb[:, kt % 2, :], in_=pb[:, :], func=AF.Exp,
                                     bias=negm[:, 0:1], scale=1.0).then_inc(eB, 1)
                    n += 1

        @block.vector
        def _(v):
            v.wait_ge(gset, 1)
            v.tensor_single_scalar(out=ident[:], in_=ioti[:], scalar=0,
                                   op=ALU.is_equal).then_inc(vset, 1)
            v.wait_ge(ldc0, 16)
            v.tensor_copy(out=rbh[:, 0:HPC], in_=rbsb[:]).then_inc(vset, 2)
            for ch in range(8):
                w = 512 if ch < 7 else 4095 - 512 * 7
                pt = pp0 if ch % 2 == 0 else pp1
                v.wait_ge(tgd, ch + 1)
                v.tensor_copy(out=tbc[:, 0:w], in_=pt[0:HPC, 0:w]).then_inc(tve, 1)
                if ch == 0:
                    v.reduce_max(out=bmax[:, 0:1], in_=tbc[:, 0:w], axis=AX)
                else:
                    v.reduce_max(out=msc[0:1, 0:1], in_=tbc[0:1, 0:1], axis=AX)
                    v.reduce_max(out=mcol[0:HPC, 0:1], in_=tbc[:, 0:w], axis=AX)
                    v.tensor_tensor(out=bmax[:, 0:1], in0=bmax[:, 0:1],
                                    in1=mcol[0:HPC, 0:1], op=ALU.max)
                v.tensor_copy(out=tb[:, ch * 512: ch * 512 + w],
                              in_=tbc[:, 0:w]).then_inc(ttb, 1)
            v.tensor_copy(out=bmaxb[:, 0:1], in_=bmax[:, 0:1]).then_inc(vset, 1)
            v.wait_ge(tgd, 9)
            v.tensor_copy(out=bmaxr[0:1, 0:HPC], in_=pmsb[0:1, 0:HPC]).then_inc(vset, 1)
            v.wait_ge(gset, 5)
            r = 0
            for cb in range(2):
                for tc in range(16):
                    pt = pp2 if r % 2 == 0 else pp3
                    v.wait_ge(prj, r + 1)
                    dst = AP(vhat, (0 * 16 + tc) * 65 + cb * 8 * 16 * 65,
                             [[HPC * 16 * 65, 128], [16 * 65, 8], [1, 64]])
                    v.tensor_copy(out=dst,
                                  in_=pt[:, :].rearrange("p (h f) -> p h f", h=8)).then_inc(vvp, 1)
                    r += 1
            v.wait_ge(gset, 4)
            n = 0
            for hp in range(NHP):
                for r in range(8):
                    qk = r // 4
                    tbk = r % 4
                    pt = pp0 if r % 2 == 0 else pp1
                    v.wait_ge(prj, 32 + 8 * hp + r + 1)
                    dsts = qh if qk == 0 else kh
                    v.tensor_copy(out=dsts[0:64, 0, tbk * 512:(tbk + 1) * 512],
                                  in_=pt[0:64, :]).then_inc(vprj, 1)
                    v.tensor_copy(out=dsts[64:128, 1, tbk * 512:(tbk + 1) * 512],
                                  in_=pt[64:128, :]).then_inc(vprj, 1)
                for hh in range(2):
                    h = 2 * hp + hh
                    for qb in range(4):
                        for kt in range(16):
                            pt = pp0 if kt % 2 == 0 else pp1
                            v.wait_ge(sA, 16 * n + kt + 1)
                            if kt == 0:
                                v.reduce_max(out=runmax[:, 0:1], in_=pt[:, :],
                                             axis=AX).then_inc(vA, 1)
                            else:
                                v.reduce_max(out=mcol[:, 0:1], in_=pt[:, :],
                                             axis=AX).then_inc(vA, 1)
                                v.tensor_tensor(out=runmax[:, 0:1], in0=runmax[:, 0:1],
                                                in1=mcol[:, 0:1], op=ALU.max)
                        v.tensor_copy(out=rmb[:, 0:1], in_=runmax[:, 0:1]).then_inc(vmx, 1)
                        v.wait_ge(mmx, 2 * n + 1)
                        v.reduce_max(out=msc[0:1, 0:1], in_=pmsb[0:1, 0:128], axis=AX)
                        v.tensor_tensor(out=msc[0:1, 1:2], in0=msc[0:1, 0:1],
                                        in1=bmaxr[0:1, h:h + 1], op=ALU.add)
                        v.tensor_scalar(out=msc[0:1, 2:3], in0=msc[0:1, 1:2],
                                        scalar1=-1.0, scalar2=None, op0=ALU.mult)
                        v.tensor_copy(out=mscb[0:1, 0:1], in_=msc[0:1, 2:3]).then_inc(vmx, 1)
                        v.wait_ge(mmx, 2 * n + 2)
                        v.tensor_copy(out=negm[:, 0:1], in_=pms[:, 0:1]).then_inc(vneg, 1)
                        v.wait_ge(sAV, 16 * (n + 1))
                        v.reciprocal(out=rrow[0:1, :], in_=pav[64:65, :])
                        v.tensor_copy(out=rrowb[0:1, :], in_=rrow[0:1, :]).then_inc(vrr, 1)
                        v.wait_ge(mrr, n + 1)
                        v.tensor_copy(out=rrsb[:, :], in_=pms[0:64, 0:512])
                        if n >= 1:
                            v.wait_ge(sto, 16 * n)
                        v.tensor_tensor(out=osb[:, :], in0=pav[0:64, :],
                                        in1=rrsb[:, :], op=ALU.mult).then_inc(vO, 1)
                        n += 1

        @block.tensor
        def _(t):
            t.wait_ge(vset, 2)
            t.wait_ge(gset, 4)
            t.wait_ge(ldc1, 16)
            for ch in range(8):
                pt = pp0 if ch % 2 == 0 else pp1
                if ch >= 2:
                    t.wait_ge(tve, ch - 1)
                w = 512 if ch < 7 else 4095 - 512 * 7
                t.matmul(pt[0:HPC, 0:w], rbh[:, 0:HPC], mb[0:NBUCK, ch * 512: ch * 512 + w],
                         start=True, stop=True).then_inc(tgd, 1)
            t.wait_ge(vset, 3)
            t.wait_ge(tve, 8)
            t.transpose(pmsb[0:1, 0:HPC], bmaxb[:, 0:1], ident[0:HPC, 0:HPC]).then_inc(tgd, 1)
            t.wait_ge(ldv, 256)
            t.wait_ge(ldxn, 256)
            r = 0
            for cb in range(2):
                for tc in range(16):
                    pt = pp2 if r % 2 == 0 else pp3
                    if r >= 2:
                        t.wait_ge(vvp, r - 1)
                    inst = None
                    for dc in range(16):
                        inst = t.matmul(pt[:, :],
                                        xnsb[:, dc * SEQ + tc * 128: dc * SEQ + (tc + 1) * 128],
                                        wvsb[:, dc * 1024 + cb * 512: dc * 1024 + (cb + 1) * 512],
                                        start=(dc == 0), stop=(dc == 15))
                    inst.then_inc(prj, 1)
                    r += 1
            n = 0
            for hp in range(NHP):
                t.wait_ge(ldw0, 16 * (hp + 1))
                t.wait_ge(ldw1, 16 * (hp + 1))
                for r in range(8):
                    qk = r // 4
                    tbk = r % 4
                    pt = pp0 if r % 2 == 0 else pp1
                    wt = wqt if qk == 0 else wkt
                    if hp == 0:
                        if r < 2:
                            t.wait_ge(tve, 8)
                        else:
                            t.wait_ge(vprj, 2 * (8 * hp + r - 1))
                    else:
                        if r < 2:
                            t.wait_ge(vA, 128 * hp)
                        else:
                            t.wait_ge(vprj, 2 * (8 * hp + r - 1))
                    inst = None
                    for dc in range(16):
                        inst = t.matmul(pt[:, :], wt[:, dc * 128:(dc + 1) * 128],
                                        xnsb[:, dc * SEQ + tbk * 512: dc * SEQ + tbk * 512 + 512],
                                        start=(dc == 0), stop=(dc == 15))
                    inst.then_inc(prj, 1)
                for hh in range(2):
                    h = 2 * hp + hh
                    for qb in range(4):
                        if n >= 1:
                            t.wait_ge(vO, n)
                        for kt in range(16):
                            pt = pp0 if kt % 2 == 0 else pp1
                            if n == 0 and kt < 2:
                                t.wait_ge(vprj, 16)
                            elif kt < 2:
                                t.wait_ge(vA, 16 * n - 2 + kt + 1)
                            else:
                                t.wait_ge(vA, 16 * n + kt - 1)
                            if kt < 2 and (hh == 0 and qb == 0):
                                t.wait_ge(vprj, 16 * (hp + 1))
                            t.matmul(pt[:, :], kh[:, hh, kt * 128:(kt + 1) * 128],
                                     qh[:, hh, qb * 512:(qb + 1) * 512],
                                     start=True, stop=True).then_inc(sA, 1)
                        t.wait_ge(vmx, 2 * n + 1)
                        if n >= 1:
                            t.wait_ge(vneg, n)
                        t.transpose(pmsb[0:1, 0:128], rmb[:, 0:1], ident[:]).then_inc(mmx, 1)
                        t.wait_ge(vmx, 2 * n + 2)
                        if n >= 1:
                            t.wait_ge(vO, n)
                        t.matmul(pms[:, 0:1], ones1[:, :], mscb[0:1, 0:1],
                                 start=True, stop=True).then_inc(mmx, 1)
                        for kt in range(16):
                            pt = pp2 if kt % 2 == 0 else pp3
                            if n == 0 and kt < 2:
                                t.wait_ge(vvp, 32)
                            elif kt < 2:
                                t.wait_ge(eB, 16 * n - 2 + kt + 1)
                            else:
                                t.wait_ge(eB, 16 * n + kt - 1)
                            m0 = qb * 512 - kt * 128 + 2047
                            t.matmul(pt[:, :], kh[:, hh, kt * 128:(kt + 1) * 128],
                                     qh[:, hh, qb * 512:(qb + 1) * 512],
                                     start=True, stop=False)
                            t.wait_ge(tha, 128 * (h + 1))
                            t.matmul(pt[:, :], ident[:, :],
                                     tall[:, h % 2, m0:m0 + 512],
                                     start=False, stop=True).then_inc(sB, 1)
                            t.wait_ge(eB, 16 * n + kt + 1)
                            if kt == 0 and n >= 1:
                                t.wait_ge(vO, n)
                            t.matmul(pav[0:65, :], vhat[:, (h * 16 + kt) * 65:(h * 16 + kt) * 65 + 65],
                                     asb[:, kt % 2, :],
                                     start=(kt == 0), stop=(kt == 15)).then_inc(sAV, 1)
                        t.wait_ge(vrr, n + 1)
                        t.matmul(pms[0:64, 0:512], ones1[0:1, 0:64], rrowb[0:1, :],
                                 start=True, stop=True).then_inc(mrr, 1)
                        n += 1
    al.close()
    return nc



def build_C():
    nc = _new_nc()
    o_cm = nc.dram_tensor("o_cm", [HID, TOK], BF16, kind="ExternalInput").ap()
    wo = nc.dram_tensor("wo", [HID, HID], F32, kind="ExternalInput").ap()
    xn0 = nc.dram_tensor("xn0", [HID, TOK], BF16, kind="ExternalInput").ap()
    WgT = nc.dram_tensor("WgT", [HID, INTER0], BF16, kind="ExternalInput").ap()
    WuT = nc.dram_tensor("WuT", [HID, INTER0], BF16, kind="ExternalInput").ap()
    Wd = nc.dram_tensor("Wd", [INTER0, HID], BF16, kind="ExternalInput").ap()
    x0 = nc.dram_tensor("x0", [TOK, HID], F32, kind="ExternalInput").ap()
    rw = nc.dram_tensor("rw", [HID, 8], F32, kind="ExternalInput").ap()
    rbv = nc.dram_tensor("rbv", [1, 8], F32, kind="ExternalInput").ap()
    gmb = nc.dram_tensor("gmb", [TOK, 8], F32, kind="ExternalInput").ap()

    x1o = nc.dram_tensor("x1o", [HID, TOK], F32, kind="ExternalOutput").ap()
    xn1o = nc.dram_tensor("xn1o", [HID, TOK], F32, kind="ExternalOutput").ap()
    xn1bo = nc.dram_tensor("xn1bo", [HID, TOK], BF16, kind="ExternalOutput").ap()
    smo = nc.dram_tensor("smo", [TOK, 8], F32, kind="ExternalOutput").ap()
    dbgC = nc.dram_tensor("dbgC", [128, 32], F32, kind="ExternalOutput").ap()
    mrs_h = nc.dram_tensor("mrs_dram", [2, TOK], F32, kind="Internal")

    al = Alloc(nc)
    xnsb = al.sb("xnsb", [128, 16 * TOK], BF16)
    osb = al.sb("osb", [128, 16 * 512], BF16)
    x0sb = al.sb("x0sb", [128, 2, 4 * 128], F32)
    hsb = al.sb("hsb", [128, 43 * 512], BF16)
    thsb = al.sb("thsb", [128, 2, 512], F32)
    v1sb = al.sb("v1sb", [128, 2, 512], F32)
    wg2 = al.sb("wg2", [128, 2, 16, 128], BF16)
    wu2 = al.sb("wu2", [128, 2, 16, 128], BF16)
    wd2 = al.sb("wd2", [128, 2, 43, 128], BF16)
    wo2 = al.sb("wo2", [128, 2, 16, 128], BF16)
    x1s = al.sb("x1s", [128, 3 * 512], F32)
    x1r = al.sb("x1r", [128, 2, 512], F32)
    x1b = al.sb("x1b", [128, 2, 512], BF16)
    x1q = al.sb("x1q", [128, 2, 512], BF16)
    onec = al.sb("onec", [128, 1], BF16)
    xn1f = al.sb("xn1f", [128, 512], F32)
    xn1t = al.sb("xn1t", [128, 512], F32)
    xn1c = al.sb("xn1c", [128, 512], BF16)
    rows = al.sb("rows", [1, 3 * 512], F32)
    mrow = al.sb("mrow", [1, 2 * 512], F32)
    rsrow = al.sb("rsrow", [1, 2 * 512], F32)
    mrep = al.sb("mrep", [128, TOK], F32)
    rrep2 = al.sb("rrep2", [128, TOK], F32)
    rwf = al.sb("rwf", [128, 16 * 8], F32)
    rwhl = al.sb("rwhl", [128, 16 * 16], BF16)
    rwt = al.sb("rwt", [128, 16 * 8], F32)
    rbrep = al.sb("rbrep", [128, 8], F32)
    gmsb = al.sb("gmsb", [128, 8 * 8], F32)
    xn1rb = al.sb("xn1rb", [128, 2, 16, 128], BF16)
    lgt = al.sb("lgt", [128, 2 * 8], F32)
    smt = al.sb("smt", [128, 8], F32)
    smt2 = al.sb("smt2", [128, 8], F32)
    nmx = al.sb("nmx", [128, 1], F32)
    ssum = al.sb("ssum", [128, 1], F32)
    rsum = al.sb("rsum", [128, 1], F32)
    idf = al.sb("idf", [128, 128], F32)
    ioti = al.sb("ioti", [128, 128], I32)

    pg0 = al.ps("pg0", [128, 512], F32)
    pg1 = al.ps("pg1", [128, 512], F32)
    pu0 = al.ps("pu0", [128, 512], F32)
    pu1 = al.ps("pu1", [128, 512], F32)
    pdn = al.ps("pdn", [128, 512], F32)
    pat = al.ps("pat", [128, 512], F32)
    pxt = al.ps("pxt", [128, 512], F32)
    pst = al.ps("pst", [128, 512], F32)

    ld0 = al.sem("ld0"); ldx0 = al.sem("ldx0")
    ldw0 = al.sem("ldw0"); ldw1a = al.sem("ldw1a"); ldw1b = al.sem("ldw1b")
    ldr = al.sem("ldr"); ldx0a = al.sem("ldx0a"); ldx0b = al.sem("ldx0b")
    ld1a = al.sem("ld1a"); ld1b = al.sem("ld1b")
    ld2a = al.sem("ld2a"); ld2b = al.sem("ld2b")
    ld3a = al.sem("ld3a"); ld3b = al.sem("ld3b")
    gset = al.sem("gset"); vset = al.sem("vset")
    mmg = al.sem("mmg"); mmu = al.sem("mmu"); vh = al.sem("vh"); sh = al.sem("sh")
    mmc = al.sem("mmc"); vc = al.sem("vc"); stx = al.sem("stx")
    mm2 = al.sem("mm2"); v2 = al.sem("v2"); s2 = al.sem("s2")
    ldb = al.sem("ldb"); ld1 = al.sem("ld1"); ld2 = al.sem("ld2"); ld3 = al.sem("ld3")
    mmr = al.sem("mmr"); vr = al.sem("vr")
    vch = al.sem("vch"); sch = al.sem("sch")
    sxo = al.sem("sxo"); ssm = al.sem("ssm")

    block = al.block()
    NIC = 43
    if True:
        @block.sync
        def _(sync):
            for i in range(16):
                sync.dma_start(out=xnsb[:, i * TOK:(i + 1) * TOK],
                               in_=xn0[i * 128:(i + 1) * 128, :]).then_inc(ld0, 16)
            n = 0
            for tb in range(2):
                if tb == 1:
                    sync.wait_ge(mmc, 3 * 15 + 2)
                for i in range(16):
                    sync.dma_start(out=osb[:, i * 512:(i + 1) * 512],
                                   in_=o_cm[i * 128:(i + 1) * 128,
                                            tb * 512:(tb + 1) * 512]).then_inc(ld0, 16)
                for oc in range(16):
                    n += 1
                    sync.wait_ge(vc, n)
                    sync.dma_start(out=x1o[oc * 128:(oc + 1) * 128, tb * 512:(tb + 1) * 512],
                                   in_=x1s[:, 2 * 512:3 * 512]).then_inc(stx, 16)
                    sync.wait_ge(stx, 16 * n)
            n = 0
            for tb in range(2):
                for oc in range(16):
                    n += 1
                    if n > 2:
                        sync.wait_ge(mm2, 2 * (n - 2))
                    sync.dma_start(out=x1r[:, n % 2, :],
                                   in_=x1o[oc * 128:(oc + 1) * 128,
                                           tb * 512:(tb + 1) * 512]).then_inc(
                                       ld1a if n % 2 == 0 else ld1b, 16)
            sync.wait_ge(s2, 66)
            sync.dma_start(out=mrep[:, :],
                           in_=AP(mrs_h, 0, [[0, 128], [1, TOK]])).then_inc(ldb, 16)
            sync.dma_start(out=rrep2[:, :],
                           in_=AP(mrs_h, TOK, [[0, 128], [1, TOK]])).then_inc(ldb, 16)
            n = 0
            for tb in range(2):
                for oc in range(16):
                    n += 1
                    if n > 2:
                        sync.wait_ge(sxo, 32 * (n - 2))
                    sync.dma_start(out=x1r[:, n % 2, :],
                                   in_=x1o[oc * 128:(oc + 1) * 128,
                                           tb * 512:(tb + 1) * 512]).then_inc(
                                       ld2a if n % 2 == 0 else ld2b, 16)
            sync.wait_ge(sxo, 32 * 32)
            for tc in range(8):
                if tc >= 2:
                    sync.wait_ge(mmr, 16 * (tc - 1))
                sync.dma_start(out=xn1rb[:, tc % 2, :, :],
                               in_=AP(xn1bo.tensor, tc * 128,
                                      [[TOK, 128], [TOK * 128, 16], [1, 128]])).then_inc(
                                   ld3a if tc % 2 == 0 else ld3b, 16)
                if tc >= 1:
                    sync.wait_ge(vr, 4 * (tc - 1) + 4)
                    sync.dma_start(out=smo[(tc - 1) * 128:tc * 128, :],
                                   in_=smt2[:, :]).then_inc(ssm, 16)
            sync.wait_ge(vr, 4 * 7 + 4)
            sync.dma_start(out=smo[7 * 128:8 * 128, :], in_=smt2[:, :]).then_inc(ssm, 16)
            sync.wait_ge(sxo, 32 * 32)
            sync.wait_ge(ssm, 16 * 8)
            sync.dma_start(out=dbgC[:, 0:32], in_=mrep[:, 0:32]).then_inc(ssm, 16)
            sync.wait_ge(ssm, 16 * 9)

        @block.gpsimd
        def _(g):
            g.iota(ioti[:], pattern=[[1, 128]], base=0, channel_multiplier=-1).then_inc(gset, 1)
            g.memset(onec[:], 1.0).then_inc(gset, 1)
            g.dma_start(out=rwf[:].rearrange("p (c e) -> p c e", c=16),
                        in_=AP(rw.tensor, 0, [[8, 128], [8 * 128, 16], [1, 8]])).then_inc(ldr, 16)
            g.dma_start(out=rbrep[:, :],
                        in_=AP(rbv.tensor, 0, [[0, 128], [1, 8]])).then_inc(ldr, 16)
            g.dma_start(out=gmsb[:].rearrange("p (c e) -> p c e", c=8),
                        in_=AP(gmb.tensor, 0, [[8, 128], [8 * 128, 8], [1, 8]])).then_inc(ldr, 16)
            for tb in range(2):
                for ic in range(NIC):
                    r = tb * NIC + ic
                    if r >= 2:
                        g.wait_ge(vh, r - 1)
                    w = min(128, INTER0 - ic * 128)
                    g.dma_start(out=wg2[:, r % 2, :, 0:w],
                                in_=AP(WgT.tensor, ic * 128,
                                       [[INTER0, 128], [INTER0 * 128, 16], [1, w]])).then_inc(ldw0, 16)
                    g.dma_start(out=wu2[:, r % 2, :, 0:w],
                                in_=AP(WuT.tensor, ic * 128,
                                       [[INTER0, 128], [INTER0 * 128, 16], [1, w]])).then_inc(ldw0, 16)
                for oc in range(16):
                    r = tb * 16 + oc
                    if r >= 2:
                        g.wait_ge(mmc, 3 * (r - 1))
                    g.dma_start(out=x0sb[:, r % 2, :].rearrange("p (s j) -> p s j", s=4),
                                in_=AP(x0.tensor, tb * 4 * 128 * HID + oc * 128,
                                       [[HID, 128], [HID * 128, 4], [1, 128]])).then_inc(
                                           ldx0a if r % 2 == 0 else ldx0b, 16)
                    if r >= 2:
                        g.wait_ge(vc, r - 1)
                    lw = ldw1a if r % 2 == 0 else ldw1b
                    g.dma_start(out=wd2[:, r % 2, :, :],
                                in_=AP(Wd.tensor, oc * 128,
                                       [[HID, 128], [HID * 128, 43], [1, 128]])).then_inc(lw, 16)
                    g.dma_start(out=wo2[:, r % 2, :, :],
                                in_=AP(wo.tensor, oc * 128,
                                       [[HID, 128], [HID * 128, 16], [1, 128]])).then_inc(lw, 16)

        @block.scalar
        def _(s):
            SC = Chain(s, sch)
            for tb in range(2):
                for ic in range(NIC):
                    r = tb * NIC + ic
                    s.wait_ge(mmg, r + 1)
                    pg = pg0 if r % 2 == 0 else pg1
                    s.activation(out=thsb[:, r % 2, :], in_=pg[:, :], func=AF.Tanh,
                                 bias=0.0, scale=0.5).then_inc(sh, 1)
            for tb in range(2):
                s.wait_ge(v2, 34 * (tb + 1))
                SC(s.activation(out=rows[0:1, 512:1024],
                                in_=rows[0:1, 0:512], func=AF.Ln,
                                bias=0.0, scale=1.0))
                SC(s.activation(out=rsrow[0:1, tb * 512:(tb + 1) * 512],
                                in_=rows[0:1, 512:1024], func=AF.Exp,
                                bias=0.0, scale=-0.5))
                s.sem_inc(s2, 1)
                s.dma_start(out=AP(mrs_h, tb * 512, [[1, 1], [1, 512]]),
                            in_=mrow[0:1, tb * 512:(tb + 1) * 512]).then_inc(s2, 16)
                s.dma_start(out=AP(mrs_h, TOK + tb * 512, [[1, 1], [1, 512]]),
                            in_=rsrow[0:1, tb * 512:(tb + 1) * 512]).then_inc(s2, 16)
            n = 0
            for tb in range(2):
                for oc in range(16):
                    n += 1
                    s.wait_ge(v2, 68 + 2 * n)
                    s.dma_start(out=xn1o[oc * 128:(oc + 1) * 128, tb * 512:(tb + 1) * 512],
                                in_=xn1f[:, :]).then_inc(sxo, 16)
                    s.dma_start(out=xn1bo[oc * 128:(oc + 1) * 128, tb * 512:(tb + 1) * 512],
                                in_=xn1c[:, :]).then_inc(sxo, 16)
                    s.wait_ge(sxo, 32 * n)
            for tc in range(8):
                s.wait_ge(vr, 4 * tc + 3)
                s.activation(out=smt[:, :], in_=lgt[:, 8:16], func=AF.Exp,
                             bias=nmx[:, 0:1], scale=1.0).then_inc(s2, 1)

        @block.vector
        def _(v):
            VC = Chain(v, vch)
            v.wait_ge(gset, 1)
            v.tensor_single_scalar(out=idf[:], in_=ioti[:], scalar=0,
                                   op=ALU.is_equal).then_inc(vset, 1)
            v.wait_ge(ldr, 48)
            v.tensor_copy(out=rwhl[:].rearrange("p (c e) -> p c e", c=16)[:, :, 0:8],
                          in_=rwf[:].rearrange("p (c e) -> p c e", c=16)).then_inc(vset, 1)
            v.tensor_tensor(out=rwt[:], in0=rwf[:],
                            in1=rwhl[:].rearrange("p (c e) -> p c e", c=16)[:, :, 0:8].opt(),
                            op=ALU.subtract)
            v.tensor_copy(out=rwhl[:].rearrange("p (c e) -> p c e", c=16)[:, :, 8:16],
                          in_=rwt[:].rearrange("p (c e) -> p c e", c=16)).then_inc(vset, 1)
            for tb in range(2):
                for ic in range(NIC):
                    r = tb * NIC + ic
                    w = min(128, INTER0 - ic * 128)
                    pg = pg0 if r % 2 == 0 else pg1
                    pu = pu0 if r % 2 == 0 else pu1
                    v.wait_ge(sh, r + 1)
                    v.wait_ge(mmu, r + 1)
                    v.scalar_tensor_tensor(out=v1sb[:, r % 2, :], in0=thsb[:, r % 2, :],
                                           scalar=1.0, in1=pg[:, :],
                                           op0=ALU.add, op1=ALU.mult)
                    v.scalar_tensor_tensor(out=hsb[:, ic * 512 + tb * 0:(ic) * 512 + 512] if False else hsb[:, ic * 512: ic * 512 + 512],
                                           in0=v1sb[:, r % 2, :], scalar=0.5, in1=pu[:, :],
                                           op0=ALU.mult, op1=ALU.mult).then_inc(vh, 1)
                for oc in range(16):
                    n = tb * 16 + oc + 1
                    v.wait_ge(mmc, 3 * n)
                    v.tensor_copy(out=x1s[:, 0:512], in_=pat[:, :])
                    v.scalar_tensor_tensor(out=x1s[:, 512:1024], in0=x1s[:, 0:512],
                                           scalar=1.0, in1=pdn[:, :], op0=ALU.mult, op1=ALU.add)
                    if n >= 2:
                        v.wait_ge(stx, 16 * (n - 1))
                    v.scalar_tensor_tensor(out=x1s[:, 1024:1536], in0=x1s[:, 512:1024],
                                           scalar=1.0, in1=pxt[:, :], op0=ALU.mult,
                                           op1=ALU.add).then_inc(vc, 1)
            n = 0
            for tb in range(2):
                for oc in range(16):
                    n += 1
                    v.wait_ge(ld1a if n % 2 == 0 else ld1b,
                              16 * ((n // 2) if n % 2 == 0 else (n + 1) // 2))
                    if n >= 3:
                        v.wait_ge(mm2, 2 * (n - 2))
                    v.tensor_copy(out=x1b[:, n % 2, :], in_=x1r[:, n % 2, :]).then_inc(v2, 1)
                    v.tensor_tensor(out=x1q[:, n % 2, :], in0=x1r[:, n % 2, :],
                                    in1=x1r[:, n % 2, :], op=ALU.mult).then_inc(v2, 1)
                v.wait_ge(mm2, 2 * 16 * (tb + 1))
                VC(v.tensor_scalar(out=mrow[0:1, tb * 512:(tb + 1) * 512], in0=pst[0:1, :],
                                   scalar1=1.0 / HID, scalar2=None, op0=ALU.mult))
                v.sem_inc(v2, 1)
                VC(v.tensor_scalar(out=rows[0:1, 0:512], in0=pat[0:1, :],
                                   scalar1=1.0 / HID, scalar2=None, op0=ALU.mult))
                VC(v.tensor_tensor(out=rows[0:1, 512:1024],
                                   in0=mrow[0:1, tb * 512:(tb + 1) * 512],
                                   in1=mrow[0:1, tb * 512:(tb + 1) * 512], op=ALU.mult))
                VC(v.tensor_tensor(out=rows[0:1, 1024:1536], in0=rows[0:1, 0:512],
                                   in1=rows[0:1, 512:1024], op=ALU.subtract))
                VC(v.tensor_scalar(out=rows[0:1, 0:512], in0=rows[0:1, 1024:1536],
                                   scalar1=LN_EPS, scalar2=None, op0=ALU.add))
                v.sem_inc(v2, 1)
            v.wait_ge(ldb, 32)
            n = 0
            for tb in range(2):
                for oc in range(16):
                    n += 1
                    v.wait_ge(ld2a if n % 2 == 0 else ld2b,
                              16 * ((n // 2) if n % 2 == 0 else (n + 1) // 2))
                    if n >= 2:
                        v.wait_ge(sxo, 32 * (n - 1))
                    v.tensor_tensor(out=xn1t[:, :], in0=x1r[:, n % 2, :],
                                    in1=mrep[:, tb * 512:(tb + 1) * 512], op=ALU.subtract)
                    v.tensor_tensor(out=xn1f[:, :], in0=xn1t[:, :],
                                    in1=rrep2[:, tb * 512:(tb + 1) * 512],
                                    op=ALU.mult).then_inc(v2, 1)
                    v.tensor_copy(out=xn1c[:, :], in_=xn1f[:, :]).then_inc(v2, 1)
            for tc in range(8):
                v.wait_ge(mmr, 16 * (tc + 1))
                VC(v.tensor_copy(out=lgt[:, 0:8], in_=pg0[:, 0:8]))
                VC(v.scalar_tensor_tensor(out=lgt[:, 8:16], in0=lgt[:, 0:8], scalar=1.0,
                                          in1=pg0[:, 8:16], op0=ALU.mult, op1=ALU.add))
                VC(v.tensor_tensor(out=lgt[:, 0:8], in0=lgt[:, 8:16],
                                   in1=gmsb[:, tc * 8:(tc + 1) * 8], op=ALU.add))
                VC(v.tensor_tensor(out=lgt[:, 8:16], in0=lgt[:, 0:8],
                                   in1=rbrep[:, :], op=ALU.add))
                v.sem_inc(vr, 2)
                VC(v.reduce_max(out=rsum[:, 0:1], in_=lgt[:, 8:16], axis=AX))
                VC(v.tensor_scalar(out=nmx[:, 0:1], in0=rsum[:, 0:1], scalar1=-1.0,
                                   scalar2=None, op0=ALU.mult))
                v.sem_inc(vr, 1)
                v.wait_ge(s2, 66 + tc + 1)
                if tc >= 1:
                    v.wait_ge(ssm, 16 * tc)
                VC(v.reduce_sum(out=ssum[:, 0:1], in_=smt[:, :], axis=AX))
                if tc == 0:
                    v.tensor_copy(out=mrep[:, 0:8], in_=lgt[:, 8:16])
                    v.tensor_copy(out=mrep[:, 8:9], in_=nmx[:, 0:1])
                    v.tensor_copy(out=mrep[:, 9:17], in_=smt[:, :])
                    v.tensor_copy(out=mrep[:, 17:18], in_=ssum[:, 0:1])
                VC(v.reciprocal(out=rsum[:, 0:1], in_=ssum[:, 0:1]))
                VC(v.tensor_scalar(out=smt2[:, :], in0=smt[:, :], scalar1=rsum[:, 0:1],
                                   scalar2=None, op0=ALU.mult))
                v.sem_inc(vr, 1)

        @block.tensor
        def _(t):
            t.wait_ge(ld0, 32 * 16)
            for tb in range(2):
                for ic in range(NIC):
                    r = tb * NIC + ic
                    w = min(128, INTER0 - ic * 128)
                    t.wait_ge(ldw0, 32 * (r + 1))
                    if r >= 2:
                        t.wait_ge(vh, r - 1)
                    pg = pg0 if r % 2 == 0 else pg1
                    pu = pu0 if r % 2 == 0 else pu1
                    inst = None
                    for dc in range(16):
                        inst = t.matmul(pg[0:w, :], wg2[:, r % 2, dc, 0:w],
                                        xnsb[:, dc * TOK + tb * 512: dc * TOK + tb * 512 + 512],
                                        start=(dc == 0), stop=(dc == 15))
                    inst.then_inc(mmg, 1)
                    inst = None
                    for dc in range(16):
                        inst = t.matmul(pu[0:w, :], wu2[:, r % 2, dc, 0:w],
                                        xnsb[:, dc * TOK + tb * 512: dc * TOK + tb * 512 + 512],
                                        start=(dc == 0), stop=(dc == 15))
                    inst.then_inc(mmu, 1)
                for oc in range(16):
                    rr = tb * 16 + oc
                    n = rr + 1
                    t.wait_ge(ldw1a if rr % 2 == 0 else ldw1b, 32 * (rr // 2 + 1))
                    t.wait_ge(vh, (tb + 1) * NIC)
                    if n >= 2:
                        t.wait_ge(vc, n - 1)
                    inst = None
                    for ic in range(NIC):
                        w = 128
                        inst = t.matmul(pdn[:, :], wd2[0:w, rr % 2, ic, :],
                                        hsb[0:w, ic * 512: ic * 512 + 512],
                                        start=(ic == 0), stop=(ic == NIC - 1))
                    inst.then_inc(mmc, 1)
                    if tb == 1:
                        t.wait_ge(ld0, 16 * 16 + 32 * 16)
                    inst = None
                    for rc in range(16):
                        inst = t.matmul(pat[:, :], wo2[:, rr % 2, rc, :],
                                        osb[:, rc * 512:(rc + 1) * 512],
                                        start=(rc == 0), stop=(rc == 15))
                    inst.then_inc(mmc, 1)
                    t.wait_ge(ldx0a if rr % 2 == 0 else ldx0b, 16 * (rr // 2 + 1))
                    t.wait_ge(vset, 1)
                    inst = None
                    for sub in range(4):
                        inst = t.transpose(pxt[:, sub * 128:(sub + 1) * 128],
                                           x0sb[:, rr % 2, sub * 128:(sub + 1) * 128],
                                           idf[:])
                    inst.then_inc(mmc, 1)
            t.wait_ge(gset, 2)
            n = 0
            for tb in range(2):
                for oc in range(16):
                    n += 1
                    t.wait_ge(v2, 2 * n + 2 * tb)
                    t.matmul(pst[0:1, :], onec[:, 0:1], x1b[:, n % 2, :],
                             start=(oc == 0), stop=(oc == 15)).then_inc(mm2, 1)
                    t.matmul(pat[0:1, :], onec[:, 0:1], x1q[:, n % 2, :],
                             start=(oc == 0), stop=(oc == 15)).then_inc(mm2, 1)
            n = 0
            for tc in range(8):
                if tc >= 1:
                    t.wait_ge(vr, 4 * tc)
                t.wait_ge(vset, 3)
                t.wait_ge(ld3a if tc % 2 == 0 else ld3b, 16 * (tc // 2 + 1))
                for dc in range(16):
                    t.matmul(pg0[:, 0:16], xn1rb[:, tc % 2, dc, :],
                             rwhl[:, dc * 16:(dc + 1) * 16],
                             start=(dc == 0), stop=(dc == 15)).then_inc(mmr, 1)
            pass
    al.close()
    return nc



def build_D():
    nc = _new_nc()
    xn1b_i = nc.dram_tensor("xn1b", [HID, TOK], BF16, kind="ExternalInput").ap()
    pcol_i = nc.dram_tensor("pcol", [128, 16], F32, kind="ExternalInput").ap()
    maas_i = nc.dram_tensor("maas", [128, 16 * 7], F32, kind="ExternalInput").ap()
    w1_i = nc.dram_tensor("w1", [HID, 160], F32, kind="ExternalInput").ap()
    w2_i = nc.dram_tensor("w2", [5, 32, HID], F32, kind="ExternalInput").ap()
    dw1_i = nc.dram_tensor("dw1", [HID, 64], F32, kind="ExternalInput").ap()
    dw2_i = nc.dram_tensor("dw2", [64, HID], F32, kind="ExternalInput").ap()
    wrT = nc.dram_tensor("wrT", [HID, HID], F32, kind="ExternalInput").ap()
    wkT = nc.dram_tensor("wkT", [HID, HID], F32, kind="ExternalInput").ap()
    wvT = nc.dram_tensor("wvT", [HID, HID], F32, kind="ExternalInput").ap()
    wgT = nc.dram_tensor("wgT", [HID, HID], F32, kind="ExternalInput").ap()

    r_o = nc.dram_tensor("r_o", [HID, TOK], BF16, kind="ExternalOutput").ap()
    k_o = nc.dram_tensor("k_o", [HID, TOK], BF16, kind="ExternalOutput").ap()
    w_o = nc.dram_tensor("w_o", [HID, TOK], F32, kind="ExternalOutput").ap()
    v_o = nc.dram_tensor("v_o", [TOK, HID], BF16, kind="ExternalOutput").ap()
    g_o = nc.dram_tensor("g_o", [HID, TOK], BF16, kind="ExternalOutput").ap()

    al = Alloc(nc)
    xnb = al.sb("xnb", [128, 16 * TOK], BF16)
    pcol = al.sb("pcol_s", [128, 16], F32)
    maas = al.sb("maas_s", [128, 16 * 7], F32)
    xxb = al.sb("xxb", [128, 16, 512], BF16)
    xxxb = al.sb("xxxb", [128, 16, 512], BF16)
    xkb = al.sb("xkb", [128, 5, 16, 512], BF16)
    w1b = al.sb("w1b_s", [128, 16, 160], BF16)
    w2b = al.sb("w2b_s", [32, 2, 16, 128], BF16)
    dw1b = al.sb("dw1b_s", [128, 16, 64], BF16)
    dw2b = al.sb("dw2b_s", [64, 16, 128], BF16)
    thf = al.sb("thf", [32, 2, 512], F32)
    thfb = al.sb("thfb", [32, 2, 512], BF16)
    thg = al.sb("thg", [128, 2, 512], F32)
    thd = al.sb("thd", [64, 512], F32)
    thdb = al.sb("thdb", [64, 512], BF16)
    mtmp = al.sb("mtmp", [128, 2, 512], F32)
    gtmp = al.sb("gtmp", [128, 2, 512], F32)
    evb = al.sb("evb", [128, 2, 512], BF16)
    evf = al.sb("evf", [128, 2, 512], F32)
    wstr = al.sb("wstr", [128, 2, 16, 128], BF16)
    identb = al.sb("identb", [128, 128], BF16)
    ioti = al.sb("ioti", [128, 128], I32)
    vtb = al.sb("vtb", [128, 2, 4, 128], BF16)

    pmM0 = al.ps("pmM0", [128, 512], F32)
    pmM1 = al.ps("pmM1", [128, 512], F32)
    pmF0 = al.ps("pmF0", [128, 512], F32)
    pmF1 = al.ps("pmF1", [128, 512], F32)
    pp0 = al.ps("pp0", [128, 512], F32)
    pp1 = al.ps("pp1", [128, 512], F32)
    ptr = al.ps("ptr", [128, 512], BF16)
    ptr1 = al.ps("ptr1", [128, 512], BF16)

    ldx = al.sem("ldx"); ldc = al.sem("ldc"); ldg0 = al.sem("ldg0")
    ldga = al.sem("ldga"); ldgb = al.sem("ldgb")
    ldwa2 = al.sem("ldwa2"); ldwb2 = al.sem("ldwb2")
    gset = al.sem("gset"); vset = al.sem("vset"); vch = al.sem("vch"); sch2 = al.sem("sch2")
    vxx = al.sem("vxx")
    mmm = al.sem("mmm"); sth = al.sem("sth"); vth = al.sem("vth")
    mmf = al.sem("mmf"); vxk = al.sem("vxk")
    mmd = al.sem("mmd"); sthd = al.sem("sthd"); vthd = al.sem("vthd")
    mmw = al.sem("mmw"); vw = al.sem("vw")
    mmp = al.sem("mmp"); sthg = al.sem("sthg"); vev = al.sem("vev")
    mmt = al.sem("mmt"); vtv = al.sem("vtv")
    sto = al.sem("sto")

    block = al.block()
    projs = [("r", r_o, wrT, 3, "cm"), ("k", k_o, wkT, 1, "cm"),
             ("g", g_o, wgT, 4, "cm_silu"), ("v", v_o, wvT, 2, "tok")]
    NST = 2 * (3 * 16 + 16 * 4 + 16)

    if True:
        @block.sync
        def _(sync):
            for i in range(16):
                sync.dma_start(out=xnb[:, i * TOK:(i + 1) * TOK],
                               in_=xn1b_i[i * 128:(i + 1) * 128, :]).then_inc(ldx, 16)
            sync.dma_start(out=pcol[:], in_=pcol_i[:, :]).then_inc(ldc, 16)
            sync.dma_start(out=maas[:], in_=maas_i[:, :]).then_inc(ldc, 16)
            sync.wait_ge(sto, 16 * NST)

        @block.gpsimd
        def _(g):
            g.iota(ioti[:], pattern=[[1, 128]], base=0, channel_multiplier=-1).then_inc(gset, 1)
            g.dma_start(out=w1b[:],
                        in_=AP(w1_i.tensor, 0, [[160, 128], [160 * 128, 16], [1, 160]])).then_inc(ldg0, 16)
            g.dma_start(out=dw1b[:],
                        in_=AP(dw1_i.tensor, 0, [[64, 128], [64 * 128, 16], [1, 64]])).then_inc(ldg0, 16)
            g.dma_start(out=dw2b[:],
                        in_=AP(dw2_i.tensor, 0, [[HID, 64], [128, 16], [1, 128]])).then_inc(ldg0, 16)
            rr = 0
            for tb in range(2):
                for f in range(5):
                    r2 = tb * 5 + f
                    if r2 >= 2:
                        g.wait_ge(mmf, 16 * (r2 - 1))
                    g.dma_start(out=w2b[:, r2 % 2, :, :],
                                in_=AP(w2_i.tensor, f * 32 * HID,
                                       [[HID, 32], [128, 16], [1, 128]])).then_inc(
                                           ldga if r2 % 2 == 0 else ldgb, 16)
                for pi, (nm, oap, wap, fi, kind) in enumerate(projs):
                    for oc in range(16):
                        if rr >= 2:
                            g.wait_ge(mmp, rr - 1)
                        g.dma_start(out=wstr[:, rr % 2, :, :],
                                    in_=AP(wap.tensor, oc * 128,
                                           [[HID, 128], [HID * 128, 16], [1, 128]])).then_inc(
                                               ldwa2 if rr % 2 == 0 else ldwb2, 16)
                        rr += 1

        @block.scalar
        def _(s):
            SC = Chain(s, sch2)
            nstore = 0
            for tb in range(2):
                for f in range(5):
                    s.wait_ge(mmm, 5 * tb + f + 1)
                    if 5 * tb + f >= 2:
                        s.wait_ge(vth, 5 * tb + f - 1)
                    s.activation(out=thf[:, f % 2, :], in_=pmM0[0:32, :] if f % 2 == 0 else pmM1[0:32, :],
                                 func=AF.Tanh, bias=0.0, scale=1.0).then_inc(sth, 1)
                s.wait_ge(mmd, tb + 1)
                s.activation(out=thd[:, :], in_=pmM0[0:64, :], func=AF.Tanh,
                             bias=0.0, scale=1.0).then_inc(sthd, 1)
                for dc in range(16):
                    s.wait_ge(vw, 16 * tb + dc + 1)
                    s.dma_start(out=w_o[dc * 128:(dc + 1) * 128, tb * 512:(tb + 1) * 512],
                                in_=evf[:, dc % 2, :]).then_inc(sto, 16)
                    nstore += 1
                    if dc >= 1:
                        pass
                for pi, (nm, oap, wap, fi, kind) in enumerate(projs):
                    for oc in range(16):
                        rr = (tb * 4 + pi) * 16 + oc
                        if kind == "cm_silu":
                            s.wait_ge(mmp, rr + 1)
                            s.activation(out=thg[:, rr % 2, :],
                                         in_=pp0[:, :] if rr % 2 == 0 else pp1[:, :],
                                         func=AF.Tanh, bias=0.0, scale=0.5).then_inc(sthg, 1)
                        if kind == "tok":
                            for sub in range(4):
                                s.wait_ge(vtv, 4 * (16 * tb + oc) + sub + 1)
                                s.dma_start(out=v_o[tb * 512 + sub * 128: tb * 512 + (sub + 1) * 128,
                                                    oc * 128:(oc + 1) * 128],
                                            in_=vtb[:, tb % 2, sub, :]).then_inc(sto, 16)
                                nstore += 1
                        else:
                            s.wait_ge(vev, rr - 16 * tb * 0 + 1)
                            s.dma_start(out=oap[oc * 128:(oc + 1) * 128, tb * 512:(tb + 1) * 512],
                                        in_=evb[:, rr % 2, :]).then_inc(sto, 16)
                            nstore += 1
                        s.wait_ge(sto, 16 * nstore)
            assert nstore == NST, nstore

        @block.vector
        def _(v):
            VC = Chain(v, vch)
            v.wait_ge(gset, 1)
            v.tensor_single_scalar(out=identb[:], in_=ioti[:], scalar=0,
                                   op=ALU.is_equal).then_inc(vset, 1)
            v.wait_ge(ldx, 256)
            v.wait_ge(ldc, 32)
            v.wait_ge(ldg0, 48)
            xn3 = xnb[:].rearrange("p (c t) -> p c t", c=16)
            for tb in range(2):
                base = tb * 512
                if tb == 1:
                    v.wait_ge(mmp, 4 * 16)
                v.tensor_tensor(out=xxb[:, :, 1:512],
                                in0=xn3[:, :, base:base + 511],
                                in1=xn3[:, :, base + 1:base + 512], op=ALU.subtract)
                if tb == 0:
                    VC(v.tensor_tensor(out=xxb[:, :, 0:1], in0=pcol[:].unsqueeze(2),
                                       in1=xn3[:, :, 0:1], op=ALU.subtract))
                else:
                    VC(v.tensor_tensor(out=xxb[:, :, 0:1], in0=xn3[:, :, 511:512],
                                       in1=xn3[:, :, 512:513], op=ALU.subtract))
                for dc in range(16):
                    v.scalar_tensor_tensor(out=mtmp[:, dc % 2, :], in0=xxb[:, dc, :],
                                           scalar=maas[:, dc:dc + 1], op0=ALU.mult,
                                           in1=xn3[:, dc, base:base + 512], op1=ALU.add)
                    v.tensor_copy(out=xxxb[:, dc, :],
                                  in_=mtmp[:, dc % 2, :]).then_inc(vxx, 1)
                for f in range(5):
                    v.wait_ge(sth, 5 * tb + f + 1)
                    if 5 * tb + f >= 2:
                        v.wait_ge(mmf, 16 * (5 * tb + f - 1))
                    v.tensor_copy(out=thfb[:, f % 2, :], in_=thf[:, f % 2, :]).then_inc(vth, 1)
                    for dc in range(16):
                        nfd = (tb * 5 + f) * 16 + dc
                        v.wait_ge(mmf, nfd + 1)
                        pm = pmF0 if dc % 2 == 0 else pmF1
                        v.tensor_scalar(out=mtmp[:, dc % 2, :], in0=pm[:, :],
                                        scalar1=maas[:, 16 * (f + 1) + dc: 16 * (f + 1) + dc + 1],
                                        scalar2=None, op0=ALU.add)
                        v.tensor_tensor(out=gtmp[:, dc % 2, :], in0=mtmp[:, dc % 2, :],
                                        in1=xxb[:, dc, :], op=ALU.mult)
                        v.tensor_tensor(out=xkb[:, f, dc, :], in0=gtmp[:, dc % 2, :],
                                        in1=xn3[:, dc, base:base + 512],
                                        op=ALU.add).then_inc(vxk, 1)
                v.wait_ge(sthd, tb + 1)
                v.tensor_copy(out=thdb[:, :], in_=thd[:, :]).then_inc(vthd, 1)
                for dc in range(16):
                    v.wait_ge(mmw, 16 * tb + dc + 1)
                    if dc >= 2:
                        v.wait_ge(sto, 16 * (128 * tb + dc - 1))
                    pw = pmF0 if dc % 2 == 0 else pmF1
                    v.tensor_scalar(out=evf[:, dc % 2, :], in0=pw[:, :],
                                    scalar1=maas[:, 16 * 6 + dc: 16 * 6 + dc + 1],
                                    scalar2=None, op0=ALU.add).then_inc(vw, 1)
                for pi, (nm, oap, wap, fi, kind) in enumerate(projs):
                    for oc in range(16):
                        rr = (tb * 4 + pi) * 16 + oc
                        pp = pp0 if rr % 2 == 0 else pp1
                        if kind == "cm":
                            v.wait_ge(mmp, rr + 1)
                            v.tensor_copy(out=evb[:, rr % 2, :], in_=pp[:, :]).then_inc(vev, 1)
                        elif kind == "cm_silu":
                            v.wait_ge(sthg, 16 * tb + oc + 1)
                            v.scalar_tensor_tensor(out=gtmp[:, rr % 2, :],
                                                   in0=thg[:, rr % 2, :], scalar=1.0,
                                                   in1=pp[:, :], op0=ALU.add, op1=ALU.mult)
                            v.tensor_scalar(out=evb[:, rr % 2, :], in0=gtmp[:, rr % 2, :],
                                            scalar1=0.5, scalar2=None,
                                            op0=ALU.mult).then_inc(vev, 1)
                        elif kind == "tok":
                            v.wait_ge(mmp, rr + 1)
                            v.tensor_copy(out=evb[:, rr % 2, :], in_=pp[:, :]).then_inc(vev, 1)
                            n_ = 16 * tb + oc
                            v.wait_ge(mmt, 4 * (n_ + 1))
                            ptx = ptr if n_ % 2 == 0 else ptr1
                            for sub in range(4):
                                v.tensor_copy(out=vtb[:, tb % 2, sub, :],
                                              in_=ptx[:, sub * 128:(sub + 1) * 128]).then_inc(vtv, 1)

        @block.tensor
        def _(t):
            t.wait_ge(vset, 1)
            t.wait_ge(ldc, 32)
            t.wait_ge(ldg0, 48)
            for tb in range(2):
                t.wait_ge(vxx, 16 * (tb + 1))
                for f in range(5):
                    if f >= 2:
                        t.wait_ge(sth, 5 * tb + f - 1)
                    pm = pmM0 if f % 2 == 0 else pmM1
                    inst = None
                    for dc in range(16):
                        inst = t.matmul(pm[0:32, :], w1b[:, dc, 32 * f:32 * (f + 1)],
                                        xxxb[:, dc, :], start=(dc == 0), stop=(dc == 15))
                    inst.then_inc(mmm, 1)
                for f in range(5):
                    r2 = tb * 5 + f
                    t.wait_ge(vth, 5 * tb + f + 1)
                    t.wait_ge(ldga if r2 % 2 == 0 else ldgb, 16 * (r2 // 2 + 1))
                    for dc in range(16):
                        nfd = (tb * 5 + f) * 16 + dc
                        if nfd >= 2:
                            t.wait_ge(vxk, nfd - 1)
                        pm = pmF0 if dc % 2 == 0 else pmF1
                        t.matmul(pm[:, :], w2b[:, r2 % 2, dc, :], thfb[:, f % 2, :],
                                 start=True, stop=True).then_inc(mmf, 1)
                t.wait_ge(vxk, 16 * 5 * (tb + 1))
                t.wait_ge(sth, 5 * (tb + 1))
                inst = None
                for dc in range(16):
                    inst = t.matmul(pmM0[0:64, :], dw1b[:, dc, :], xkb[:, 0, dc, :],
                                    start=(dc == 0), stop=(dc == 15))
                inst.then_inc(mmd, 1)
                t.wait_ge(vthd, tb + 1)
                for dc in range(16):
                    if 16 * tb + dc >= 2:
                        t.wait_ge(vw, 16 * tb + dc - 1)
                    pw = pmF0 if dc % 2 == 0 else pmF1
                    t.matmul(pw[:, :], dw2b[:, dc, :], thdb[:, :],
                             start=True, stop=True).then_inc(mmw, 1)
                for pi, (nm, oap, wap, fi, kind) in enumerate(projs):
                    for oc in range(16):
                        rr = (tb * 4 + pi) * 16 + oc
                        t.wait_ge(ldwa2 if rr % 2 == 0 else ldwb2, 16 * (rr // 2 + 1))
                        if rr >= 2:
                            t.wait_ge(vev, rr - 1)
                        pp = pp0 if rr % 2 == 0 else pp1
                        inst = None
                        for dc in range(16):
                            inst = t.matmul(pp[:, :], wstr[:, rr % 2, dc, :],
                                            xkb[:, fi, dc, tb * 0:512] if False else xkb[:, fi, dc, :],
                                            start=(dc == 0), stop=(dc == 15))
                        inst.then_inc(mmp, 1)
                        if kind == "tok":
                            t.wait_ge(vev, rr + 1)
                            n_ = 16 * tb + oc
                            if n_ >= 2:
                                t.wait_ge(vtv, 4 * (n_ - 1))
                            ptx = ptr if n_ % 2 == 0 else ptr1
                            inst = None
                            for sub in range(4):
                                inst = t.transpose(ptx[:, sub * 128:(sub + 1) * 128],
                                                   evb[:, rr % 2, sub * 128:(sub + 1) * 128],
                                                   identb[:]).then_inc(mmt, 1)
    al.close()
    return nc



def build_E():
    nc = _new_nc()
    r_i = nc.dram_tensor("r_i", [1024, SEQ], BF16, kind="ExternalInput").ap()
    k_i = nc.dram_tensor("k_i", [1024, SEQ], BF16, kind="ExternalInput").ap()
    w_i = nc.dram_tensor("w_i", [1024, SEQ], F32, kind="ExternalInput").ap()
    v_i = nc.dram_tensor("v_i", [SEQ, 1024], BF16, kind="ExternalInput").ap()
    st_i = nc.dram_tensor("st_i", [1024, 64], F32, kind="ExternalInput").ap()
    faT_i = nc.dram_tensor("faT_i", [64, 16], F32, kind="ExternalInput").ap()
    lnw_i = nc.dram_tensor("lnw_i", [1, 1024], F32, kind="ExternalInput").ap()
    lnb_i = nc.dram_tensor("lnb_i", [1, 1024], F32, kind="ExternalInput").ap()
    yn_o = nc.dram_tensor("yn_o", [SEQ, 1024], BF16, kind="ExternalOutput").ap()

    al = Alloc(nc)
    W = 16 * 64
    rc = al.sb("rc", [64, 2, W], BF16)
    kc = al.sb("kc", [64, 2, W], BF16)
    wc = al.sb("wc", [64, 2, W], F32)
    ex = al.sb("ex", [64, W], F32)
    cum = al.sb("cum", [64, W], F32)
    tmp1 = al.sb("tmp1", [64, W], F32)
    tmp2 = al.sb("tmp2", [64, W], F32)
    cumL = al.sb("cumL", [64, 16], F32)
    aL = al.sb("aL", [64, 16], F32)
    aLt = al.sb("aLt", [64, W], F32)
    expR = al.sb("expR", [64, W], F32)
    expK = al.sb("expK", [64, W], F32)
    expKh = al.sb("expKh", [64, W], F32)
    ktil = al.sb("ktil", [64, W], BF16)
    khat = al.sb("khat", [64, W], BF16)
    rku = al.sb("rku", [64, W], BF16)
    ystk = al.sb("ystk", [128, W], BF16)
    sv = al.sb("sv", [128, 2, W], BF16)
    vb0 = al.sb("vb0", [64, 2, W], BF16)
    ssb = al.sb("ssb", [64, W], F32)
    khT0 = al.sb("khT0", [64, 8 * 64], BF16)
    khT1 = al.sb("khT1", [64, 8 * 64], BF16)
    maskr = al.sb("maskr", [64, W], BF16)
    onesj = al.sb("onesj", [64, 1], BF16)
    urep = al.sb("urep", [64, W], F32)
    faT = al.sb("faT", [64, 16], F32)
    dsb = al.sb("dsb", [64, 16], F32)
    drep = al.sb("drep", [64, W], F32)
    yf = al.sb("yf", [64, W], F32)
    y3 = al.sb("y3", [64, W], F32)
    mrow = al.sb("mrow_e", [64, 16], F32)
    vrow = al.sb("vrow_e", [64, 16], F32)
    rsr = al.sb("rsr_e", [64, 16], F32)
    sc2 = al.sb("sc2_e", [64, 16], F32)
    mrep2 = al.sb("mrep2", [64, W], F32)
    rsrep = al.sb("rsrep", [64, W], F32)
    lnwr = al.sb("lnwr", [64, W], F32)
    lnbr = al.sb("lnbr", [64, W], F32)
    ynb = al.sb("ynb", [64, 2, W], BF16)
    ioti = al.sb("ioti_e", [64, 64], I32)
    identb = al.sb("identb_e", [64, 64], BF16)
    onesT = al.sb("onesT", [64, 64], F32)

    pP = al.ps("pP", [64, W], F32)
    pY = al.ps("pY", [64, W], F32)
    pSK = al.ps("pSK", [64, W], F32)
    pT = al.ps("pT", [128, 512], BF16)
    pd = al.ps("pd", [64, 16], F32)

    ldca = al.sem("ldca"); ldcb = al.sem("ldcb")
    ldi = al.sem("ldi")
    gset = al.sem("gset"); vset = al.sem("vset"); vch = al.sem("vch"); sch = al.sem("sch")
    vpre = al.sem("vpre")
    spre = al.sem("spre")
    mmP = al.sem("mmP"); vPM = al.sem("vPM")
    mmd = al.sem("mmd"); vd = al.sem("vd")
    mmY = al.sem("mmY"); vY = al.sem("vY")
    mmT = al.sem("mmT"); vT = al.sem("vT")
    mmS = al.sem("mmS"); vS = al.sem("vS")
    vgn = al.sem("vgn"); sgn = al.sem("sgn")
    stoa = al.sem("stoa"); stob = al.sem("stob")

    block = al.block()
    NC_ = NCHUNK
    if True:
        @block.sync
        def _(sync):
            sync.dma_start(out=ssb[:].rearrange("p (g i) -> p g i", g=16),
                           in_=AP(st_i.tensor, 0, [[64, 64], [64 * 64, 16], [1, 64]])).then_inc(ldi, 16)
            sync.dma_start(out=faT[:], in_=faT_i[:, :]).then_inc(ldi, 16)
            sync.dma_start(out=lnwr[:], in_=AP(lnw_i.tensor, 0, [[0, 64], [1, W]])).then_inc(ldi, 16)
            sync.dma_start(out=lnbr[:], in_=AP(lnb_i.tensor, 0, [[0, 64], [1, W]])).then_inc(ldi, 16)
            for it in range(NC_ + 2):
              c = it
              if it < NC_:
                if c >= 2:
                    sync.wait_ge(vS, c - 1)
                sync.dma_start(out=rc[:, c % 2, :].rearrange("p (g t) -> p g t", g=16),
                               in_=AP(r_i.tensor, c * 64, [[SEQ, 64], [64 * SEQ, 16], [1, 64]])).then_inc(ldca if c % 2 == 0 else ldcb, 16)
                sync.dma_start(out=kc[:, c % 2, :].rearrange("p (g t) -> p g t", g=16),
                               in_=AP(k_i.tensor, c * 64, [[SEQ, 64], [64 * SEQ, 16], [1, 64]])).then_inc(ldca if c % 2 == 0 else ldcb, 16)
                sync.dma_start(out=wc[:, c % 2, :].rearrange("p (g t) -> p g t", g=16),
                               in_=AP(w_i.tensor, c * 64, [[SEQ, 64], [64 * SEQ, 16], [1, 64]])).then_inc(ldca if c % 2 == 0 else ldcb, 16)
                sync.dma_start(out=sv[64:128, c % 2, :].rearrange("p (g i) -> p g i", g=16),
                               in_=AP(v_i.tensor, c * 64 * 1024, [[1024, 64], [64, 16], [1, 64]])).then_inc(ldca if c % 2 == 0 else ldcb, 16)
                sync.dma_start(out=vb0[:, c % 2, :].rearrange("p (g i) -> p g i", g=16),
                               in_=AP(v_i.tensor, c * 64 * 1024, [[1024, 64], [64, 16], [1, 64]])).then_inc(ldca if c % 2 == 0 else ldcb, 16)
              c2 = it - 2
              if c2 >= 0:
                sync.wait_ge(vgn, 8 * (c2 + 1))
                sync.dma_start(out=yn_o[c2 * 64:(c2 + 1) * 64, :],
                               in_=ynb[:, c2 % 2, :]).then_inc(stoa if c2 % 2 == 0 else stob, 16)
            sync.wait_ge(stoa, 16 * (NC_ // 2))
            sync.wait_ge(stob, 16 * (NC_ // 2))

        @block.gpsimd
        def _(g):
            g.iota(ioti[:], pattern=[[1, 64]], base=0, channel_multiplier=-1).then_inc(gset, 1)
            g.memset(onesj[:], 1.0).then_inc(gset, 1)
            g.memset(onesT[:], 1.0).then_inc(gset, 1)

        @block.scalar
        def _(s):
            SC = Chain(s, sch)
            for c in range(NC_):
                s.wait_ge(ldca if c % 2 == 0 else ldcb, 16 * (5 * (c // 2) + 3))
                s.activation(out=ex[:, :], in_=wc[:, c % 2, :], func=AF.Exp,
                             bias=0.0, scale=1.0).then_inc(spre, 1)
                s.wait_ge(vpre, 6 * c + 2)
                s.activation(out=expR[:, :], in_=tmp1[:, :], func=AF.Exp,
                             bias=0.0, scale=1.0).then_inc(spre, 1)
                s.activation(out=expK[:, :], in_=cum[:, :], func=AF.Exp,
                             bias=0.0, scale=1.0).then_inc(spre, 1)
                s.wait_ge(vpre, 6 * c + 3)
                s.activation(out=expKh[:, :], in_=tmp2[:, :], func=AF.Exp,
                             bias=0.0, scale=1.0).then_inc(spre, 1)
                s.wait_ge(vpre, 6 * c + 4)
                SC(s.activation(out=aL[:, :], in_=cumL[:, :], func=AF.Exp,
                                bias=0.0, scale=-1.0))
                s.sem_inc(spre, 1)
                s.wait_ge(vgn, 8 * c + 4)
                SC(s.activation(out=sc2[:, :], in_=vrow[:, :], func=AF.Ln,
                                bias=0.0, scale=1.0))
                SC(s.activation(out=rsr[:, :], in_=sc2[:, :], func=AF.Exp,
                                bias=0.0, scale=-0.5))
                s.sem_inc(sgn, 1)

        @block.vector
        def _(v):
            VC = Chain(v, vch)
            v.wait_ge(gset, 3)
            v.tensor_scalar(out=identb[:, :], in0=ioti[:], scalar1=0, scalar2=None,
                            op0=ALU.is_equal).then_inc(vset, 1)
            v.tensor_scalar(out=maskr[:].rearrange("p (g t) -> p g t", g=16),
                            in0=ioti[:].unsqueeze(1).broadcast_to([64, 16, 64]),
                            scalar1=0, scalar2=None, op0=ALU.is_gt).then_inc(vset, 1)
            v.wait_ge(ldi, 64)
            v.tensor_copy(out=urep[:].rearrange("p (g t) -> p g t", g=16),
                          in_=faT[:].unsqueeze(2).broadcast_to([64, 16, 64])).then_inc(vset, 1)
            v.tensor_copy(out=sv[0:64, 0, :], in_=ssb[:]).then_inc(vset, 1)
            for c in range(NC_):
                cb = c % 2
                v.wait_ge(spre, 5 * c + 1)
                inst = None
                for g2 in range(16):
                    inst = v.tensor_tensor_scan(out=cum[:, g2 * 64:(g2 + 1) * 64],
                                                data0=onesT[:, :],
                                                data1=ex[:, g2 * 64:(g2 + 1) * 64],
                                                initial=0.0, op0=ALU.mult, op1=ALU.add)
                VC(inst)
                v.sem_inc(vpre, 1)
                v.tensor_tensor(out=tmp1[:, :], in0=ex[:, :], in1=cum[:, :],
                                op=ALU.subtract).then_inc(vpre, 1)
                VC(v.tensor_copy(out=cumL[:, :],
                                 in_=cum[:].rearrange("p (g t) -> p g t", g=16)[:, :, 63]))
                VC(v.tensor_copy(out=aLt[:].rearrange("p (g t) -> p g t", g=16),
                                 in_=cumL[:].unsqueeze(2).broadcast_to([64, 16, 64])))
                v.tensor_tensor(out=tmp2[:, :], in0=cum[:, :], in1=aLt[:, :],
                                op=ALU.subtract).then_inc(vpre, 1)
                v.sem_inc(vpre, 1)
                v.wait_ge(spre, 5 * c + 5)
                VC(v.tensor_copy(out=aLt[:].rearrange("p (g t) -> p g t", g=16),
                                 in_=aL[:].unsqueeze(2).broadcast_to([64, 16, 64])))
                v.sem_inc(vpre, 1)
                v.wait_ge(spre, 5 * c + 4)
                v.tensor_tensor(out=ystk[0:64, :], in0=rc[:, cb, :], in1=expR[:, :],
                                op=ALU.mult).then_inc(vpre, 1)
                v.tensor_tensor(out=ktil[:, :], in0=kc[:, cb, :], in1=expK[:, :],
                                op=ALU.mult).then_inc(vPM, 1)
                v.tensor_tensor(out=khat[:, :], in0=kc[:, cb, :], in1=expKh[:, :],
                                op=ALU.mult).then_inc(vT, 1)
                v.tensor_tensor(out=tmp1[:, :], in0=rc[:, cb, :], in1=kc[:, cb, :],
                                op=ALU.mult)
                v.tensor_tensor(out=rku[:, :], in0=tmp1[:, :], in1=urep[:, :],
                                op=ALU.mult).then_inc(vd, 1)
                v.wait_ge(mmP, 16 * (c + 1))
                v.tensor_tensor(out=ystk[64:128, :], in0=pP[:, :], in1=maskr[:, :],
                                op=ALU.mult).then_inc(vPM, 1)
                v.wait_ge(mmd, 16 * (c + 1))
                VC(v.tensor_copy(out=dsb[:, :], in_=pd[:, :]))
                VC(v.tensor_copy(out=drep[:].rearrange("p (g t) -> p g t", g=16),
                                 in_=dsb[:].unsqueeze(2).broadcast_to([64, 16, 64])))
                v.sem_inc(vd, 1)
                v.wait_ge(ldca if c % 2 == 0 else ldcb, 16 * (5 * (c // 2) + 5))
                v.tensor_tensor(out=y3[:, :], in0=vb0[:, cb, :], in1=drep[:, :],
                                op=ALU.mult)
                v.wait_ge(mmY, 16 * (c + 1))
                v.scalar_tensor_tensor(out=yf[:, :], in0=y3[:, :], scalar=1.0,
                                       in1=pY[:, :], op0=ALU.mult, op1=ALU.add).then_inc(vY, 1)
                v.wait_ge(mmT, 8 * (c + 1))
                v.tensor_copy(out=khT0[:, :], in_=pT[0:64, 0:512]).then_inc(vT, 1)
                v.tensor_copy(out=khT1[:, :], in_=pT[64:128, 0:512]).then_inc(vT, 1)
                v.wait_ge(mmS, 16 * (c + 1))
                v.tensor_tensor(out=tmp2[:, :], in0=ssb[:, :], in1=aLt[:, :], op=ALU.mult)
                v.scalar_tensor_tensor(out=ssb[:, :], in0=tmp2[:, :], scalar=1.0,
                                       in1=pSK[:, :], op0=ALU.mult, op1=ALU.add)
                if c < NC_ - 1:
                    v.tensor_copy(out=sv[0:64, (c + 1) % 2, :], in_=ssb[:, :]).then_inc(vS, 1)
                else:
                    v.sem_inc(vS, 1)
                v.tensor_tensor(out=tmp1[:, :], in0=yf[:, :], in1=yf[:, :], op=ALU.mult)
                VC(v.reduce_sum(out=mrow[:, :].unsqueeze(2),
                                in_=yf[:].rearrange("p (g t) -> p g t", g=16), axis=AX))
                VC(v.reduce_sum(out=vrow[:, :].unsqueeze(2),
                                in_=tmp1[:].rearrange("p (g t) -> p g t", g=16), axis=AX))
                VC(v.tensor_scalar(out=mrow[:, :], in0=mrow[:, :], scalar1=1.0 / 64,
                                   scalar2=None, op0=ALU.mult))
                v.sem_inc(vgn, 3)
                VC(v.tensor_scalar(out=sc2[:, :], in0=vrow[:, :], scalar1=1.0 / 64,
                                   scalar2=None, op0=ALU.mult))
                VC(v.tensor_tensor(out=vrow[:, :], in0=mrow[:, :], in1=mrow[:, :],
                                   op=ALU.mult))
                VC(v.scalar_tensor_tensor(out=vrow[:, :], in0=vrow[:, :], scalar=-1.0,
                                          in1=sc2[:, :], op0=ALU.mult, op1=ALU.add))
                VC(v.tensor_scalar(out=vrow[:, :], in0=vrow[:, :], scalar1=GN_EPS,
                                   scalar2=None, op0=ALU.add))
                v.sem_inc(vgn, 1)
                VC(v.tensor_copy(out=mrep2[:].rearrange("p (g t) -> p g t", g=16),
                                 in_=mrow[:].unsqueeze(2).broadcast_to([64, 16, 64])))
                v.wait_ge(sgn, c + 1)
                VC(v.tensor_copy(out=rsrep[:].rearrange("p (g t) -> p g t", g=16),
                                 in_=rsr[:].unsqueeze(2).broadcast_to([64, 16, 64])))
                v.tensor_tensor(out=tmp1[:, :], in0=yf[:, :], in1=mrep2[:, :],
                                op=ALU.subtract)
                v.tensor_tensor(out=tmp2[:, :], in0=tmp1[:, :], in1=rsrep[:, :],
                                op=ALU.mult)
                v.tensor_tensor(out=tmp1[:, :], in0=tmp2[:, :], in1=lnwr[:, :],
                                op=ALU.mult)
                if c >= 2:
                    v.wait_ge(stoa if c % 2 == 0 else stob, 16 * (c // 2))
                v.tensor_tensor(out=ynb[:, c % 2, :], in0=tmp1[:, :], in1=lnbr[:, :],
                                op=ALU.add).then_inc(vgn, 4)

        @block.tensor
        def _(t):
            t.wait_ge(vset, 4)
            for c in range(NC_):
                cb = c % 2
                t.wait_ge(vPM, 2 * c + 1)
                t.wait_ge(vpre, 6 * c + 6)
                for g2 in range(16):
                    t.matmul(pP[:, g2 * 64:(g2 + 1) * 64], ktil[:, g2 * 64:(g2 + 1) * 64],
                             ystk[0:64, g2 * 64:(g2 + 1) * 64],
                             start=True, stop=True).then_inc(mmP, 1)
                t.wait_ge(vd, 2 * c + 1)
                for g2 in range(16):
                    t.matmul(pd[:, g2:g2 + 1], rku[:, g2 * 64:(g2 + 1) * 64],
                             onesj[:, 0:1], start=True, stop=True).then_inc(mmd, 1)
                t.wait_ge(vPM, 2 * c + 2)
                t.wait_ge(ldca if c % 2 == 0 else ldcb, 16 * (5 * (c // 2) + 5))
                t.wait_ge(vS, c)
                for g2 in range(16):
                    t.matmul(pY[:, g2 * 64:(g2 + 1) * 64], ystk[:, g2 * 64:(g2 + 1) * 64],
                             sv[:, cb, g2 * 64:(g2 + 1) * 64],
                             start=True, stop=True).then_inc(mmY, 1)
                t.wait_ge(vT, 3 * c + 1)
                for q2 in range(8):
                    t.transpose(pT[:, q2 * 64:(q2 + 1) * 64],
                                khat[:, q2 * 128:(q2 + 1) * 128],
                                identb[:, :]).then_inc(mmT, 1)
                t.wait_ge(vT, 3 * c + 3)
                for g2 in range(16):
                    src_t = khT0 if g2 % 2 == 0 else khT1
                    t.matmul(pSK[:, g2 * 64:(g2 + 1) * 64],
                             src_t[:, (g2 // 2) * 64:(g2 // 2) * 64 + 64],
                             vb0[:, cb, g2 * 64:(g2 + 1) * 64],
                             start=True, stop=True).then_inc(mmS, 1)
    al.close()
    return nc



def build_F1():
    nc = _new_nc()
    yn_i = nc.dram_tensor("yn_i", [HID, TOK], BF16, kind="ExternalInput").ap()
    g_i = nc.dram_tensor("g_i", [HID, TOK], BF16, kind="ExternalInput").ap()
    woT_i = nc.dram_tensor("woT_i", [HID, HID], F32, kind="ExternalInput").ap()
    x1_i = nc.dram_tensor("x1_i", [HID, TOK], F32, kind="ExternalInput").ap()
    o1_o = nc.dram_tensor("o1_o", [HID, TOK], F32, kind="ExternalOutput").ap()

    al = Alloc(nc)
    ygsb = al.sb("ygsb", [128, 16 * TOK], BF16)
    ynst = al.sb("ynst", [128, 2, TOK], BF16)
    gst = al.sb("gst", [128, 2, TOK], BF16)
    wo2 = al.sb("wo2_f1", [128, 2, 16, 128], BF16)
    x1r = al.sb("x1r_f1", [128, 2, TOK], F32)
    o1s = al.sb("o1s", [128, 2, TOK], F32)
    pa0 = al.ps("pa0", [128, 512], F32)
    pa1 = al.ps("pa1", [128, 512], F32)

    ld0a = al.sem("ld0a"); ld0b = al.sem("ld0b")
    ldwa = al.sem("ldwa"); ldwb = al.sem("ldwb")
    ldxa = al.sem("ldxa"); ldxb = al.sem("ldxb")
    mm1 = al.sem("mm1"); vv = al.sem("vv")
    stxa = al.sem("stxa"); stxb = al.sem("stxb")

    block = al.block()
    if True:
        @block.sync
        def _(sync):
            for ic in range(16):
                if ic >= 2:
                    sync.wait_ge(vv, ic - 1)
                l0 = ld0a if ic % 2 == 0 else ld0b
                sync.dma_start(out=ynst[:, ic % 2, :],
                               in_=yn_i[ic * 128:(ic + 1) * 128, :]).then_inc(l0, 16)
                sync.dma_start(out=gst[:, ic % 2, :],
                               in_=g_i[ic * 128:(ic + 1) * 128, :]).then_inc(l0, 16)
            for oc in range(16):
                if oc >= 2:
                    sync.wait_ge(vv, 16 + 2 * (oc - 1))
                sync.dma_start(out=x1r[:, oc % 2, :],
                               in_=x1_i[oc * 128:(oc + 1) * 128, :]).then_inc(
                                   ldxa if oc % 2 == 0 else ldxb, 16)
                sync.wait_ge(vv, 16 + 2 * (oc + 1))
                sync.dma_start(out=o1_o[oc * 128:(oc + 1) * 128, :],
                               in_=o1s[:, oc % 2, :]).then_inc(stxa if oc % 2 == 0 else stxb, 16)
            sync.wait_ge(stxa, 16 * 8)
            sync.wait_ge(stxb, 16 * 8)

        @block.gpsimd
        def _(g):
            for oc in range(16):
                if oc >= 2:
                    g.wait_ge(mm1, 2 * (oc - 1))
                g.dma_start(out=wo2[:, oc % 2, :, :],
                            in_=AP(woT_i.tensor, oc * 128,
                                   [[HID, 128], [HID * 128, 16], [1, 128]])).then_inc(
                                       ldwa if oc % 2 == 0 else ldwb, 16)

        @block.vector
        def _(v):
            for ic in range(16):
                v.wait_ge(ld0a if ic % 2 == 0 else ld0b, 32 * (ic // 2 + 1))
                v.tensor_tensor(out=ygsb[:, ic * TOK:(ic + 1) * TOK],
                                in0=ynst[:, ic % 2, :], in1=gst[:, ic % 2, :],
                                op=ALU.mult).then_inc(vv, 1)
            for oc in range(16):
                v.wait_ge(mm1, 2 * (oc + 1))
                v.wait_ge(ldxa if oc % 2 == 0 else ldxb, 16 * (oc // 2 + 1))
                if oc >= 2:
                    v.wait_ge(stxa if oc % 2 == 0 else stxb, 16 * (oc // 2))
                v.tensor_tensor(out=o1s[:, oc % 2, 0:512], in0=pa0[:, :],
                                in1=x1r[:, oc % 2, 0:512], op=ALU.add).then_inc(vv, 1)
                v.tensor_tensor(out=o1s[:, oc % 2, 512:1024], in0=pa1[:, :],
                                in1=x1r[:, oc % 2, 512:1024], op=ALU.add).then_inc(vv, 1)

        @block.tensor
        def _(t):
            t.wait_ge(vv, 16)
            for oc in range(16):
                t.wait_ge(ldwa if oc % 2 == 0 else ldwb, 16 * (oc // 2 + 1))
                if oc >= 1:
                    t.wait_ge(vv, 16 + 2 * oc)
                inst = None
                for dc in range(16):
                    inst = t.matmul(pa0[:, :], wo2[:, oc % 2, dc, :],
                                    ygsb[:, dc * TOK:dc * TOK + 512],
                                    start=(dc == 0), stop=(dc == 15))
                inst.then_inc(mm1, 1)
                inst = None
                for dc in range(16):
                    inst = t.matmul(pa1[:, :], wo2[:, oc % 2, dc, :],
                                    ygsb[:, dc * TOK + 512:dc * TOK + 1024],
                                    start=(dc == 0), stop=(dc == 15))
                inst.then_inc(mm1, 1)
    al.close()
    return nc



def build_F2():
    nc = _new_nc()
    xn1b = nc.dram_tensor("xn1b", [HID, TOK], BF16, kind="ExternalInput").ap()
    o1_i = nc.dram_tensor("o1_i", [HID, TOK], F32, kind="ExternalInput").ap()
    smy_i = nc.dram_tensor("smy_i", [8, TOK], F32, kind="ExternalInput").ap()
    smf_i = nc.dram_tensor("smf_i", [8, 8192], F32, kind="ExternalInput").ap()
    wg_i = nc.dram_tensor("wg_i", [HID, DFF1P], F32, kind="ExternalInput").ap()
    wu_i = nc.dram_tensor("wu_i", [HID, DFF1P], F32, kind="ExternalInput").ap()
    wd_i = nc.dram_tensor("wd_i", [DFF1P, HID], F32, kind="ExternalInput").ap()
    bgc_i = nc.dram_tensor("bgc_i", [128, 43], F32, kind="ExternalInput").ap()
    buc_i = nc.dram_tensor("buc_i", [128, 43], F32, kind="ExternalInput").ap()
    bdc_i = nc.dram_tensor("bdc_i", [128, 16], F32, kind="ExternalInput").ap()
    out_o = nc.dram_tensor("out_o", [HID, TOK], F32, kind="ExternalOutput").ap()

    al = Alloc(nc)
    smfs = al.sb("smfs", [8, 8192], F32)
    mskb = al.sb("mskb", [8, 8192], BF16)
    smys = al.sb("smys", [8, TOK], F32)
    lo = al.sb("lo", [8, 2], F32)
    hi = al.sb("hi", [8, 2], F32)
    midc = al.sb("midc", [8, 1], F32)
    cnt = al.sb("cnt", [8, 1], F32)
    sel = al.sb("sel", [8, 1], F32)
    tt1 = al.sb("tt1", [8, 1], F32)
    tt2 = al.sb("tt2", [8, 1], F32)
    kcol = al.sb("kcol", [8, 1], F32)
    ones8 = al.sb("ones8", [8, 1], F32)
    onec = al.sb("onec_f2", [1, 128], F32)
    msky = al.sb("msky", [8, TOK], F32)
    smm = al.sb("smm", [8, TOK], F32)
    srow = al.sb("srow", [1, TOK], F32)
    srep = al.sb("srep_f2", [128, TOK], F32)
    bgcs = al.sb("bgcs", [128, 43], F32)
    bucs = al.sb("bucs", [128, 43], F32)
    bdcs = al.sb("bdcs", [128, 16], F32)
    xnp = al.sb("xnp", [128, 16, 512], BF16)
    hsb = al.sb("hsb_f2", [128, 43 * 512], BF16)
    wg2 = al.sb("wg2_f2", [128, 2, 16, 128], BF16)
    wu2 = al.sb("wu2_f2", [128, 2, 16, 128], BF16)
    wd2 = al.sb("wd2_f2", [128, 2, 43, 128], BF16)
    pgb = al.sb("pgb", [128, 2, 512], F32)
    pub = al.sb("pub", [128, 2, 512], F32)
    thsb = al.sb("thsb_f2", [128, 2, 512], F32)
    v1sb = al.sb("v1sb_f2", [128, 2, 512], F32)
    dcb = al.sb("dcb", [128, 2, 512], F32)
    tsc = al.sb("tsc", [128, 2, 512], F32)
    o1r = al.sb("o1r", [128, 2, 512], F32)
    outs = al.sb("outs", [128, 2, 512], F32)

    pg0 = al.ps("pg0_f2", [128, 512], F32)
    pg1 = al.ps("pg1_f2", [128, 512], F32)
    pu0 = al.ps("pu0_f2", [128, 512], F32)
    pu1 = al.ps("pu1_f2", [128, 512], F32)
    pdn = al.ps("pdn_f2", [128, 512], F32)

    ldp = al.sem("ldp"); ldx = al.sem("ldx"); ldo = al.sem("ldo")
    ldw0 = al.sem("ldw0"); ldw1a = al.sem("ldw1a"); ldw1b = al.sem("ldw1b")
    vchf = al.sem("vchf")
    gset = al.sem("gset")
    vpro = al.sem("vpro"); mmpro = al.sem("mmpro")
    mmg = al.sem("mmg"); mmu = al.sem("mmu"); mmc = al.sem("mmc")
    vh = al.sem("vh"); vh2 = al.sem("vh2"); sh = al.sem("sh")
    vc = al.sem("vc"); vo = al.sem("vo")
    stoc = al.sem("stoc"); stod = al.sem("stod")

    NIC = 43
    NBIS = 42
    block = al.block()
    if True:
        @block.sync
        def _(sync):
            sync.dma_start(out=smfs[:, :], in_=smf_i[:, :]).then_inc(ldp, 16)
            sync.dma_start(out=smys[:, :], in_=smy_i[:, :]).then_inc(ldp, 16)
            sync.dma_start(out=bgcs[:, :], in_=bgc_i[:, :]).then_inc(ldp, 16)
            sync.dma_start(out=bucs[:, :], in_=buc_i[:, :]).then_inc(ldp, 16)
            sync.dma_start(out=bdcs[:, :], in_=bdc_i[:, :]).then_inc(ldp, 16)
            for tp in range(2):
                if tp == 1:
                    sync.wait_ge(mmu, NIC)
                for dc in range(16):
                    sync.dma_start(out=xnp[:, dc, :],
                                   in_=xn1b[dc * 128:(dc + 1) * 128,
                                            tp * 512:(tp + 1) * 512]).then_inc(ldx, 16)
                for oc in range(16):
                    rr = tp * 16 + oc
                    if rr >= 2:
                        sync.wait_ge(vo, rr - 1)
                    sync.dma_start(out=o1r[:, rr % 2, :],
                                   in_=o1_i[oc * 128:(oc + 1) * 128,
                                            tp * 512:(tp + 1) * 512]).then_inc(ldo, 16)
                    if rr >= 1:
                        sync.wait_ge(vo, rr)
                        po, pt = (rr - 1) % 16, (rr - 1) // 16
                        sync.dma_start(out=out_o[po * 128:(po + 1) * 128,
                                                 pt * 512:(pt + 1) * 512],
                                       in_=outs[:, (rr - 1) % 2, :]).then_inc(
                                           stoc if (rr - 1) % 2 == 0 else stod, 16)
            sync.wait_ge(vo, 32)
            sync.dma_start(out=out_o[15 * 128:16 * 128, 512:1024],
                           in_=outs[:, 31 % 2, :]).then_inc(stod, 16)
            sync.wait_ge(stoc, 16 * 16)
            sync.wait_ge(stod, 16 * 16)

        @block.gpsimd
        def _(g):
            g.memset(kcol[:], float(CAP_K)).then_inc(gset, 1)
            g.memset(ones8[:], 1.0).then_inc(gset, 1)
            g.memset(onec[:], 1.0).then_inc(gset, 1)
            g.memset(lo[:], 0.0).then_inc(gset, 1)
            g.memset(hi[:], 1.0).then_inc(gset, 1)
            for tp in range(2):
                for f in range(NIC):
                    r = tp * NIC + f
                    if r >= 2:
                        g.wait_ge(vh, r - 1)
                    g.dma_start(out=wg2[:, r % 2, :, :],
                                in_=AP(wg_i.tensor, f * 128,
                                       [[DFF1P, 128], [DFF1P * 128, 16], [1, 128]])).then_inc(ldw0, 16)
                    g.dma_start(out=wu2[:, r % 2, :, :],
                                in_=AP(wu_i.tensor, f * 128,
                                       [[DFF1P, 128], [DFF1P * 128, 16], [1, 128]])).then_inc(ldw0, 16)
                for oc in range(16):
                    rr = tp * 16 + oc
                    if rr >= 2:
                        g.wait_ge(mmc, rr - 1)
                    g.dma_start(out=wd2[:, rr % 2, :, :],
                                in_=AP(wd_i.tensor, oc * 128,
                                       [[HID, 128], [HID * 128, 43], [1, 128]])).then_inc(
                                           ldw1a if rr % 2 == 0 else ldw1b, 16)

        @block.scalar
        def _(s):
            for tp in range(2):
                for f in range(NIC):
                    r = tp * NIC + f
                    s.wait_ge(vh2, 2 * r + 1)
                    s.activation(out=thsb[:, r % 2, :], in_=pgb[:, r % 2, :],
                                 func=AF.Tanh, bias=0.0, scale=0.5).then_inc(sh, 1)

        @block.vector
        def _(v):
            VC = Chain(v, vchf)
            v.wait_ge(ldp, 80)
            v.wait_ge(gset, 5)
            for it in range(NBIS):
                i2, o2 = it % 2, (it + 1) % 2
                VC(v.tensor_tensor(out=tt1[:, :], in0=lo[:, i2:i2 + 1],
                                   in1=hi[:, i2:i2 + 1], op=ALU.add))
                VC(v.tensor_scalar(out=midc[:, :], in0=tt1[:, :], scalar1=0.5,
                                   scalar2=None, op0=ALU.mult))
                VC(v.tensor_scalar(out=mskb[:, :], in0=smfs[:, :],
                                   scalar1=midc[:, 0:1], scalar2=None, op0=ALU.is_ge))
                VC(v.reduce_sum(out=cnt[:, :], in_=mskb[:, :], axis=AX))
                VC(v.tensor_scalar(out=sel[:, :], in0=cnt[:, :],
                                   scalar1=kcol[:, 0:1], scalar2=None, op0=ALU.is_ge))
                VC(v.tensor_tensor(out=tt1[:, :], in0=midc[:, :], in1=lo[:, i2:i2 + 1],
                                   op=ALU.subtract))
                VC(v.tensor_tensor(out=tt2[:, :], in0=sel[:, :], in1=tt1[:, :], op=ALU.mult))
                VC(v.tensor_tensor(out=lo[:, o2:o2 + 1], in0=lo[:, i2:i2 + 1],
                                   in1=tt2[:, :], op=ALU.add))
                VC(v.tensor_tensor(out=tt1[:, :], in0=hi[:, i2:i2 + 1], in1=midc[:, :],
                                   op=ALU.subtract))
                VC(v.tensor_tensor(out=tt2[:, :], in0=sel[:, :], in1=tt1[:, :], op=ALU.mult))
                VC(v.tensor_tensor(out=hi[:, o2:o2 + 1], in0=midc[:, :], in1=tt2[:, :],
                                   op=ALU.add))
            tf = NBIS % 2
            VC(v.tensor_scalar(out=msky[:, :], in0=smys[:, :],
                               scalar1=lo[:, tf:tf + 1], scalar2=None, op0=ALU.is_ge))
            v.tensor_tensor(out=smm[:, :], in0=smys[:, :], in1=msky[:, :],
                            op=ALU.mult).then_inc(vpro, 1)
            v.wait_ge(mmpro, 2)
            v.tensor_copy(out=srow[0:1, 0:512], in_=pg0[0:1, :]).then_inc(vpro, 1)
            v.tensor_copy(out=srow[0:1, 512:1024], in_=pg1[0:1, :]).then_inc(vpro, 1)
            v.wait_ge(mmpro, 4)
            v.tensor_copy(out=srep[:, 0:512], in_=pu0[:, :])
            v.tensor_copy(out=srep[:, 512:1024], in_=pu1[:, :]).then_inc(vpro, 1)
            for tp in range(2):
                for f in range(NIC):
                    r = tp * NIC + f
                    if tp == 1 and f == 0:
                        v.wait_ge(mmc, 16)
                    v.wait_ge(mmg, r + 1)
                    pg = pg0 if r % 2 == 0 else pg1
                    pu = pu0 if r % 2 == 0 else pu1
                    v.tensor_scalar(out=pgb[:, r % 2, :], in0=pg[:, :],
                                    scalar1=bgcs[:, f:f + 1], scalar2=None,
                                    op0=ALU.add).then_inc(vh2, 1)
                    v.wait_ge(mmu, r + 1)
                    v.tensor_scalar(out=pub[:, r % 2, :], in0=pu[:, :],
                                    scalar1=bucs[:, f:f + 1], scalar2=None,
                                    op0=ALU.add).then_inc(vh2, 1)
                    v.wait_ge(sh, r + 1)
                    v.scalar_tensor_tensor(out=v1sb[:, r % 2, :], in0=thsb[:, r % 2, :],
                                           scalar=1.0, in1=pgb[:, r % 2, :],
                                           op0=ALU.add, op1=ALU.mult)
                    v.scalar_tensor_tensor(out=hsb[:, f * 512:(f + 1) * 512],
                                           in0=v1sb[:, r % 2, :], scalar=0.5,
                                           in1=pub[:, r % 2, :],
                                           op0=ALU.mult, op1=ALU.mult).then_inc(vh, 1)
                for oc in range(16):
                    rr = tp * 16 + oc
                    v.wait_ge(mmc, rr + 1)
                    v.tensor_scalar(out=dcb[:, rr % 2, :], in0=pdn[:, :],
                                    scalar1=bdcs[:, oc:oc + 1], scalar2=None,
                                    op0=ALU.add).then_inc(vc, 1)
                    v.tensor_tensor(out=tsc[:, rr % 2, :], in0=dcb[:, rr % 2, :],
                                    in1=srep[:, tp * 512:(tp + 1) * 512], op=ALU.mult)
                    v.wait_ge(ldo, 16 * (rr + 1))
                    if rr >= 2:
                        v.wait_ge(stoc if rr % 2 == 0 else stod, 16 * (rr // 2))
                    v.tensor_tensor(out=outs[:, rr % 2, :], in0=tsc[:, rr % 2, :],
                                    in1=o1r[:, rr % 2, :], op=ALU.add).then_inc(vo, 1)

        @block.tensor
        def _(t):
            t.wait_ge(vpro, 1)
            t.wait_ge(gset, 3)
            t.matmul(pg0[0:1, :], ones8[:, 0:1], smm[:, 0:512],
                     start=True, stop=True).then_inc(mmpro, 1)
            t.matmul(pg1[0:1, :], ones8[:, 0:1], smm[:, 512:1024],
                     start=True, stop=True).then_inc(mmpro, 1)
            t.wait_ge(vpro, 3)
            t.matmul(pu0[:, :], onec[0:1, :], srow[0:1, 0:512],
                     start=True, stop=True).then_inc(mmpro, 1)
            t.matmul(pu1[:, :], onec[0:1, :], srow[0:1, 512:1024],
                     start=True, stop=True).then_inc(mmpro, 1)
            t.wait_ge(vpro, 4)
            for tp in range(2):
                for f in range(NIC):
                    r = tp * NIC + f
                    t.wait_ge(ldx, 16 * 16 * (tp + 1))
                    t.wait_ge(ldw0, 32 * (r + 1))
                    if r >= 2:
                        t.wait_ge(vh2, 2 * (r - 1))
                    pg = pg0 if r % 2 == 0 else pg1
                    pu = pu0 if r % 2 == 0 else pu1
                    inst = None
                    for dc in range(16):
                        inst = t.matmul(pg[:, :], wg2[:, r % 2, dc, :], xnp[:, dc, :],
                                        start=(dc == 0), stop=(dc == 15))
                    inst.then_inc(mmg, 1)
                    inst = None
                    for dc in range(16):
                        inst = t.matmul(pu[:, :], wu2[:, r % 2, dc, :], xnp[:, dc, :],
                                        start=(dc == 0), stop=(dc == 15))
                    inst.then_inc(mmu, 1)
                for oc in range(16):
                    rr = tp * 16 + oc
                    t.wait_ge(ldw1a if rr % 2 == 0 else ldw1b, 16 * (rr // 2 + 1))
                    t.wait_ge(vh, NIC * (tp + 1))
                    t.wait_ge(vc, rr)
                    inst = None
                    for ic in range(NIC):
                        inst = t.matmul(pdn[:, :], wd2[:, rr % 2, ic, :],
                                        hsb[:, ic * 512:(ic + 1) * 512],
                                        start=(ic == 0), stop=(ic == NIC - 1))
                    inst.then_inc(mmc, 1)
    al.close()
    return nc


_NC_CACHE = {}


def _get_nc(name, builder):
    if name not in _NC_CACHE:
        _NC_CACHE[name] = builder()
    return _NC_CACHE[name]


_WARMED = set()


def _run(name, builder, in_maps):
    nc = _get_nc(name, builder)
    if name not in _WARMED:
        run_bass_kernel_spmd(nc, in_maps, core_ids=list(range(NCORE)))
        _WARMED.add(name)
    res = run_bass_kernel_spmd(nc, in_maps, core_ids=list(range(NCORE)))
    return res.results


def kernel(**inputs):
    f32 = np.float32
    ca = np.ascontiguousarray
    x = inputs["L0_attn_hidden_states"]
    xf = ca(np.asarray(x, f32).reshape(B * SEQ, HID))
    MT = _rel_bucket_table()
    GMB = _gumbel_const()

    gT = ca(np.asarray(inputs["L0_ffn_gate_idx"]).T).astype(np.int32)
    uT = ca(np.asarray(inputs["L0_ffn_up_idx"]).T).astype(np.int32)
    dT = ca(np.asarray(inputs["L0_ffn_down_idx"]).T).astype(np.int32)
    lut = np.asarray(inputs["L0_ffn_lut"], f32).reshape(1, 16)
    mapsA = []
    for c in range(NCORE):
        sl = slice(1024 * c, 1024 * c + 1024)
        mapsA.append({"x0": ca(xf[sl]),
                      "gT_idx": ca(gT[256 * c:256 * c + 256]),
                      "uT_idx": ca(uT[256 * c:256 * c + 256]),
                      "d_idx": ca(dT[688 * c:688 * c + 688]),
                      "lut": lut})
    resA = _run("A", build_A, mapsA)
    xn_cm = [np.asarray(resA[c]["xn_cm"]) for c in range(NCORE)]
    WgT = np.concatenate([np.asarray(resA[c]["WgT"]) for c in range(NCORE)], 0)
    WuT = np.concatenate([np.asarray(resA[c]["WuT"]) for c in range(NCORE)], 0)
    Wd = np.concatenate([np.asarray(resA[c]["Wd"]) for c in range(NCORE)], 0)
    xn_full = [ca(np.concatenate([xn_cm[2 * b], xn_cm[2 * b + 1]], 1))
               for b in range(B)]

    Wq = np.asarray(inputs["L0_attn_Wq"], f32)
    Wk = np.asarray(inputs["L0_attn_Wk"], f32)
    Wv = np.asarray(inputs["L0_attn_Wv"], f32)
    relb = np.asarray(inputs["L0_attn_rel_bias"], f32)
    mapsB = []
    for c in range(NCORE):
        b, p = c // 2, c % 2
        cs = slice(1024 * p, 1024 * p + 1024)
        mapsB.append({"xn": xn_full[b], "wq": ca(Wq[:, cs]), "wk": ca(Wk[:, cs]),
                      "wv": ca(Wv[:, cs]), "rb": ca(relb[:, 16 * p:16 * p + 16]),
                      "mt": MT})
    resB = _run("B", build_B, mapsB)
    o_full = [np.concatenate([np.asarray(resB[2 * b]["o_cm"]),
                              np.asarray(resB[2 * b + 1]["o_cm"])], 0)
              for b in range(B)]

    Wo0 = np.asarray(inputs["L0_attn_Wo"], f32)
    rw = np.asarray(inputs["L1_ffn_router_w"], f32)
    rbv = np.asarray(inputs["L1_ffn_router_b"], f32).reshape(1, 8)
    mapsC = []
    for c in range(NCORE):
        b, p = c // 2, c % 2
        ts_ = slice(1024 * p, 1024 * p + 1024)
        mapsC.append({"o_cm": ca(o_full[b][:, ts_]), "wo": Wo0,
                      "xn0": xn_cm[c], "WgT": WgT, "WuT": WuT, "Wd": Wd,
                      "x0": ca(xf[1024 * c:1024 * c + 1024]),
                      "rw": rw, "rbv": rbv,
                      "gmb": ca(GMB[1024 * c:1024 * c + 1024])})
    resC = _run("C", build_C, mapsC)

    maas = np.concatenate(
        [ca(np.asarray(inputs["L1_attn_time_maa_" + nm], f32).reshape(16, 128).T)
         for nm in ("x", "w", "k", "v", "r", "g")]
        + [ca(np.asarray(inputs["L1_attn_time_decay"], f32).reshape(16, 128).T)],
        axis=1)
    w1 = np.asarray(inputs["L1_attn_time_maa_w1"], f32)
    w2 = np.asarray(inputs["L1_attn_time_maa_w2"], f32)
    dw1 = np.asarray(inputs["L1_attn_time_decay_w1"], f32)
    dw2 = np.asarray(inputs["L1_attn_time_decay_w2"], f32)
    wrT = ca(np.asarray(inputs["L1_attn_W_r"], f32).T)
    wkT = ca(np.asarray(inputs["L1_attn_W_k"], f32).T)
    wvT = ca(np.asarray(inputs["L1_attn_W_v"], f32).T)
    wgT = ca(np.asarray(inputs["L1_attn_W_g"], f32).T)
    mapsD = []
    for c in range(NCORE):
        b, p = c // 2, c % 2
        if p == 0:
            pcol = np.zeros((128, 16), f32)
        else:
            prev = np.asarray(resC[2 * b]["xn1o"])[:, 1023]
            pcol = ca(prev.reshape(16, 128).T)
        mapsD.append({"xn1b": np.asarray(resC[c]["xn1bo"]), "pcol": pcol,
                      "maas": maas, "w1": w1, "w2": w2, "dw1": dw1, "dw2": dw2,
                      "wrT": wrT, "wkT": wkT, "wvT": wvT, "wgT": wgT})
    resD = _run("D", build_D, mapsD)

    st = np.asarray(inputs["L1_attn_time_state"], f32)
    fa = np.asarray(inputs["L1_attn_time_faaaa"], f32)
    lnw = np.asarray(inputs["L1_attn_ln_w"], f32)
    lnb = np.asarray(inputs["L1_attn_ln_b"], f32)
    mapsE = []
    for c in range(NCORE):
        b, p = c // 2, c % 2
        rows = slice(1024 * p, 1024 * p + 1024)
        mapsE.append({
            "r_i": ca(np.concatenate([np.asarray(resD[2 * b]["r_o"])[rows],
                                      np.asarray(resD[2 * b + 1]["r_o"])[rows]], 1)),
            "k_i": ca(np.concatenate([np.asarray(resD[2 * b]["k_o"])[rows],
                                      np.asarray(resD[2 * b + 1]["k_o"])[rows]], 1)),
            "w_i": ca(np.concatenate([np.asarray(resD[2 * b]["w_o"])[rows],
                                      np.asarray(resD[2 * b + 1]["w_o"])[rows]], 1)),
            "v_i": ca(np.concatenate([np.asarray(resD[2 * b]["v_o"])[:, rows],
                                      np.asarray(resD[2 * b + 1]["v_o"])[:, rows]], 0)),
            "st_i": ca(st[16 * p:16 * p + 16].reshape(1024, 64)),
            "faT_i": ca(fa[16 * p:16 * p + 16].T),
            "lnw_i": ca(lnw[rows].reshape(1, 1024)),
            "lnb_i": ca(lnb[rows].reshape(1, 1024))})
    resE = _run("E", build_E, mapsE)

    woT1 = ca(np.asarray(inputs["L1_attn_W_o"], f32).T)
    yn_cm = [ca(np.concatenate([np.asarray(resE[2 * b]["yn_o"]).T,
                                np.asarray(resE[2 * b + 1]["yn_o"]).T], 0))
             for b in range(B)]
    mapsF1 = []
    for c in range(NCORE):
        b, p = c // 2, c % 2
        ts_ = slice(1024 * p, 1024 * p + 1024)
        mapsF1.append({"yn_i": ca(yn_cm[b][:, ts_]),
                       "g_i": np.asarray(resD[c]["g_o"]),
                       "woT_i": woT1, "x1_i": np.asarray(resC[c]["x1o"])})
    resF1 = _run("F1", build_F1, mapsF1)

    smf = ca(np.concatenate([np.asarray(resC[c]["smo"]) for c in range(NCORE)],
                            0).T)
    wgp = np.zeros((HID, DFF1P), f32)
    wgp[:, :DFF1] = np.asarray(inputs["L1_ffn_w_gate"], f32)
    wup = np.zeros((HID, DFF1P), f32)
    wup[:, :DFF1] = np.asarray(inputs["L1_ffn_w_up"], f32)
    wdp = np.zeros((DFF1P, HID), f32)
    wdp[:DFF1, :] = np.asarray(inputs["L1_ffn_w_down"], f32)
    bgp = np.zeros(DFF1P, f32); bgp[:DFF1] = np.asarray(inputs["L1_ffn_b_gate"], f32)
    bup = np.zeros(DFF1P, f32); bup[:DFF1] = np.asarray(inputs["L1_ffn_b_up"], f32)
    bgc = ca(bgp.reshape(43, 128).T)
    buc = ca(bup.reshape(43, 128).T)
    bdc = ca(np.asarray(inputs["L1_ffn_b_down"], f32).reshape(16, 128).T)
    mapsF2 = []
    for c in range(NCORE):
        mapsF2.append({"xn1b": np.asarray(resC[c]["xn1bo"]),
                       "o1_i": np.asarray(resF1[c]["o1_o"]),
                       "smy_i": ca(smf[:, 1024 * c:1024 * c + 1024]),
                       "smf_i": smf, "wg_i": wgp, "wu_i": wup, "wd_i": wdp,
                       "bgc_i": bgc, "buc_i": buc, "bdc_i": bdc})
    resF2 = _run("F2", build_F2, mapsF2)

    out = np.empty((B * SEQ, HID), f32)
    for c in range(NCORE):
        out[1024 * c:1024 * c + 1024, :] = np.asarray(resF2[c]["out_o"]).T
    return out.reshape(B, SEQ, HID)


# version 08 (final)
# speedup vs baseline: 1.0028x; 1.0028x over previous
import math
from contextlib import ExitStack
import numpy as np
import ml_dtypes
import concourse.bass as bass
import concourse.mybir as mybir
from concourse.bass_utils import run_bass_kernel_spmd
from concourse.ap import AP

F32 = mybir.dt.float32
BF16 = mybir.dt.bfloat16
I32 = mybir.dt.int32
AF = mybir.ActivationFunctionType
ALU = mybir.AluOpType
AX = mybir.AxisListType.X

BF = ml_dtypes.bfloat16

HID = 2048
SEQ = 2048
B = 4
NCORE = 8
TOK = 1024
NH = 32
DK = 64
HPC = 16
INTER0 = 5504
DFF1 = 5461
DFF1P = 5504
NBUCK = 32
LN_EPS = 1e-5
GN_EPS = 1e-5 * 64.0
CHUNK = 64
NCHUNK = SEQ // CHUNK
CAP_K = 1024
DFF1P = 5504



def _rel_bucket_table():
    d = np.arange(-2047, 2048)
    rp = np.maximum(d, 0)
    max_exact = NBUCK // 2
    is_small = rp < max_exact
    with np.errstate(divide="ignore"):
        rp_large = max_exact + (
            np.log(np.maximum(rp, 1).astype(np.float32) / max_exact)
            / math.log(128 / max_exact) * (NBUCK - max_exact)
        ).astype(np.int32)
    rp_large = np.minimum(rp_large, NBUCK - 1)
    bucket = np.where(is_small, rp, rp_large)
    M_T = np.zeros((NBUCK, 4095), np.float32)
    M_T[bucket, np.arange(4095)] = 1.0
    return M_T


def _threefry2x32(k0, k1, x0, x1):
    def rotl(v, d):
        return ((v << np.uint32(d)) | (v >> np.uint32(32 - d))).astype(np.uint32)
    rot = [13, 15, 26, 6, 17, 29, 16, 24]
    ks0, ks1 = np.uint32(k0), np.uint32(k1)
    ks2 = np.uint32(ks0 ^ ks1 ^ np.uint32(0x1BD11BDA))
    x0 = (x0 + ks0).astype(np.uint32)
    x1 = (x1 + ks1).astype(np.uint32)
    keys = [(ks1, ks2), (ks2, ks0), (ks0, ks1), (ks1, ks2), (ks2, ks0)]
    for g in range(5):
        for r in (rot[0:4] if g % 2 == 0 else rot[4:8]):
            x0 = (x0 + x1).astype(np.uint32)
            x1 = rotl(x1, r)
            x1 = (x1 ^ x0).astype(np.uint32)
        a, b = keys[g]
        x0 = (x0 + a).astype(np.uint32)
        x1 = (x1 + b + np.uint32(g + 1)).astype(np.uint32)
    return x0, x1


def _gumbel_const():
    n = B * SEQ * 8
    i = np.arange(n, dtype=np.uint32)
    o0, o1 = _threefry2x32(0, 42, np.zeros(n, np.uint32), i)
    bits = (o0 ^ o1).astype(np.uint32)
    u = (((bits >> np.uint32(9)) | np.uint32(0x3F800000)).view(np.float32)
         - np.float32(1.0)).reshape(B * SEQ, 8)
    eps = np.float32(1e-10)
    return (-np.log(-np.log(u + eps) + eps)).astype(np.float32)


def _new_nc():
    return bass.Bass("TRN2", target_bir_lowering=False, debug=False,
                     detect_race_conditions=False)


class Chain:

    def __init__(self, eng, sem):
        self.eng = eng
        self.sem = sem
        self.n = 0

    def __call__(self, inst):
        self.n += 16
        inst.then_inc(self.sem, 16)
        self.eng.wait_ge(self.sem, self.n)
        return inst


class Alloc:

    def __init__(self, nc):
        self.nc = nc
        self.es = ExitStack()

    def sb(self, name, shape, dtype):
        return self.es.enter_context(self.nc.sbuf_tensor(name, shape, dtype))

    def ps(self, name, shape, dtype=F32):
        return self.es.enter_context(self.nc.psum_tensor(name, shape, dtype))

    def sem(self, name):
        return self.es.enter_context(self.nc.semaphore(name))

    def block(self):
        return self.es.enter_context(self.nc.Block())

    def close(self):
        self.es.close()



def build_A():
    nc = _new_nc()
    x0 = nc.dram_tensor("x0", [TOK, HID], F32, kind="ExternalInput").ap()
    gT_idx = nc.dram_tensor("gT_idx", [256, INTER0], I32, kind="ExternalInput").ap()
    uT_idx = nc.dram_tensor("uT_idx", [256, INTER0], I32, kind="ExternalInput").ap()
    d_idx = nc.dram_tensor("d_idx", [688, HID], I32, kind="ExternalInput").ap()
    lut = nc.dram_tensor("lut", [1, 16], F32, kind="ExternalInput").ap()

    xn_cm = nc.dram_tensor("xn_cm", [HID, TOK], BF16, kind="ExternalOutput").ap()
    WgT = nc.dram_tensor("WgT", [256, INTER0], BF16, kind="ExternalOutput").ap()
    WuT = nc.dram_tensor("WuT", [256, INTER0], BF16, kind="ExternalOutput").ap()
    Wd = nc.dram_tensor("Wd", [688, HID], BF16, kind="ExternalOutput").ap()
    dbg = nc.dram_tensor("dbg", [128, 32], F32, kind="ExternalOutput").ap()

    mat_tiles = []
    for src, dst, rows, cols in ((gT_idx, WgT, 256, INTER0),
                                 (uT_idx, WuT, 256, INTER0),
                                 (d_idx, Wd, 688, HID)):
        r = 0
        while r < rows:
            pr = min(128, rows - r)
            mat_tiles.append((src, dst, r, pr, cols))
            r += pr
    NMT = len(mat_tiles)

    al = Alloc(nc)
    xt = al.sb("xt", [128, 2, HID], F32)
    xsq = al.sb("xsq", [128, HID], F32)
    rstd = al.sb("rstd", [128, 8], F32)
    sc = al.sb("sc", [128, 6], F32)
    xnb = al.sb("xnb", [128, 8 * HID], BF16)
    xncm = al.sb("xncm", [128, 16 * TOK], BF16)
    ident = al.sb("ident", [128, 128], BF16)
    ioti = al.sb("ioti", [128, 128], I32)
    lutsb = al.sb("lutsb", [1, 16], F32)
    lutb = al.sb("lutb", [1, 16], BF16)
    lutrep = al.sb("lutrep", [128, 16], F32)
    ones1 = al.sb("ones1", [1, 128], BF16)
    idxi = al.sb("idxi", [128, 2, INTER0], I32)
    idxb = al.sb("idxb", [128, INTER0], BF16)
    wmat = al.sb("wmat", [128, INTER0], BF16)
    term = al.sb("term", [128, INTER0], BF16)
    ptr0 = al.ps("ptr0", [128, 128], BF16)
    ptr1 = al.ps("ptr1", [128, 128], BF16)
    plut = al.ps("plut", [128, 16], F32)
    warm = al.sb("warm", [1, 8], F32)
    wrm0 = al.sem("wrm0"); wrm1 = al.sem("wrm1"); wrm2 = al.sem("wrm2")
    ldl = al.sem("ldl")
    ld0 = al.sem("ld0")
    ld1 = al.sem("ld1")
    ldm0 = al.sem("ldm0")
    ldm1 = al.sem("ldm1")
    va = al.sem("va")
    sa = al.sem("sa")
    vb = al.sem("vb")
    tr = al.sem("tr")
    ev = al.sem("ev")
    mt = al.sem("mt")
    ms = al.sem("ms")
    pre = al.sem("pre")
    so = al.sem("so")
    block = al.block()
    if True:
        @block.sync
        def _(sync):
            for w_ in range(3):
                sync.dma_start(out=warm[0:1, w_:w_ + 1],
                               in_=AP(lut.tensor, 0, [[1, 1], [1, 1]])).then_inc(wrm0, 16)
                sync.wait_ge(wrm0, 16 * (w_ + 1))
            sync.dma_start(out=lutsb[:], in_=lut[:, :]).then_inc(ldl, 16)
            for i in range(8):
                if i >= 2:
                    sync.wait_ge(vb, i - 1)
                sync.dma_start(out=xt[:, i % 2, :],
                               in_=x0[i * 128:(i + 1) * 128, :]).then_inc(ld0 if i % 2 == 0 else ld1, 16)
            for dc in range(16):
                sync.wait_ge(ev, (dc + 1) * 8)
                sync.dma_start(out=xn_cm[dc * 128:(dc + 1) * 128, :],
                               in_=xncm[:, dc * TOK:(dc + 1) * TOK]).then_inc(so, 16)
            sync.wait_ge(so, 16 * 16)
            sync.dma_start(out=dbg[:, 0:8], in_=xsq[:, 0:8]).then_inc(so, 16)
            sync.wait_ge(so, 16 * 17)

        @block.gpsimd
        def _(g):
            for w_ in range(3):
                g.dma_start(out=warm[0:1, 3 + w_:4 + w_],
                            in_=AP(lut.tensor, 0, [[1, 1], [1, 1]])).then_inc(wrm1, 16)
                g.wait_ge(wrm1, 16 * (w_ + 1))
            g.iota(ioti[:], pattern=[[1, 128]], base=0, channel_multiplier=-1).then_inc(pre, 1)
            g.memset(ones1[:], 1.0).then_inc(pre, 1)
            for j, (src, dst, r0, pr, cols) in enumerate(mat_tiles):
                if j >= 2:
                    g.wait_ge(mt, j - 1)
                g.dma_start(out=idxi[0:pr, j % 2, 0:cols],
                            in_=src[r0:r0 + pr, :]).then_inc(ldm0 if j % 2 == 0 else ldm1, 16)


        @block.vector
        def _(v):
            v.wait_ge(pre, 1)
            v.tensor_single_scalar(out=ident[:], in_=ioti[:], scalar=0,
                                   op=ALU.is_equal).then_inc(pre, 1)
            v.wait_ge(ldl, 16)
            v.tensor_copy(out=lutb[:], in_=lutsb[:]).then_inc(pre, 1)
            for i in range(8):
                v.wait_ge(ld0 if i % 2 == 0 else ld1, 16 * (i // 2 + 1))
                x = xt[:, i % 2, :]
                v.tensor_tensor(out=xsq[:], in0=x, in1=x, op=ALU.mult)
                v.reduce_sum(out=sc[:, 0:1], in_=x, axis=AX)
                v.reduce_sum(out=sc[:, 1:2], in_=xsq[:], axis=AX)
                v.tensor_scalar(out=sc[:, 0:1], in0=sc[:, 0:1], scalar1=1.0 / HID,
                                scalar2=None, op0=ALU.mult)
                v.tensor_scalar(out=sc[:, 1:2], in0=sc[:, 1:2], scalar1=1.0 / HID,
                                scalar2=None, op0=ALU.mult)
                v.tensor_tensor(out=sc[:, 2:3], in0=sc[:, 0:1], in1=sc[:, 0:1], op=ALU.mult)
                v.tensor_tensor(out=sc[:, 3:4], in0=sc[:, 1:2], in1=sc[:, 2:3],
                                op=ALU.subtract)
                v.tensor_scalar(out=sc[:, 4:5], in0=sc[:, 0:1], scalar1=-1.0,
                                scalar2=None, op0=ALU.mult)
                v.tensor_scalar(out=sc[:, 3:4], in0=sc[:, 3:4], scalar1=LN_EPS,
                                scalar2=None, op0=ALU.add).then_inc(va, 1)
                v.wait_ge(sa, i + 1)
                v.tensor_scalar(out=xnb[:, i * HID:(i + 1) * HID], in0=x,
                                scalar1=sc[:, 4:5], scalar2=rstd[:, i:i + 1],
                                op0=ALU.add, op1=ALU.mult).then_inc(vb, 1)
            v.tensor_copy(out=xsq[:, 0:8], in_=rstd[:, 0:8])
            for dc in range(16):
                for i in range(8):
                    j = dc * 8 + i
                    v.wait_ge(tr, j + 1)
                    v.tensor_copy(out=xncm[:, dc * TOK + i * 128: dc * TOK + (i + 1) * 128],
                                  in_=(ptr0 if j % 2 == 0 else ptr1)[:, :]).then_inc(ev, 1)
            v.wait_ge(pre, 5)
            v.tensor_copy(out=lutrep[:], in_=plut[:])
            for j, (src, dst, r0, pr, cols) in enumerate(mat_tiles):
                v.wait_ge(ldm0 if j % 2 == 0 else ldm1, 16 * (j // 2 + 1))
                if j >= 1:
                    v.wait_ge(ms, 16 * j)
                v.tensor_copy(out=idxb[0:pr, 0:cols], in_=idxi[0:pr, j % 2, 0:cols])
                v.tensor_scalar(out=wmat[0:pr, 0:cols], in0=idxb[0:pr, 0:cols],
                                scalar1=0.0, scalar2=lutrep[0:pr, 0:1],
                                op0=ALU.is_equal, op1=ALU.mult)
                for bb in range(1, 16):
                    v.tensor_scalar(out=term[0:pr, 0:cols], in0=idxb[0:pr, 0:cols],
                                    scalar1=float(bb), scalar2=lutrep[0:pr, bb:bb + 1],
                                    op0=ALU.is_equal, op1=ALU.mult)
                    inst = v.tensor_tensor(out=wmat[0:pr, 0:cols], in0=wmat[0:pr, 0:cols],
                                           in1=term[0:pr, 0:cols], op=ALU.add)
                inst.then_inc(mt, 1)

        @block.scalar
        def _(s):
            for w_ in range(2):
                s.dma_start(out=warm[0:1, 6 + w_:7 + w_],
                            in_=AP(lut.tensor, 0, [[1, 1], [1, 1]])).then_inc(wrm2, 16)
                s.wait_ge(wrm2, 16 * (w_ + 1))
            for i in range(8):
                s.wait_ge(va, i + 1)
                s.activation(out=sc[:, 5:6], in_=sc[:, 3:4], func=AF.Ln,
                             bias=0.0, scale=1.0)
                s.activation(out=rstd[:, i:i + 1], in_=sc[:, 5:6], func=AF.Exp,
                             bias=0.0, scale=-0.5).then_inc(sa, 1)
            for j, (_src, dst, r0, pr, cols) in enumerate(mat_tiles):
                s.wait_ge(mt, j + 1)
                s.dma_start(out=dst[r0:r0 + pr, :], in_=wmat[0:pr, 0:cols]).then_inc(ms, 16)
            s.wait_ge(ms, 16 * len(mat_tiles))

        @block.tensor
        def _(t):
            t.wait_ge(pre, 4)
            t.matmul(plut[:], ones1[:], lutb[:], start=True, stop=True).then_inc(pre, 1)
            for dc in range(16):
                for i in range(8):
                    j = dc * 8 + i
                    t.wait_ge(vb, i + 1)
                    if j >= 2:
                        t.wait_ge(ev, j - 1)
                    t.transpose((ptr0 if j % 2 == 0 else ptr1)[:, :],
                                xnb[:, i * HID + dc * 128: i * HID + dc * 128 + 128],
                                ident[:]).then_inc(tr, 1)
    al.close()
    return nc




def build_B():
    nc = _new_nc()
    xn = nc.dram_tensor("xn", [HID, SEQ], BF16, kind="ExternalInput").ap()
    wq = nc.dram_tensor("wq", [HID, 1024], F32, kind="ExternalInput").ap()
    wk = nc.dram_tensor("wk", [HID, 1024], F32, kind="ExternalInput").ap()
    wv = nc.dram_tensor("wv", [HID, 1024], F32, kind="ExternalInput").ap()
    rb = nc.dram_tensor("rb", [NBUCK, HPC], F32, kind="ExternalInput").ap()
    mt_ = nc.dram_tensor("mt", [NBUCK, 4095], F32, kind="ExternalInput").ap()
    o_cm = nc.dram_tensor("o_cm", [1024, SEQ], BF16, kind="ExternalOutput").ap()

    al = Alloc(nc)
    xnsb = al.sb("xnsb", [128, 16 * SEQ], BF16)
    wvsb = al.sb("wvsb", [128, 16 * 1024], BF16)
    mb = al.sb("mb", [2 * NBUCK, 4095], BF16)
    rbs = al.sb("rbs", [2 * NBUCK, HPC], BF16)
    rbsb = al.sb("rbsb", [NBUCK, HPC], F32)
    rbh = al.sb("rbh", [NBUCK, 2 * HPC], BF16)
    rbt = al.sb("rbt", [NBUCK, HPC], F32)
    rbl = al.sb("rbl", [NBUCK, HPC], BF16)
    tbc = al.sb("tbc", [HPC, 512], F32)
    tb = al.sb("tb", [HPC, 4095], BF16)
    bmax = al.sb("bmax", [HPC, 1], F32)
    bmaxb = al.sb("bmaxb", [HPC, 1], BF16)
    bmaxr = al.sb("bmaxr", [1, HPC], F32)
    tall = al.sb("tall", [128, 2, 4095], BF16)
    ident = al.sb("ident", [128, 128], BF16)
    ioti = al.sb("ioti", [128, 128], I32)
    ones1 = al.sb("ones1", [1, 128], BF16)
    wqt = al.sb("wqt", [128, 16 * 128], BF16)
    wkt = al.sb("wkt", [128, 16 * 128], BF16)
    qh = al.sb("qh", [128, 2, SEQ], BF16)
    kh = al.sb("kh", [128, 2, SEQ], BF16)
    vhat = al.sb("vhat", [128, HPC * 16 * 65], BF16)
    asb = al.sb("asb", [128, 2, 512], BF16)
    runmax = al.sb("runmax", [128, 1], F32)
    mcol = al.sb("mcol", [128, 1], F32)
    rmb = al.sb("rmb", [128, 1], BF16)
    msc = al.sb("msc", [1, 4], F32)
    mscb = al.sb("mscb", [1, 1], BF16)
    negm = al.sb("negm", [128, 1], F32)
    rrow = al.sb("rrow", [1, 512], F32)
    rrowb = al.sb("rrowb", [1, 512], BF16)
    rrsb = al.sb("rrsb", [64, 512], F32)
    osb = al.sb("osb", [64, 512], BF16)

    pp0 = al.ps("pp0", [128, 512], F32)
    pp1 = al.ps("pp1", [128, 512], F32)
    pp2 = al.ps("pp2", [128, 512], F32)
    pp3 = al.ps("pp3", [128, 512], F32)
    pav = al.ps("pav", [128, 512], F32)
    pms = al.ps("pms", [128, 512], F32)
    pmsb = al.ps("pmsb", [128, 128], BF16)

    ldxn = al.sem("ldxn"); ldw0 = al.sem("ldw0"); ldw1 = al.sem("ldw1")
    ldv = al.sem("ldv"); ldc0 = al.sem("ldc0"); ldc1 = al.sem("ldc1")
    tgd = al.sem("tgd"); tve = al.sem("tve"); ttb = al.sem("ttb"); tha = al.sem("tha")
    prj = al.sem("prj"); vprj = al.sem("vprj"); vvp = al.sem("vvp")
    gset = al.sem("gset")
    vset = al.sem("vset")
    sA = al.sem("sA"); vA = al.sem("vA"); sB = al.sem("sB"); eB = al.sem("eB")
    sAV = al.sem("sAV"); vO = al.sem("vO")
    mmx = al.sem("mmx"); vmx = al.sem("vmx"); vneg = al.sem("vneg")
    vrr = al.sem("vrr"); mrr = al.sem("mrr"); sto = al.sem("sto")

    block = al.block()
    NHP = HPC // 2

    if True:
        @block.sync
        def _(sync):
            sync.dma_start(out=rbsb[:], in_=rb[:, :]).then_inc(ldc0, 16)
            for i in range(16):
                sync.dma_start(out=xnsb[:, i * SEQ:(i + 1) * SEQ],
                               in_=xn[i * 128:(i + 1) * 128, :]).then_inc(ldxn, 16)
            n = 0
            for h in range(HPC):
                for qb in range(4):
                    n += 1
                    sync.wait_ge(vO, n)
                    sync.dma_start(out=o_cm[h * 64:(h + 1) * 64, qb * 512:(qb + 1) * 512],
                                   in_=osb[0:64, :]).then_inc(sto, 16)
                    sync.wait_ge(sto, 16 * n)

        @block.gpsimd
        def _(g):
            g.iota(ioti[:], pattern=[[1, 128]], base=0, channel_multiplier=-1).then_inc(gset, 1)
            g.memset(ones1[:], 1.0).then_inc(gset, 1)
            g.memset(qh[:], 0.0).then_inc(gset, 1)
            g.memset(kh[:], 0.0).then_inc(gset, 1)
            g.memset(vhat[:], 1.0).then_inc(gset, 1)
            g.memset(tall[:], 0.0).then_inc(gset, 1)
            g.dma_start(out=mb[0:NBUCK, :], in_=mt_[:, :]).then_inc(ldc1, 16)
            for dc in range(16):
                src_v = AP(wv.tensor, dc * 128 * 1024, [[1024, 128], [1, 1024]])
                g.dma_start(out=wvsb[:, dc * 1024:(dc + 1) * 1024],
                            in_=src_v).then_inc(ldv, 16)
            def tall_chain(h):
                buf = h % 2
                if h >= 2:
                    g.wait_ge(sB, 64 * (h - 1))
                base = 16 * 8 * h
                g.dma_start(out=tall[0:1, buf, 0:4095],
                            in_=tb[h:h + 1, 0:4095]).then_inc(tha, 16)
                for a in range(7):
                    w = 1 << a
                    g.wait_ge(tha, base + 16 * (a + 1))
                    g.dma_start(out=tall[w:2 * w, buf, w:4095],
                                in_=tall[0:w, buf, 0:4095 - w]).then_inc(tha, 16)

            g.wait_ge(ttb, 8)
            for hp in range(NHP):
                if hp >= 1:
                    g.wait_ge(prj, 32 + 8 * hp)
                src_q = AP(wq.tensor, hp * 128, [[1024, 128], [1024 * 128, 16], [1, 128]])
                src_k = AP(wk.tensor, hp * 128, [[1024, 128], [1024 * 128, 16], [1, 128]])
                g.dma_start(out=wqt[:].rearrange("p (c f) -> p c f", c=16),
                            in_=src_q).then_inc(ldw0, 16)
                g.dma_start(out=wkt[:].rearrange("p (c f) -> p c f", c=16),
                            in_=src_k).then_inc(ldw1, 16)
                tall_chain(2 * hp)
                tall_chain(2 * hp + 1)

        @block.scalar
        def _(s):
            n = 0
            for h in range(HPC):
                for qb in range(4):
                    s.wait_ge(vneg, n + 1)
                    for kt in range(16):
                        s.wait_ge(sB, 16 * n + kt + 1)
                        if kt >= 2:
                            s.wait_ge(sAV, 16 * n + kt - 1)
                        pb = pp2 if kt % 2 == 0 else pp3
                        s.activation(out=asb[:, kt % 2, :], in_=pb[:, :], func=AF.Exp,
                                     bias=negm[:, 0:1], scale=1.0).then_inc(eB, 1)
                    n += 1

        @block.vector
        def _(v):
            v.wait_ge(gset, 1)
            v.tensor_single_scalar(out=ident[:], in_=ioti[:], scalar=0,
                                   op=ALU.is_equal).then_inc(vset, 1)
            v.wait_ge(ldc0, 16)
            v.tensor_copy(out=rbh[:, 0:HPC], in_=rbsb[:]).then_inc(vset, 2)
            for ch in range(8):
                w = 512 if ch < 7 else 4095 - 512 * 7
                pt = pp0 if ch % 2 == 0 else pp1
                v.wait_ge(tgd, ch + 1)
                v.tensor_copy(out=tbc[:, 0:w], in_=pt[0:HPC, 0:w]).then_inc(tve, 1)
                if ch == 0:
                    v.reduce_max(out=bmax[:, 0:1], in_=tbc[:, 0:w], axis=AX)
                else:
                    v.reduce_max(out=msc[0:1, 0:1], in_=tbc[0:1, 0:1], axis=AX)
                    v.reduce_max(out=mcol[0:HPC, 0:1], in_=tbc[:, 0:w], axis=AX)
                    v.tensor_tensor(out=bmax[:, 0:1], in0=bmax[:, 0:1],
                                    in1=mcol[0:HPC, 0:1], op=ALU.max)
                v.tensor_copy(out=tb[:, ch * 512: ch * 512 + w],
                              in_=tbc[:, 0:w]).then_inc(ttb, 1)
            v.tensor_copy(out=bmaxb[:, 0:1], in_=bmax[:, 0:1]).then_inc(vset, 1)
            v.wait_ge(tgd, 9)
            v.tensor_copy(out=bmaxr[0:1, 0:HPC], in_=pmsb[0:1, 0:HPC]).then_inc(vset, 1)
            v.wait_ge(gset, 5)
            r = 0
            for cb in range(2):
                for tc in range(16):
                    pt = pp2 if r % 2 == 0 else pp3
                    v.wait_ge(prj, r + 1)
                    dst = AP(vhat, (0 * 16 + tc) * 65 + cb * 8 * 16 * 65,
                             [[HPC * 16 * 65, 128], [16 * 65, 8], [1, 64]])
                    v.tensor_copy(out=dst,
                                  in_=pt[:, :].rearrange("p (h f) -> p h f", h=8)).then_inc(vvp, 1)
                    r += 1
            v.wait_ge(gset, 4)
            n = 0
            for hp in range(NHP):
                for r in range(8):
                    qk = r // 4
                    tbk = r % 4
                    pt = pp0 if r % 2 == 0 else pp1
                    v.wait_ge(prj, 32 + 8 * hp + r + 1)
                    dsts = qh if qk == 0 else kh
                    v.tensor_copy(out=dsts[0:64, 0, tbk * 512:(tbk + 1) * 512],
                                  in_=pt[0:64, :]).then_inc(vprj, 1)
                    v.tensor_copy(out=dsts[64:128, 1, tbk * 512:(tbk + 1) * 512],
                                  in_=pt[64:128, :]).then_inc(vprj, 1)
                for hh in range(2):
                    h = 2 * hp + hh
                    for qb in range(4):
                        for kt in range(16):
                            pt = pp0 if kt % 2 == 0 else pp1
                            v.wait_ge(sA, 16 * n + kt + 1)
                            if kt == 0:
                                v.reduce_max(out=runmax[:, 0:1], in_=pt[:, :],
                                             axis=AX).then_inc(vA, 1)
                            else:
                                v.reduce_max(out=mcol[:, 0:1], in_=pt[:, :],
                                             axis=AX).then_inc(vA, 1)
                                v.tensor_tensor(out=runmax[:, 0:1], in0=runmax[:, 0:1],
                                                in1=mcol[:, 0:1], op=ALU.max)
                        v.tensor_copy(out=rmb[:, 0:1], in_=runmax[:, 0:1]).then_inc(vmx, 1)
                        v.wait_ge(mmx, 2 * n + 1)
                        v.reduce_max(out=msc[0:1, 0:1], in_=pmsb[0:1, 0:128], axis=AX)
                        v.tensor_tensor(out=msc[0:1, 1:2], in0=msc[0:1, 0:1],
                                        in1=bmaxr[0:1, h:h + 1], op=ALU.add)
                        v.tensor_scalar(out=msc[0:1, 2:3], in0=msc[0:1, 1:2],
                                        scalar1=-1.0, scalar2=None, op0=ALU.mult)
                        v.tensor_copy(out=mscb[0:1, 0:1], in_=msc[0:1, 2:3]).then_inc(vmx, 1)
                        v.wait_ge(mmx, 2 * n + 2)
                        v.tensor_copy(out=negm[:, 0:1], in_=pms[:, 0:1]).then_inc(vneg, 1)
                        v.wait_ge(sAV, 16 * (n + 1))
                        v.reciprocal(out=rrow[0:1, :], in_=pav[64:65, :])
                        v.tensor_copy(out=rrowb[0:1, :], in_=rrow[0:1, :]).then_inc(vrr, 1)
                        v.wait_ge(mrr, n + 1)
                        v.tensor_copy(out=rrsb[:, :], in_=pms[0:64, 0:512])
                        if n >= 1:
                            v.wait_ge(sto, 16 * n)
                        v.tensor_tensor(out=osb[:, :], in0=pav[0:64, :],
                                        in1=rrsb[:, :], op=ALU.mult).then_inc(vO, 1)
                        n += 1

        @block.tensor
        def _(t):
            t.wait_ge(vset, 2)
            t.wait_ge(gset, 4)
            t.wait_ge(ldc1, 16)
            for ch in range(8):
                pt = pp0 if ch % 2 == 0 else pp1
                if ch >= 2:
                    t.wait_ge(tve, ch - 1)
                w = 512 if ch < 7 else 4095 - 512 * 7
                t.matmul(pt[0:HPC, 0:w], rbh[:, 0:HPC], mb[0:NBUCK, ch * 512: ch * 512 + w],
                         start=True, stop=True).then_inc(tgd, 1)
            t.wait_ge(vset, 3)
            t.wait_ge(tve, 8)
            t.transpose(pmsb[0:1, 0:HPC], bmaxb[:, 0:1], ident[0:HPC, 0:HPC]).then_inc(tgd, 1)
            t.wait_ge(ldv, 256)
            t.wait_ge(ldxn, 256)
            r = 0
            for cb in range(2):
                for tc in range(16):
                    pt = pp2 if r % 2 == 0 else pp3
                    if r >= 2:
                        t.wait_ge(vvp, r - 1)
                    inst = None
                    for dc in range(16):
                        inst = t.matmul(pt[:, :],
                                        xnsb[:, dc * SEQ + tc * 128: dc * SEQ + (tc + 1) * 128],
                                        wvsb[:, dc * 1024 + cb * 512: dc * 1024 + (cb + 1) * 512],
                                        start=(dc == 0), stop=(dc == 15))
                    inst.then_inc(prj, 1)
                    r += 1
            n = 0
            for hp in range(NHP):
                t.wait_ge(ldw0, 16 * (hp + 1))
                t.wait_ge(ldw1, 16 * (hp + 1))
                for r in range(8):
                    qk = r // 4
                    tbk = r % 4
                    pt = pp0 if r % 2 == 0 else pp1
                    wt = wqt if qk == 0 else wkt
                    if hp == 0:
                        if r < 2:
                            t.wait_ge(tve, 8)
                        else:
                            t.wait_ge(vprj, 2 * (8 * hp + r - 1))
                    else:
                        if r < 2:
                            t.wait_ge(vA, 128 * hp)
                        else:
                            t.wait_ge(vprj, 2 * (8 * hp + r - 1))
                    inst = None
                    for dc in range(16):
                        inst = t.matmul(pt[:, :], wt[:, dc * 128:(dc + 1) * 128],
                                        xnsb[:, dc * SEQ + tbk * 512: dc * SEQ + tbk * 512 + 512],
                                        start=(dc == 0), stop=(dc == 15))
                    inst.then_inc(prj, 1)
                for hh in range(2):
                    h = 2 * hp + hh
                    for qb in range(4):
                        for kt in range(16):
                            pt = pp0 if kt % 2 == 0 else pp1
                            if n == 0 and kt < 2:
                                t.wait_ge(vprj, 16)
                            elif kt < 2:
                                t.wait_ge(vA, 16 * n - 2 + kt + 1)
                            else:
                                t.wait_ge(vA, 16 * n + kt - 1)
                            if kt < 2 and (hh == 0 and qb == 0):
                                t.wait_ge(vprj, 16 * (hp + 1))
                            t.matmul(pt[:, :], kh[:, hh, kt * 128:(kt + 1) * 128],
                                     qh[:, hh, qb * 512:(qb + 1) * 512],
                                     start=True, stop=True).then_inc(sA, 1)
                        t.wait_ge(vmx, 2 * n + 1)
                        if n >= 1:
                            t.wait_ge(vneg, n)
                        t.transpose(pmsb[0:1, 0:128], rmb[:, 0:1], ident[:]).then_inc(mmx, 1)
                        t.wait_ge(vmx, 2 * n + 2)
                        if n >= 1:
                            t.wait_ge(vO, n)
                        t.matmul(pms[:, 0:1], ones1[:, :], mscb[0:1, 0:1],
                                 start=True, stop=True).then_inc(mmx, 1)
                        for kt in range(16):
                            pt = pp2 if kt % 2 == 0 else pp3
                            if n == 0 and kt < 2:
                                t.wait_ge(vvp, 32)
                            elif kt < 2:
                                t.wait_ge(eB, 16 * n - 2 + kt + 1)
                            else:
                                t.wait_ge(eB, 16 * n + kt - 1)
                            m0 = qb * 512 - kt * 128 + 2047
                            t.matmul(pt[:, :], kh[:, hh, kt * 128:(kt + 1) * 128],
                                     qh[:, hh, qb * 512:(qb + 1) * 512],
                                     start=True, stop=False)
                            t.wait_ge(tha, 128 * (h + 1))
                            t.matmul(pt[:, :], ident[:, :],
                                     tall[:, h % 2, m0:m0 + 512],
                                     start=False, stop=True).then_inc(sB, 1)
                            t.wait_ge(eB, 16 * n + kt + 1)
                            if kt == 0 and n >= 1:
                                t.wait_ge(vO, n)
                            t.matmul(pav[0:65, :], vhat[:, (h * 16 + kt) * 65:(h * 16 + kt) * 65 + 65],
                                     asb[:, kt % 2, :],
                                     start=(kt == 0), stop=(kt == 15)).then_inc(sAV, 1)
                        t.wait_ge(vrr, n + 1)
                        t.matmul(pms[0:64, 0:512], ones1[0:1, 0:64], rrowb[0:1, :],
                                 start=True, stop=True).then_inc(mrr, 1)
                        n += 1
    al.close()
    return nc



def build_C():
    nc = _new_nc()
    o_cm = nc.dram_tensor("o_cm", [HID, TOK], BF16, kind="ExternalInput").ap()
    wo = nc.dram_tensor("wo", [HID, HID], F32, kind="ExternalInput").ap()
    xn0 = nc.dram_tensor("xn0", [HID, TOK], BF16, kind="ExternalInput").ap()
    WgT = nc.dram_tensor("WgT", [HID, INTER0], BF16, kind="ExternalInput").ap()
    WuT = nc.dram_tensor("WuT", [HID, INTER0], BF16, kind="ExternalInput").ap()
    Wd = nc.dram_tensor("Wd", [INTER0, HID], BF16, kind="ExternalInput").ap()
    x0 = nc.dram_tensor("x0", [TOK, HID], F32, kind="ExternalInput").ap()
    rw = nc.dram_tensor("rw", [HID, 8], F32, kind="ExternalInput").ap()
    rbv = nc.dram_tensor("rbv", [1, 8], F32, kind="ExternalInput").ap()
    gmb = nc.dram_tensor("gmb", [TOK, 8], F32, kind="ExternalInput").ap()

    x1o = nc.dram_tensor("x1o", [HID, TOK], F32, kind="ExternalOutput").ap()
    xn1o = nc.dram_tensor("xn1o", [HID, TOK], F32, kind="ExternalOutput").ap()
    xn1bo = nc.dram_tensor("xn1bo", [HID, TOK], BF16, kind="ExternalOutput").ap()
    smo = nc.dram_tensor("smo", [TOK, 8], F32, kind="ExternalOutput").ap()
    dbgC = nc.dram_tensor("dbgC", [128, 32], F32, kind="ExternalOutput").ap()
    mrs_h = nc.dram_tensor("mrs_dram", [2, TOK], F32, kind="Internal")

    al = Alloc(nc)
    xnsb = al.sb("xnsb", [128, 16 * TOK], BF16)
    osb = al.sb("osb", [128, 16 * 512], BF16)
    x0sb = al.sb("x0sb", [128, 2, 4 * 128], F32)
    hsb = al.sb("hsb", [128, 43 * 512], BF16)
    thsb = al.sb("thsb", [128, 2, 512], F32)
    v1sb = al.sb("v1sb", [128, 2, 512], F32)
    wg2 = al.sb("wg2", [128, 2, 16, 128], BF16)
    wu2 = al.sb("wu2", [128, 2, 16, 128], BF16)
    wd2 = al.sb("wd2", [128, 2, 43, 128], BF16)
    wo2 = al.sb("wo2", [128, 2, 16, 128], BF16)
    x1s = al.sb("x1s", [128, 3 * 512], F32)
    x1r = al.sb("x1r", [128, 2, 512], F32)
    x1b = al.sb("x1b", [128, 2, 512], BF16)
    x1q = al.sb("x1q", [128, 2, 512], BF16)
    onec = al.sb("onec", [128, 1], BF16)
    xn1f = al.sb("xn1f", [128, 512], F32)
    xn1t = al.sb("xn1t", [128, 512], F32)
    xn1c = al.sb("xn1c", [128, 512], BF16)
    rows = al.sb("rows", [1, 3 * 512], F32)
    mrow = al.sb("mrow", [1, 2 * 512], F32)
    rsrow = al.sb("rsrow", [1, 2 * 512], F32)
    mrep = al.sb("mrep", [128, TOK], F32)
    rrep2 = al.sb("rrep2", [128, TOK], F32)
    rwf = al.sb("rwf", [128, 16 * 8], F32)
    rwhl = al.sb("rwhl", [128, 16 * 16], BF16)
    rwt = al.sb("rwt", [128, 16 * 8], F32)
    rbrep = al.sb("rbrep", [128, 8], F32)
    gmsb = al.sb("gmsb", [128, 8 * 8], F32)
    xn1rb = al.sb("xn1rb", [128, 2, 16, 128], BF16)
    lgt = al.sb("lgt", [128, 2 * 8], F32)
    smt = al.sb("smt", [128, 8], F32)
    smt2 = al.sb("smt2", [128, 8], F32)
    nmx = al.sb("nmx", [128, 1], F32)
    ssum = al.sb("ssum", [128, 1], F32)
    rsum = al.sb("rsum", [128, 1], F32)
    idf = al.sb("idf", [128, 128], F32)
    ioti = al.sb("ioti", [128, 128], I32)

    pg0 = al.ps("pg0", [128, 512], F32)
    pg1 = al.ps("pg1", [128, 512], F32)
    pu0 = al.ps("pu0", [128, 512], F32)
    pu1 = al.ps("pu1", [128, 512], F32)
    pdn = al.ps("pdn", [128, 512], F32)
    pat = al.ps("pat", [128, 512], F32)
    pxt = al.ps("pxt", [128, 512], F32)
    pst = al.ps("pst", [128, 512], F32)

    ld0 = al.sem("ld0"); ldx0 = al.sem("ldx0")
    ldw0 = al.sem("ldw0"); ldw1a = al.sem("ldw1a"); ldw1b = al.sem("ldw1b")
    ldr = al.sem("ldr"); ldx0a = al.sem("ldx0a"); ldx0b = al.sem("ldx0b")
    ld1a = al.sem("ld1a"); ld1b = al.sem("ld1b")
    ld2a = al.sem("ld2a"); ld2b = al.sem("ld2b")
    ld3a = al.sem("ld3a"); ld3b = al.sem("ld3b")
    gset = al.sem("gset"); vset = al.sem("vset")
    mmg = al.sem("mmg"); mmu = al.sem("mmu"); vh = al.sem("vh"); sh = al.sem("sh")
    mmc = al.sem("mmc"); vc = al.sem("vc"); stx = al.sem("stx")
    mm2 = al.sem("mm2"); v2 = al.sem("v2"); s2 = al.sem("s2")
    ldb = al.sem("ldb"); ld1 = al.sem("ld1"); ld2 = al.sem("ld2"); ld3 = al.sem("ld3")
    mmr = al.sem("mmr"); vr = al.sem("vr")
    vch = al.sem("vch"); sch = al.sem("sch")
    sxo = al.sem("sxo"); ssm = al.sem("ssm")

    block = al.block()
    NIC = 43
    if True:
        @block.sync
        def _(sync):
            for i in range(16):
                sync.dma_start(out=xnsb[:, i * TOK:(i + 1) * TOK],
                               in_=xn0[i * 128:(i + 1) * 128, :]).then_inc(ld0, 16)
            n = 0
            for tb in range(2):
                if tb == 1:
                    sync.wait_ge(mmc, 3 * 15 + 2)
                for i in range(16):
                    sync.dma_start(out=osb[:, i * 512:(i + 1) * 512],
                                   in_=o_cm[i * 128:(i + 1) * 128,
                                            tb * 512:(tb + 1) * 512]).then_inc(ld0, 16)
                for oc in range(16):
                    n += 1
                    sync.wait_ge(vc, n)
                    sync.dma_start(out=x1o[oc * 128:(oc + 1) * 128, tb * 512:(tb + 1) * 512],
                                   in_=x1s[:, 2 * 512:3 * 512]).then_inc(stx, 16)
                    sync.wait_ge(stx, 16 * n)
            n = 0
            for tb in range(2):
                for oc in range(16):
                    n += 1
                    if n > 2:
                        sync.wait_ge(mm2, 2 * (n - 2))
                    sync.dma_start(out=x1r[:, n % 2, :],
                                   in_=x1o[oc * 128:(oc + 1) * 128,
                                           tb * 512:(tb + 1) * 512]).then_inc(
                                       ld1a if n % 2 == 0 else ld1b, 16)
            sync.wait_ge(s2, 66)
            sync.dma_start(out=mrep[:, :],
                           in_=AP(mrs_h, 0, [[0, 128], [1, TOK]])).then_inc(ldb, 16)
            sync.dma_start(out=rrep2[:, :],
                           in_=AP(mrs_h, TOK, [[0, 128], [1, TOK]])).then_inc(ldb, 16)
            n = 0
            for tb in range(2):
                for oc in range(16):
                    n += 1
                    if n > 2:
                        sync.wait_ge(sxo, 32 * (n - 2))
                    sync.dma_start(out=x1r[:, n % 2, :],
                                   in_=x1o[oc * 128:(oc + 1) * 128,
                                           tb * 512:(tb + 1) * 512]).then_inc(
                                       ld2a if n % 2 == 0 else ld2b, 16)
            sync.wait_ge(sxo, 32 * 32)
            for tc in range(8):
                if tc >= 2:
                    sync.wait_ge(mmr, 16 * (tc - 1))
                sync.dma_start(out=xn1rb[:, tc % 2, :, :],
                               in_=AP(xn1bo.tensor, tc * 128,
                                      [[TOK, 128], [TOK * 128, 16], [1, 128]])).then_inc(
                                   ld3a if tc % 2 == 0 else ld3b, 16)
                if tc >= 1:
                    sync.wait_ge(vr, 4 * (tc - 1) + 4)
                    sync.dma_start(out=smo[(tc - 1) * 128:tc * 128, :],
                                   in_=smt2[:, :]).then_inc(ssm, 16)
            sync.wait_ge(vr, 4 * 7 + 4)
            sync.dma_start(out=smo[7 * 128:8 * 128, :], in_=smt2[:, :]).then_inc(ssm, 16)
            sync.wait_ge(sxo, 32 * 32)
            sync.wait_ge(ssm, 16 * 8)
            sync.dma_start(out=dbgC[:, 0:32], in_=mrep[:, 0:32]).then_inc(ssm, 16)
            sync.wait_ge(ssm, 16 * 9)

        @block.gpsimd
        def _(g):
            g.iota(ioti[:], pattern=[[1, 128]], base=0, channel_multiplier=-1).then_inc(gset, 1)
            g.memset(onec[:], 1.0).then_inc(gset, 1)
            g.dma_start(out=rwf[:].rearrange("p (c e) -> p c e", c=16),
                        in_=AP(rw.tensor, 0, [[8, 128], [8 * 128, 16], [1, 8]])).then_inc(ldr, 16)
            g.dma_start(out=rbrep[:, :],
                        in_=AP(rbv.tensor, 0, [[0, 128], [1, 8]])).then_inc(ldr, 16)
            g.dma_start(out=gmsb[:].rearrange("p (c e) -> p c e", c=8),
                        in_=AP(gmb.tensor, 0, [[8, 128], [8 * 128, 8], [1, 8]])).then_inc(ldr, 16)
            for tb in range(2):
                for ic in range(NIC):
                    r = tb * NIC + ic
                    if r >= 2:
                        g.wait_ge(vh, r - 1)
                    w = min(128, INTER0 - ic * 128)
                    g.dma_start(out=wg2[:, r % 2, :, 0:w],
                                in_=AP(WgT.tensor, ic * 128,
                                       [[INTER0, 128], [INTER0 * 128, 16], [1, w]])).then_inc(ldw0, 16)
                    g.dma_start(out=wu2[:, r % 2, :, 0:w],
                                in_=AP(WuT.tensor, ic * 128,
                                       [[INTER0, 128], [INTER0 * 128, 16], [1, w]])).then_inc(ldw0, 16)
                for oc in range(16):
                    r = tb * 16 + oc
                    if r >= 2:
                        g.wait_ge(mmc, 3 * (r - 1))
                    g.dma_start(out=x0sb[:, r % 2, :].rearrange("p (s j) -> p s j", s=4),
                                in_=AP(x0.tensor, tb * 4 * 128 * HID + oc * 128,
                                       [[HID, 128], [HID * 128, 4], [1, 128]])).then_inc(
                                           ldx0a if r % 2 == 0 else ldx0b, 16)
                    if r >= 2:
                        g.wait_ge(vc, r - 1)
                    lw = ldw1a if r % 2 == 0 else ldw1b
                    g.dma_start(out=wd2[:, r % 2, :, :],
                                in_=AP(Wd.tensor, oc * 128,
                                       [[HID, 128], [HID * 128, 43], [1, 128]])).then_inc(lw, 16)
                    g.dma_start(out=wo2[:, r % 2, :, :],
                                in_=AP(wo.tensor, oc * 128,
                                       [[HID, 128], [HID * 128, 16], [1, 128]])).then_inc(lw, 16)

        @block.scalar
        def _(s):
            SC = Chain(s, sch)
            for tb in range(2):
                for ic in range(NIC):
                    r = tb * NIC + ic
                    s.wait_ge(mmg, r + 1)
                    pg = pg0 if r % 2 == 0 else pg1
                    s.activation(out=thsb[:, r % 2, :], in_=pg[:, :], func=AF.Tanh,
                                 bias=0.0, scale=0.5).then_inc(sh, 1)
            for tb in range(2):
                s.wait_ge(v2, 34 * (tb + 1))
                SC(s.activation(out=rows[0:1, 512:1024],
                                in_=rows[0:1, 0:512], func=AF.Ln,
                                bias=0.0, scale=1.0))
                SC(s.activation(out=rsrow[0:1, tb * 512:(tb + 1) * 512],
                                in_=rows[0:1, 512:1024], func=AF.Exp,
                                bias=0.0, scale=-0.5))
                s.sem_inc(s2, 1)
                s.dma_start(out=AP(mrs_h, tb * 512, [[1, 1], [1, 512]]),
                            in_=mrow[0:1, tb * 512:(tb + 1) * 512]).then_inc(s2, 16)
                s.dma_start(out=AP(mrs_h, TOK + tb * 512, [[1, 1], [1, 512]]),
                            in_=rsrow[0:1, tb * 512:(tb + 1) * 512]).then_inc(s2, 16)
            n = 0
            for tb in range(2):
                for oc in range(16):
                    n += 1
                    s.wait_ge(v2, 68 + 2 * n)
                    s.dma_start(out=xn1o[oc * 128:(oc + 1) * 128, tb * 512:(tb + 1) * 512],
                                in_=xn1f[:, :]).then_inc(sxo, 16)
                    s.dma_start(out=xn1bo[oc * 128:(oc + 1) * 128, tb * 512:(tb + 1) * 512],
                                in_=xn1c[:, :]).then_inc(sxo, 16)
                    s.wait_ge(sxo, 32 * n)
            for tc in range(8):
                s.wait_ge(vr, 4 * tc + 3)
                s.activation(out=smt[:, :], in_=lgt[:, 8:16], func=AF.Exp,
                             bias=nmx[:, 0:1], scale=1.0).then_inc(s2, 1)

        @block.vector
        def _(v):
            VC = Chain(v, vch)
            v.wait_ge(gset, 1)
            v.tensor_single_scalar(out=idf[:], in_=ioti[:], scalar=0,
                                   op=ALU.is_equal).then_inc(vset, 1)
            v.wait_ge(ldr, 48)
            v.tensor_copy(out=rwhl[:].rearrange("p (c e) -> p c e", c=16)[:, :, 0:8],
                          in_=rwf[:].rearrange("p (c e) -> p c e", c=16)).then_inc(vset, 1)
            v.tensor_tensor(out=rwt[:], in0=rwf[:],
                            in1=rwhl[:].rearrange("p (c e) -> p c e", c=16)[:, :, 0:8].opt(),
                            op=ALU.subtract)
            v.tensor_copy(out=rwhl[:].rearrange("p (c e) -> p c e", c=16)[:, :, 8:16],
                          in_=rwt[:].rearrange("p (c e) -> p c e", c=16)).then_inc(vset, 1)
            for tb in range(2):
                for ic in range(NIC):
                    r = tb * NIC + ic
                    w = min(128, INTER0 - ic * 128)
                    pg = pg0 if r % 2 == 0 else pg1
                    pu = pu0 if r % 2 == 0 else pu1
                    v.wait_ge(sh, r + 1)
                    v.wait_ge(mmu, r + 1)
                    v.scalar_tensor_tensor(out=v1sb[:, r % 2, :], in0=thsb[:, r % 2, :],
                                           scalar=1.0, in1=pg[:, :],
                                           op0=ALU.add, op1=ALU.mult)
                    v.scalar_tensor_tensor(out=hsb[:, ic * 512 + tb * 0:(ic) * 512 + 512] if False else hsb[:, ic * 512: ic * 512 + 512],
                                           in0=v1sb[:, r % 2, :], scalar=0.5, in1=pu[:, :],
                                           op0=ALU.mult, op1=ALU.mult).then_inc(vh, 1)
                for oc in range(16):
                    n = tb * 16 + oc + 1
                    v.wait_ge(mmc, 3 * n)
                    v.tensor_copy(out=x1s[:, 0:512], in_=pat[:, :])
                    v.scalar_tensor_tensor(out=x1s[:, 512:1024], in0=x1s[:, 0:512],
                                           scalar=1.0, in1=pdn[:, :], op0=ALU.mult, op1=ALU.add)
                    if n >= 2:
                        v.wait_ge(stx, 16 * (n - 1))
                    v.scalar_tensor_tensor(out=x1s[:, 1024:1536], in0=x1s[:, 512:1024],
                                           scalar=1.0, in1=pxt[:, :], op0=ALU.mult,
                                           op1=ALU.add).then_inc(vc, 1)
            n = 0
            for tb in range(2):
                for oc in range(16):
                    n += 1
                    v.wait_ge(ld1a if n % 2 == 0 else ld1b,
                              16 * ((n // 2) if n % 2 == 0 else (n + 1) // 2))
                    if n >= 3:
                        v.wait_ge(mm2, 2 * (n - 2))
                    v.tensor_copy(out=x1b[:, n % 2, :], in_=x1r[:, n % 2, :]).then_inc(v2, 1)
                    v.tensor_tensor(out=x1q[:, n % 2, :], in0=x1r[:, n % 2, :],
                                    in1=x1r[:, n % 2, :], op=ALU.mult).then_inc(v2, 1)
                v.wait_ge(mm2, 2 * 16 * (tb + 1))
                VC(v.tensor_scalar(out=mrow[0:1, tb * 512:(tb + 1) * 512], in0=pst[0:1, :],
                                   scalar1=1.0 / HID, scalar2=None, op0=ALU.mult))
                v.sem_inc(v2, 1)
                VC(v.tensor_scalar(out=rows[0:1, 0:512], in0=pat[0:1, :],
                                   scalar1=1.0 / HID, scalar2=None, op0=ALU.mult))
                VC(v.tensor_tensor(out=rows[0:1, 512:1024],
                                   in0=mrow[0:1, tb * 512:(tb + 1) * 512],
                                   in1=mrow[0:1, tb * 512:(tb + 1) * 512], op=ALU.mult))
                VC(v.tensor_tensor(out=rows[0:1, 1024:1536], in0=rows[0:1, 0:512],
                                   in1=rows[0:1, 512:1024], op=ALU.subtract))
                VC(v.tensor_scalar(out=rows[0:1, 0:512], in0=rows[0:1, 1024:1536],
                                   scalar1=LN_EPS, scalar2=None, op0=ALU.add))
                v.sem_inc(v2, 1)
            v.wait_ge(ldb, 32)
            n = 0
            for tb in range(2):
                for oc in range(16):
                    n += 1
                    v.wait_ge(ld2a if n % 2 == 0 else ld2b,
                              16 * ((n // 2) if n % 2 == 0 else (n + 1) // 2))
                    if n >= 2:
                        v.wait_ge(sxo, 32 * (n - 1))
                    v.tensor_tensor(out=xn1t[:, :], in0=x1r[:, n % 2, :],
                                    in1=mrep[:, tb * 512:(tb + 1) * 512], op=ALU.subtract)
                    v.tensor_tensor(out=xn1f[:, :], in0=xn1t[:, :],
                                    in1=rrep2[:, tb * 512:(tb + 1) * 512],
                                    op=ALU.mult).then_inc(v2, 1)
                    v.tensor_copy(out=xn1c[:, :], in_=xn1f[:, :]).then_inc(v2, 1)
            for tc in range(8):
                v.wait_ge(mmr, 16 * (tc + 1))
                VC(v.tensor_copy(out=lgt[:, 0:8], in_=pg0[:, 0:8]))
                VC(v.scalar_tensor_tensor(out=lgt[:, 8:16], in0=lgt[:, 0:8], scalar=1.0,
                                          in1=pg0[:, 8:16], op0=ALU.mult, op1=ALU.add))
                VC(v.tensor_tensor(out=lgt[:, 0:8], in0=lgt[:, 8:16],
                                   in1=gmsb[:, tc * 8:(tc + 1) * 8], op=ALU.add))
                VC(v.tensor_tensor(out=lgt[:, 8:16], in0=lgt[:, 0:8],
                                   in1=rbrep[:, :], op=ALU.add))
                v.sem_inc(vr, 2)
                VC(v.reduce_max(out=rsum[:, 0:1], in_=lgt[:, 8:16], axis=AX))
                VC(v.tensor_scalar(out=nmx[:, 0:1], in0=rsum[:, 0:1], scalar1=-1.0,
                                   scalar2=None, op0=ALU.mult))
                v.sem_inc(vr, 1)
                v.wait_ge(s2, 66 + tc + 1)
                if tc >= 1:
                    v.wait_ge(ssm, 16 * tc)
                VC(v.reduce_sum(out=ssum[:, 0:1], in_=smt[:, :], axis=AX))
                if tc == 0:
                    v.tensor_copy(out=mrep[:, 0:8], in_=lgt[:, 8:16])
                    v.tensor_copy(out=mrep[:, 8:9], in_=nmx[:, 0:1])
                    v.tensor_copy(out=mrep[:, 9:17], in_=smt[:, :])
                    v.tensor_copy(out=mrep[:, 17:18], in_=ssum[:, 0:1])
                VC(v.reciprocal(out=rsum[:, 0:1], in_=ssum[:, 0:1]))
                VC(v.tensor_scalar(out=smt2[:, :], in0=smt[:, :], scalar1=rsum[:, 0:1],
                                   scalar2=None, op0=ALU.mult))
                v.sem_inc(vr, 1)

        @block.tensor
        def _(t):
            t.wait_ge(ld0, 32 * 16)
            for tb in range(2):
                for ic in range(NIC):
                    r = tb * NIC + ic
                    w = min(128, INTER0 - ic * 128)
                    t.wait_ge(ldw0, 32 * (r + 1))
                    if r >= 2:
                        t.wait_ge(vh, r - 1)
                    pg = pg0 if r % 2 == 0 else pg1
                    pu = pu0 if r % 2 == 0 else pu1
                    inst = None
                    for dc in range(16):
                        inst = t.matmul(pg[0:w, :], wg2[:, r % 2, dc, 0:w],
                                        xnsb[:, dc * TOK + tb * 512: dc * TOK + tb * 512 + 512],
                                        start=(dc == 0), stop=(dc == 15))
                    inst.then_inc(mmg, 1)
                    inst = None
                    for dc in range(16):
                        inst = t.matmul(pu[0:w, :], wu2[:, r % 2, dc, 0:w],
                                        xnsb[:, dc * TOK + tb * 512: dc * TOK + tb * 512 + 512],
                                        start=(dc == 0), stop=(dc == 15))
                    inst.then_inc(mmu, 1)
                for oc in range(16):
                    rr = tb * 16 + oc
                    n = rr + 1
                    t.wait_ge(ldw1a if rr % 2 == 0 else ldw1b, 32 * (rr // 2 + 1))
                    t.wait_ge(vh, (tb + 1) * NIC)
                    if n >= 2:
                        t.wait_ge(vc, n - 1)
                    inst = None
                    for ic in range(NIC):
                        w = 128
                        inst = t.matmul(pdn[:, :], wd2[0:w, rr % 2, ic, :],
                                        hsb[0:w, ic * 512: ic * 512 + 512],
                                        start=(ic == 0), stop=(ic == NIC - 1))
                    inst.then_inc(mmc, 1)
                    if tb == 1:
                        t.wait_ge(ld0, 16 * 16 + 32 * 16)
                    inst = None
                    for rc in range(16):
                        inst = t.matmul(pat[:, :], wo2[:, rr % 2, rc, :],
                                        osb[:, rc * 512:(rc + 1) * 512],
                                        start=(rc == 0), stop=(rc == 15))
                    inst.then_inc(mmc, 1)
                    t.wait_ge(ldx0a if rr % 2 == 0 else ldx0b, 16 * (rr // 2 + 1))
                    t.wait_ge(vset, 1)
                    inst = None
                    for sub in range(4):
                        inst = t.transpose(pxt[:, sub * 128:(sub + 1) * 128],
                                           x0sb[:, rr % 2, sub * 128:(sub + 1) * 128],
                                           idf[:])
                    inst.then_inc(mmc, 1)
            t.wait_ge(gset, 2)
            n = 0
            for tb in range(2):
                for oc in range(16):
                    n += 1
                    t.wait_ge(v2, 2 * n + 2 * tb)
                    t.matmul(pst[0:1, :], onec[:, 0:1], x1b[:, n % 2, :],
                             start=(oc == 0), stop=(oc == 15)).then_inc(mm2, 1)
                    t.matmul(pat[0:1, :], onec[:, 0:1], x1q[:, n % 2, :],
                             start=(oc == 0), stop=(oc == 15)).then_inc(mm2, 1)
            n = 0
            for tc in range(8):
                if tc >= 1:
                    t.wait_ge(vr, 4 * tc)
                t.wait_ge(vset, 3)
                t.wait_ge(ld3a if tc % 2 == 0 else ld3b, 16 * (tc // 2 + 1))
                for dc in range(16):
                    t.matmul(pg0[:, 0:16], xn1rb[:, tc % 2, dc, :],
                             rwhl[:, dc * 16:(dc + 1) * 16],
                             start=(dc == 0), stop=(dc == 15)).then_inc(mmr, 1)
            pass
    al.close()
    return nc



def build_D():
    nc = _new_nc()
    xn1b_i = nc.dram_tensor("xn1b", [HID, TOK], BF16, kind="ExternalInput").ap()
    pcol_i = nc.dram_tensor("pcol", [128, 16], F32, kind="ExternalInput").ap()
    maas_i = nc.dram_tensor("maas", [128, 16 * 7], F32, kind="ExternalInput").ap()
    w1_i = nc.dram_tensor("w1", [HID, 160], F32, kind="ExternalInput").ap()
    w2_i = nc.dram_tensor("w2", [5, 32, HID], F32, kind="ExternalInput").ap()
    dw1_i = nc.dram_tensor("dw1", [HID, 64], F32, kind="ExternalInput").ap()
    dw2_i = nc.dram_tensor("dw2", [64, HID], F32, kind="ExternalInput").ap()
    wrT = nc.dram_tensor("wrT", [HID, HID], F32, kind="ExternalInput").ap()
    wkT = nc.dram_tensor("wkT", [HID, HID], F32, kind="ExternalInput").ap()
    wvT = nc.dram_tensor("wvT", [HID, HID], F32, kind="ExternalInput").ap()
    wgT = nc.dram_tensor("wgT", [HID, HID], F32, kind="ExternalInput").ap()

    r_o = nc.dram_tensor("r_o", [HID, TOK], BF16, kind="ExternalOutput").ap()
    k_o = nc.dram_tensor("k_o", [HID, TOK], BF16, kind="ExternalOutput").ap()
    w_o = nc.dram_tensor("w_o", [HID, TOK], F32, kind="ExternalOutput").ap()
    v_o = nc.dram_tensor("v_o", [TOK, HID], BF16, kind="ExternalOutput").ap()
    g_o = nc.dram_tensor("g_o", [HID, TOK], BF16, kind="ExternalOutput").ap()

    al = Alloc(nc)
    xnb = al.sb("xnb", [128, 16 * TOK], BF16)
    pcol = al.sb("pcol_s", [128, 16], F32)
    maas = al.sb("maas_s", [128, 16 * 7], F32)
    xxb = al.sb("xxb", [128, 16, 512], BF16)
    xxxb = al.sb("xxxb", [128, 16, 512], BF16)
    xkb = al.sb("xkb", [128, 5, 16, 512], BF16)
    w1b = al.sb("w1b_s", [128, 16, 160], BF16)
    w2b = al.sb("w2b_s", [32, 2, 16, 128], BF16)
    dw1b = al.sb("dw1b_s", [128, 16, 64], BF16)
    dw2b = al.sb("dw2b_s", [64, 16, 128], BF16)
    thf = al.sb("thf", [32, 2, 512], F32)
    thfb = al.sb("thfb", [32, 2, 512], BF16)
    thg = al.sb("thg", [128, 2, 512], F32)
    thd = al.sb("thd", [64, 512], F32)
    thdb = al.sb("thdb", [64, 512], BF16)
    mtmp = al.sb("mtmp", [128, 2, 512], F32)
    gtmp = al.sb("gtmp", [128, 2, 512], F32)
    evb = al.sb("evb", [128, 2, 512], BF16)
    evf = al.sb("evf", [128, 2, 512], F32)
    wstr = al.sb("wstr", [128, 2, 16, 128], BF16)
    identb = al.sb("identb", [128, 128], BF16)
    ioti = al.sb("ioti", [128, 128], I32)
    vtb = al.sb("vtb", [128, 2, 4, 128], BF16)

    pmM0 = al.ps("pmM0", [128, 512], F32)
    pmM1 = al.ps("pmM1", [128, 512], F32)
    pmF0 = al.ps("pmF0", [128, 512], F32)
    pmF1 = al.ps("pmF1", [128, 512], F32)
    pp0 = al.ps("pp0", [128, 512], F32)
    pp1 = al.ps("pp1", [128, 512], F32)
    ptr = al.ps("ptr", [128, 512], BF16)
    ptr1 = al.ps("ptr1", [128, 512], BF16)

    ldx = al.sem("ldx"); ldc = al.sem("ldc"); ldg0 = al.sem("ldg0")
    ldga = al.sem("ldga"); ldgb = al.sem("ldgb")
    ldwa2 = al.sem("ldwa2"); ldwb2 = al.sem("ldwb2")
    gset = al.sem("gset"); vset = al.sem("vset"); vch = al.sem("vch"); sch2 = al.sem("sch2")
    vxx = al.sem("vxx")
    mmm = al.sem("mmm"); sth = al.sem("sth"); vth = al.sem("vth")
    mmf = al.sem("mmf"); vxk = al.sem("vxk")
    mmd = al.sem("mmd"); sthd = al.sem("sthd"); vthd = al.sem("vthd")
    mmw = al.sem("mmw"); vw = al.sem("vw")
    mmp = al.sem("mmp"); sthg = al.sem("sthg"); vev = al.sem("vev")
    mmt = al.sem("mmt"); vtv = al.sem("vtv")
    sto = al.sem("sto")

    block = al.block()
    projs = [("r", r_o, wrT, 3, "cm"), ("k", k_o, wkT, 1, "cm"),
             ("g", g_o, wgT, 4, "cm_silu"), ("v", v_o, wvT, 2, "tok")]
    NST = 2 * (3 * 16 + 16 * 4 + 16)

    if True:
        @block.sync
        def _(sync):
            for i in range(16):
                sync.dma_start(out=xnb[:, i * TOK:(i + 1) * TOK],
                               in_=xn1b_i[i * 128:(i + 1) * 128, :]).then_inc(ldx, 16)
            sync.dma_start(out=pcol[:], in_=pcol_i[:, :]).then_inc(ldc, 16)
            sync.dma_start(out=maas[:], in_=maas_i[:, :]).then_inc(ldc, 16)
            sync.wait_ge(sto, 16 * NST)

        @block.gpsimd
        def _(g):
            g.iota(ioti[:], pattern=[[1, 128]], base=0, channel_multiplier=-1).then_inc(gset, 1)
            g.dma_start(out=w1b[:],
                        in_=AP(w1_i.tensor, 0, [[160, 128], [160 * 128, 16], [1, 160]])).then_inc(ldg0, 16)
            g.dma_start(out=dw1b[:],
                        in_=AP(dw1_i.tensor, 0, [[64, 128], [64 * 128, 16], [1, 64]])).then_inc(ldg0, 16)
            g.dma_start(out=dw2b[:],
                        in_=AP(dw2_i.tensor, 0, [[HID, 64], [128, 16], [1, 128]])).then_inc(ldg0, 16)
            rr = 0
            for tb in range(2):
                for f in range(5):
                    r2 = tb * 5 + f
                    if r2 >= 2:
                        g.wait_ge(mmf, 16 * (r2 - 1))
                    g.dma_start(out=w2b[:, r2 % 2, :, :],
                                in_=AP(w2_i.tensor, f * 32 * HID,
                                       [[HID, 32], [128, 16], [1, 128]])).then_inc(
                                           ldga if r2 % 2 == 0 else ldgb, 16)
                for pi, (nm, oap, wap, fi, kind) in enumerate(projs):
                    for oc in range(16):
                        if rr >= 2:
                            g.wait_ge(mmp, rr - 1)
                        g.dma_start(out=wstr[:, rr % 2, :, :],
                                    in_=AP(wap.tensor, oc * 128,
                                           [[HID, 128], [HID * 128, 16], [1, 128]])).then_inc(
                                               ldwa2 if rr % 2 == 0 else ldwb2, 16)
                        rr += 1

        @block.scalar
        def _(s):
            SC = Chain(s, sch2)
            nstore = 0
            for tb in range(2):
                for f in range(5):
                    s.wait_ge(mmm, 5 * tb + f + 1)
                    if 5 * tb + f >= 2:
                        s.wait_ge(vth, 5 * tb + f - 1)
                    s.activation(out=thf[:, f % 2, :], in_=pmM0[0:32, :] if f % 2 == 0 else pmM1[0:32, :],
                                 func=AF.Tanh, bias=0.0, scale=1.0).then_inc(sth, 1)
                s.wait_ge(mmd, tb + 1)
                s.activation(out=thd[:, :], in_=pmM0[0:64, :], func=AF.Tanh,
                             bias=0.0, scale=1.0).then_inc(sthd, 1)
                for dc in range(16):
                    s.wait_ge(vw, 16 * tb + dc + 1)
                    s.dma_start(out=w_o[dc * 128:(dc + 1) * 128, tb * 512:(tb + 1) * 512],
                                in_=evf[:, dc % 2, :]).then_inc(sto, 16)
                    nstore += 1
                    if dc >= 1:
                        pass
                for pi, (nm, oap, wap, fi, kind) in enumerate(projs):
                    for oc in range(16):
                        rr = (tb * 4 + pi) * 16 + oc
                        if kind == "cm_silu":
                            s.wait_ge(mmp, rr + 1)
                            s.activation(out=thg[:, rr % 2, :],
                                         in_=pp0[:, :] if rr % 2 == 0 else pp1[:, :],
                                         func=AF.Tanh, bias=0.0, scale=0.5).then_inc(sthg, 1)
                        if kind == "tok":
                            for sub in range(4):
                                s.wait_ge(vtv, 4 * (16 * tb + oc) + sub + 1)
                                s.dma_start(out=v_o[tb * 512 + sub * 128: tb * 512 + (sub + 1) * 128,
                                                    oc * 128:(oc + 1) * 128],
                                            in_=vtb[:, tb % 2, sub, :]).then_inc(sto, 16)
                                nstore += 1
                        else:
                            s.wait_ge(vev, rr - 16 * tb * 0 + 1)
                            s.dma_start(out=oap[oc * 128:(oc + 1) * 128, tb * 512:(tb + 1) * 512],
                                        in_=evb[:, rr % 2, :]).then_inc(sto, 16)
                            nstore += 1
                        s.wait_ge(sto, 16 * nstore)
            assert nstore == NST, nstore

        @block.vector
        def _(v):
            VC = Chain(v, vch)
            v.wait_ge(gset, 1)
            v.tensor_single_scalar(out=identb[:], in_=ioti[:], scalar=0,
                                   op=ALU.is_equal).then_inc(vset, 1)
            v.wait_ge(ldx, 256)
            v.wait_ge(ldc, 32)
            v.wait_ge(ldg0, 48)
            xn3 = xnb[:].rearrange("p (c t) -> p c t", c=16)
            for tb in range(2):
                base = tb * 512
                if tb == 1:
                    v.wait_ge(mmp, 4 * 16)
                v.tensor_tensor(out=xxb[:, :, 1:512],
                                in0=xn3[:, :, base:base + 511],
                                in1=xn3[:, :, base + 1:base + 512], op=ALU.subtract)
                if tb == 0:
                    VC(v.tensor_tensor(out=xxb[:, :, 0:1], in0=pcol[:].unsqueeze(2),
                                       in1=xn3[:, :, 0:1], op=ALU.subtract))
                else:
                    VC(v.tensor_tensor(out=xxb[:, :, 0:1], in0=xn3[:, :, 511:512],
                                       in1=xn3[:, :, 512:513], op=ALU.subtract))
                for dc in range(16):
                    v.scalar_tensor_tensor(out=mtmp[:, dc % 2, :], in0=xxb[:, dc, :],
                                           scalar=maas[:, dc:dc + 1], op0=ALU.mult,
                                           in1=xn3[:, dc, base:base + 512], op1=ALU.add)
                    v.tensor_copy(out=xxxb[:, dc, :],
                                  in_=mtmp[:, dc % 2, :]).then_inc(vxx, 1)
                for f in range(5):
                    v.wait_ge(sth, 5 * tb + f + 1)
                    if 5 * tb + f >= 2:
                        v.wait_ge(mmf, 16 * (5 * tb + f - 1))
                    v.tensor_copy(out=thfb[:, f % 2, :], in_=thf[:, f % 2, :]).then_inc(vth, 1)
                    for dc in range(16):
                        nfd = (tb * 5 + f) * 16 + dc
                        v.wait_ge(mmf, nfd + 1)
                        pm = pmF0 if dc % 2 == 0 else pmF1
                        v.tensor_scalar(out=mtmp[:, dc % 2, :], in0=pm[:, :],
                                        scalar1=maas[:, 16 * (f + 1) + dc: 16 * (f + 1) + dc + 1],
                                        scalar2=None, op0=ALU.add)
                        v.tensor_tensor(out=gtmp[:, dc % 2, :], in0=mtmp[:, dc % 2, :],
                                        in1=xxb[:, dc, :], op=ALU.mult)
                        v.tensor_tensor(out=xkb[:, f, dc, :], in0=gtmp[:, dc % 2, :],
                                        in1=xn3[:, dc, base:base + 512],
                                        op=ALU.add).then_inc(vxk, 1)
                v.wait_ge(sthd, tb + 1)
                v.tensor_copy(out=thdb[:, :], in_=thd[:, :]).then_inc(vthd, 1)
                for dc in range(16):
                    v.wait_ge(mmw, 16 * tb + dc + 1)
                    if dc >= 2:
                        v.wait_ge(sto, 16 * (128 * tb + dc - 1))
                    pw = pmF0 if dc % 2 == 0 else pmF1
                    v.tensor_scalar(out=evf[:, dc % 2, :], in0=pw[:, :],
                                    scalar1=maas[:, 16 * 6 + dc: 16 * 6 + dc + 1],
                                    scalar2=None, op0=ALU.add).then_inc(vw, 1)
                for pi, (nm, oap, wap, fi, kind) in enumerate(projs):
                    for oc in range(16):
                        rr = (tb * 4 + pi) * 16 + oc
                        pp = pp0 if rr % 2 == 0 else pp1
                        if kind == "cm":
                            v.wait_ge(mmp, rr + 1)
                            v.tensor_copy(out=evb[:, rr % 2, :], in_=pp[:, :]).then_inc(vev, 1)
                        elif kind == "cm_silu":
                            v.wait_ge(sthg, 16 * tb + oc + 1)
                            v.scalar_tensor_tensor(out=gtmp[:, rr % 2, :],
                                                   in0=thg[:, rr % 2, :], scalar=1.0,
                                                   in1=pp[:, :], op0=ALU.add, op1=ALU.mult)
                            v.tensor_scalar(out=evb[:, rr % 2, :], in0=gtmp[:, rr % 2, :],
                                            scalar1=0.5, scalar2=None,
                                            op0=ALU.mult).then_inc(vev, 1)
                        elif kind == "tok":
                            v.wait_ge(mmp, rr + 1)
                            v.tensor_copy(out=evb[:, rr % 2, :], in_=pp[:, :]).then_inc(vev, 1)
                            n_ = 16 * tb + oc
                            v.wait_ge(mmt, 4 * (n_ + 1))
                            ptx = ptr if n_ % 2 == 0 else ptr1
                            for sub in range(4):
                                v.tensor_copy(out=vtb[:, tb % 2, sub, :],
                                              in_=ptx[:, sub * 128:(sub + 1) * 128]).then_inc(vtv, 1)

        @block.tensor
        def _(t):
            t.wait_ge(vset, 1)
            t.wait_ge(ldc, 32)
            t.wait_ge(ldg0, 48)
            for tb in range(2):
                t.wait_ge(vxx, 16 * (tb + 1))
                for f in range(5):
                    if f >= 2:
                        t.wait_ge(sth, 5 * tb + f - 1)
                    pm = pmM0 if f % 2 == 0 else pmM1
                    inst = None
                    for dc in range(16):
                        inst = t.matmul(pm[0:32, :], w1b[:, dc, 32 * f:32 * (f + 1)],
                                        xxxb[:, dc, :], start=(dc == 0), stop=(dc == 15))
                    inst.then_inc(mmm, 1)
                for f in range(5):
                    r2 = tb * 5 + f
                    t.wait_ge(vth, 5 * tb + f + 1)
                    t.wait_ge(ldga if r2 % 2 == 0 else ldgb, 16 * (r2 // 2 + 1))
                    for dc in range(16):
                        nfd = (tb * 5 + f) * 16 + dc
                        if nfd >= 2:
                            t.wait_ge(vxk, nfd - 1)
                        pm = pmF0 if dc % 2 == 0 else pmF1
                        t.matmul(pm[:, :], w2b[:, r2 % 2, dc, :], thfb[:, f % 2, :],
                                 start=True, stop=True).then_inc(mmf, 1)
                t.wait_ge(vxk, 16 * 5 * (tb + 1))
                t.wait_ge(sth, 5 * (tb + 1))
                inst = None
                for dc in range(16):
                    inst = t.matmul(pmM0[0:64, :], dw1b[:, dc, :], xkb[:, 0, dc, :],
                                    start=(dc == 0), stop=(dc == 15))
                inst.then_inc(mmd, 1)
                t.wait_ge(vthd, tb + 1)
                for dc in range(16):
                    if 16 * tb + dc >= 2:
                        t.wait_ge(vw, 16 * tb + dc - 1)
                    pw = pmF0 if dc % 2 == 0 else pmF1
                    t.matmul(pw[:, :], dw2b[:, dc, :], thdb[:, :],
                             start=True, stop=True).then_inc(mmw, 1)
                for pi, (nm, oap, wap, fi, kind) in enumerate(projs):
                    for oc in range(16):
                        rr = (tb * 4 + pi) * 16 + oc
                        t.wait_ge(ldwa2 if rr % 2 == 0 else ldwb2, 16 * (rr // 2 + 1))
                        if rr >= 2:
                            t.wait_ge(vev, rr - 1)
                        pp = pp0 if rr % 2 == 0 else pp1
                        inst = None
                        for dc in range(16):
                            inst = t.matmul(pp[:, :], wstr[:, rr % 2, dc, :],
                                            xkb[:, fi, dc, tb * 0:512] if False else xkb[:, fi, dc, :],
                                            start=(dc == 0), stop=(dc == 15))
                        inst.then_inc(mmp, 1)
                        if kind == "tok":
                            t.wait_ge(vev, rr + 1)
                            n_ = 16 * tb + oc
                            if n_ >= 2:
                                t.wait_ge(vtv, 4 * (n_ - 1))
                            ptx = ptr if n_ % 2 == 0 else ptr1
                            inst = None
                            for sub in range(4):
                                inst = t.transpose(ptx[:, sub * 128:(sub + 1) * 128],
                                                   evb[:, rr % 2, sub * 128:(sub + 1) * 128],
                                                   identb[:]).then_inc(mmt, 1)
    al.close()
    return nc



def build_E():
    nc = _new_nc()
    r_i = nc.dram_tensor("r_i", [1024, SEQ], BF16, kind="ExternalInput").ap()
    k_i = nc.dram_tensor("k_i", [1024, SEQ], BF16, kind="ExternalInput").ap()
    w_i = nc.dram_tensor("w_i", [1024, SEQ], F32, kind="ExternalInput").ap()
    v_i = nc.dram_tensor("v_i", [SEQ, 1024], BF16, kind="ExternalInput").ap()
    st_i = nc.dram_tensor("st_i", [1024, 64], F32, kind="ExternalInput").ap()
    faT_i = nc.dram_tensor("faT_i", [64, 16], F32, kind="ExternalInput").ap()
    lnw_i = nc.dram_tensor("lnw_i", [1, 1024], F32, kind="ExternalInput").ap()
    lnb_i = nc.dram_tensor("lnb_i", [1, 1024], F32, kind="ExternalInput").ap()
    yn_o = nc.dram_tensor("yn_o", [SEQ, 1024], BF16, kind="ExternalOutput").ap()

    al = Alloc(nc)
    W = 16 * 64
    rc = al.sb("rc", [64, 2, W], BF16)
    kc = al.sb("kc", [64, 2, W], BF16)
    wc = al.sb("wc", [64, 2, W], F32)
    ex = al.sb("ex", [64, W], F32)
    cum = al.sb("cum", [64, W], F32)
    tmp1 = al.sb("tmp1", [64, W], F32)
    tmp2 = al.sb("tmp2", [64, W], F32)
    cumL = al.sb("cumL", [64, 16], F32)
    aL = al.sb("aL", [64, 16], F32)
    aLt = al.sb("aLt", [64, W], F32)
    expR = al.sb("expR", [64, W], F32)
    expK = al.sb("expK", [64, W], F32)
    expKh = al.sb("expKh", [64, W], F32)
    ktil = al.sb("ktil", [64, W], BF16)
    khat = al.sb("khat", [64, W], BF16)
    rku = al.sb("rku", [64, W], BF16)
    ystk = al.sb("ystk", [128, W], BF16)
    sv = al.sb("sv", [128, 2, W], BF16)
    vb0 = al.sb("vb0", [64, 2, W], BF16)
    ssb = al.sb("ssb", [64, W], F32)
    khT0 = al.sb("khT0", [64, 8 * 64], BF16)
    khT1 = al.sb("khT1", [64, 8 * 64], BF16)
    maskr = al.sb("maskr", [64, W], BF16)
    onesj = al.sb("onesj", [64, 1], BF16)
    urep = al.sb("urep", [64, W], F32)
    faT = al.sb("faT", [64, 16], F32)
    dsb = al.sb("dsb", [64, 16], F32)
    drep = al.sb("drep", [64, W], F32)
    yf = al.sb("yf", [64, W], F32)
    y3 = al.sb("y3", [64, W], F32)
    mrow = al.sb("mrow_e", [64, 16], F32)
    vrow = al.sb("vrow_e", [64, 16], F32)
    rsr = al.sb("rsr_e", [64, 16], F32)
    sc2 = al.sb("sc2_e", [64, 16], F32)
    mrep2 = al.sb("mrep2", [64, W], F32)
    rsrep = al.sb("rsrep", [64, W], F32)
    lnwr = al.sb("lnwr", [64, W], F32)
    lnbr = al.sb("lnbr", [64, W], F32)
    ynb = al.sb("ynb", [64, 2, W], BF16)
    ioti = al.sb("ioti_e", [64, 64], I32)
    identb = al.sb("identb_e", [64, 64], BF16)
    onesT = al.sb("onesT", [64, 64], F32)

    pP = al.ps("pP", [64, W], F32)
    pY = al.ps("pY", [64, W], F32)
    pSK = al.ps("pSK", [64, W], F32)
    pT = al.ps("pT", [128, 512], BF16)
    pd = al.ps("pd", [64, 16], F32)

    ldca = al.sem("ldca"); ldcb = al.sem("ldcb")
    ldi = al.sem("ldi")
    gset = al.sem("gset"); vset = al.sem("vset"); vch = al.sem("vch"); sch = al.sem("sch")
    vpre = al.sem("vpre")
    spre = al.sem("spre")
    mmP = al.sem("mmP"); vPM = al.sem("vPM")
    mmd = al.sem("mmd"); vd = al.sem("vd")
    mmY = al.sem("mmY"); vY = al.sem("vY")
    mmT = al.sem("mmT"); vT = al.sem("vT")
    mmS = al.sem("mmS"); vS = al.sem("vS")
    vgn = al.sem("vgn"); sgn = al.sem("sgn")
    stoa = al.sem("stoa"); stob = al.sem("stob")

    block = al.block()
    NC_ = NCHUNK
    if True:
        @block.sync
        def _(sync):
            sync.dma_start(out=ssb[:].rearrange("p (g i) -> p g i", g=16),
                           in_=AP(st_i.tensor, 0, [[64, 64], [64 * 64, 16], [1, 64]])).then_inc(ldi, 16)
            sync.dma_start(out=faT[:], in_=faT_i[:, :]).then_inc(ldi, 16)
            sync.dma_start(out=lnwr[:], in_=AP(lnw_i.tensor, 0, [[0, 64], [1, W]])).then_inc(ldi, 16)
            sync.dma_start(out=lnbr[:], in_=AP(lnb_i.tensor, 0, [[0, 64], [1, W]])).then_inc(ldi, 16)
            for it in range(NC_ + 2):
              c = it
              if it < NC_:
                if c >= 2:
                    sync.wait_ge(vS, c - 1)
                sync.dma_start(out=rc[:, c % 2, :].rearrange("p (g t) -> p g t", g=16),
                               in_=AP(r_i.tensor, c * 64, [[SEQ, 64], [64 * SEQ, 16], [1, 64]])).then_inc(ldca if c % 2 == 0 else ldcb, 16)
                sync.dma_start(out=kc[:, c % 2, :].rearrange("p (g t) -> p g t", g=16),
                               in_=AP(k_i.tensor, c * 64, [[SEQ, 64], [64 * SEQ, 16], [1, 64]])).then_inc(ldca if c % 2 == 0 else ldcb, 16)
                sync.dma_start(out=wc[:, c % 2, :].rearrange("p (g t) -> p g t", g=16),
                               in_=AP(w_i.tensor, c * 64, [[SEQ, 64], [64 * SEQ, 16], [1, 64]])).then_inc(ldca if c % 2 == 0 else ldcb, 16)
                sync.dma_start(out=sv[64:128, c % 2, :].rearrange("p (g i) -> p g i", g=16),
                               in_=AP(v_i.tensor, c * 64 * 1024, [[1024, 64], [64, 16], [1, 64]])).then_inc(ldca if c % 2 == 0 else ldcb, 16)
                sync.dma_start(out=vb0[:, c % 2, :].rearrange("p (g i) -> p g i", g=16),
                               in_=AP(v_i.tensor, c * 64 * 1024, [[1024, 64], [64, 16], [1, 64]])).then_inc(ldca if c % 2 == 0 else ldcb, 16)
              c2 = it - 2
              if c2 >= 0:
                sync.wait_ge(vgn, 8 * (c2 + 1))
                sync.dma_start(out=yn_o[c2 * 64:(c2 + 1) * 64, :],
                               in_=ynb[:, c2 % 2, :]).then_inc(stoa if c2 % 2 == 0 else stob, 16)
            sync.wait_ge(stoa, 16 * (NC_ // 2))
            sync.wait_ge(stob, 16 * (NC_ // 2))

        @block.gpsimd
        def _(g):
            g.iota(ioti[:], pattern=[[1, 64]], base=0, channel_multiplier=-1).then_inc(gset, 1)
            g.memset(onesj[:], 1.0).then_inc(gset, 1)
            g.memset(onesT[:], 1.0).then_inc(gset, 1)

        @block.scalar
        def _(s):
            SC = Chain(s, sch)
            for c in range(NC_):
                s.wait_ge(ldca if c % 2 == 0 else ldcb, 16 * (5 * (c // 2) + 3))
                s.activation(out=ex[:, :], in_=wc[:, c % 2, :], func=AF.Exp,
                             bias=0.0, scale=1.0).then_inc(spre, 1)
                s.wait_ge(vpre, 6 * c + 2)
                s.activation(out=expR[:, :], in_=tmp1[:, :], func=AF.Exp,
                             bias=0.0, scale=1.0).then_inc(spre, 1)
                s.activation(out=expK[:, :], in_=cum[:, :], func=AF.Exp,
                             bias=0.0, scale=1.0).then_inc(spre, 1)
                s.wait_ge(vpre, 6 * c + 3)
                s.activation(out=expKh[:, :], in_=tmp2[:, :], func=AF.Exp,
                             bias=0.0, scale=1.0).then_inc(spre, 1)
                s.wait_ge(vpre, 6 * c + 4)
                SC(s.activation(out=aL[:, :], in_=cumL[:, :], func=AF.Exp,
                                bias=0.0, scale=-1.0))
                s.sem_inc(spre, 1)
                s.wait_ge(vgn, 8 * c + 4)
                SC(s.activation(out=sc2[:, :], in_=vrow[:, :], func=AF.Ln,
                                bias=0.0, scale=1.0))
                SC(s.activation(out=rsr[:, :], in_=sc2[:, :], func=AF.Exp,
                                bias=0.0, scale=-0.5))
                s.sem_inc(sgn, 1)

        @block.vector
        def _(v):
            VC = Chain(v, vch)
            v.wait_ge(gset, 3)
            v.tensor_scalar(out=identb[:, :], in0=ioti[:], scalar1=0, scalar2=None,
                            op0=ALU.is_equal).then_inc(vset, 1)
            v.tensor_scalar(out=maskr[:].rearrange("p (g t) -> p g t", g=16),
                            in0=ioti[:].unsqueeze(1).broadcast_to([64, 16, 64]),
                            scalar1=0, scalar2=None, op0=ALU.is_gt).then_inc(vset, 1)
            v.wait_ge(ldi, 64)
            v.tensor_copy(out=urep[:].rearrange("p (g t) -> p g t", g=16),
                          in_=faT[:].unsqueeze(2).broadcast_to([64, 16, 64])).then_inc(vset, 1)
            v.tensor_copy(out=sv[0:64, 0, :], in_=ssb[:]).then_inc(vset, 1)
            for c in range(NC_):
                cb = c % 2
                v.wait_ge(spre, 5 * c + 1)
                inst = None
                for g2 in range(16):
                    inst = v.tensor_tensor_scan(out=cum[:, g2 * 64:(g2 + 1) * 64],
                                                data0=onesT[:, :],
                                                data1=ex[:, g2 * 64:(g2 + 1) * 64],
                                                initial=0.0, op0=ALU.mult, op1=ALU.add)
                VC(inst)
                v.sem_inc(vpre, 1)
                v.tensor_tensor(out=tmp1[:, :], in0=ex[:, :], in1=cum[:, :],
                                op=ALU.subtract).then_inc(vpre, 1)
                VC(v.tensor_copy(out=cumL[:, :],
                                 in_=cum[:].rearrange("p (g t) -> p g t", g=16)[:, :, 63]))
                VC(v.tensor_copy(out=aLt[:].rearrange("p (g t) -> p g t", g=16),
                                 in_=cumL[:].unsqueeze(2).broadcast_to([64, 16, 64])))
                v.tensor_tensor(out=tmp2[:, :], in0=cum[:, :], in1=aLt[:, :],
                                op=ALU.subtract).then_inc(vpre, 1)
                v.sem_inc(vpre, 1)
                v.wait_ge(spre, 5 * c + 5)
                VC(v.tensor_copy(out=aLt[:].rearrange("p (g t) -> p g t", g=16),
                                 in_=aL[:].unsqueeze(2).broadcast_to([64, 16, 64])))
                v.sem_inc(vpre, 1)
                v.wait_ge(spre, 5 * c + 4)
                v.tensor_tensor(out=ystk[0:64, :], in0=rc[:, cb, :], in1=expR[:, :],
                                op=ALU.mult).then_inc(vpre, 1)
                v.tensor_tensor(out=ktil[:, :], in0=kc[:, cb, :], in1=expK[:, :],
                                op=ALU.mult).then_inc(vPM, 1)
                v.tensor_tensor(out=khat[:, :], in0=kc[:, cb, :], in1=expKh[:, :],
                                op=ALU.mult).then_inc(vT, 1)
                v.tensor_tensor(out=tmp1[:, :], in0=rc[:, cb, :], in1=kc[:, cb, :],
                                op=ALU.mult)
                v.tensor_tensor(out=rku[:, :], in0=tmp1[:, :], in1=urep[:, :],
                                op=ALU.mult).then_inc(vd, 1)
                v.wait_ge(mmP, 16 * (c + 1))
                v.tensor_tensor(out=ystk[64:128, :], in0=pP[:, :], in1=maskr[:, :],
                                op=ALU.mult).then_inc(vPM, 1)
                v.wait_ge(mmd, 16 * (c + 1))
                VC(v.tensor_copy(out=dsb[:, :], in_=pd[:, :]))
                VC(v.tensor_copy(out=drep[:].rearrange("p (g t) -> p g t", g=16),
                                 in_=dsb[:].unsqueeze(2).broadcast_to([64, 16, 64])))
                v.sem_inc(vd, 1)
                v.wait_ge(ldca if c % 2 == 0 else ldcb, 16 * (5 * (c // 2) + 5))
                v.tensor_tensor(out=y3[:, :], in0=vb0[:, cb, :], in1=drep[:, :],
                                op=ALU.mult)
                v.wait_ge(mmY, 16 * (c + 1))
                v.scalar_tensor_tensor(out=yf[:, :], in0=y3[:, :], scalar=1.0,
                                       in1=pY[:, :], op0=ALU.mult, op1=ALU.add).then_inc(vY, 1)
                v.wait_ge(mmT, 8 * (c + 1))
                v.tensor_copy(out=khT0[:, :], in_=pT[0:64, 0:512]).then_inc(vT, 1)
                v.tensor_copy(out=khT1[:, :], in_=pT[64:128, 0:512]).then_inc(vT, 1)
                v.wait_ge(mmS, 16 * (c + 1))
                v.tensor_tensor(out=tmp2[:, :], in0=ssb[:, :], in1=aLt[:, :], op=ALU.mult)
                v.scalar_tensor_tensor(out=ssb[:, :], in0=tmp2[:, :], scalar=1.0,
                                       in1=pSK[:, :], op0=ALU.mult, op1=ALU.add)
                if c < NC_ - 1:
                    v.tensor_copy(out=sv[0:64, (c + 1) % 2, :], in_=ssb[:, :]).then_inc(vS, 1)
                else:
                    v.sem_inc(vS, 1)
                v.tensor_tensor(out=tmp1[:, :], in0=yf[:, :], in1=yf[:, :], op=ALU.mult)
                VC(v.reduce_sum(out=mrow[:, :].unsqueeze(2),
                                in_=yf[:].rearrange("p (g t) -> p g t", g=16), axis=AX))
                VC(v.reduce_sum(out=vrow[:, :].unsqueeze(2),
                                in_=tmp1[:].rearrange("p (g t) -> p g t", g=16), axis=AX))
                VC(v.tensor_scalar(out=mrow[:, :], in0=mrow[:, :], scalar1=1.0 / 64,
                                   scalar2=None, op0=ALU.mult))
                v.sem_inc(vgn, 3)
                VC(v.tensor_scalar(out=sc2[:, :], in0=vrow[:, :], scalar1=1.0 / 64,
                                   scalar2=None, op0=ALU.mult))
                VC(v.tensor_tensor(out=vrow[:, :], in0=mrow[:, :], in1=mrow[:, :],
                                   op=ALU.mult))
                VC(v.scalar_tensor_tensor(out=vrow[:, :], in0=vrow[:, :], scalar=-1.0,
                                          in1=sc2[:, :], op0=ALU.mult, op1=ALU.add))
                VC(v.tensor_scalar(out=vrow[:, :], in0=vrow[:, :], scalar1=GN_EPS,
                                   scalar2=None, op0=ALU.add))
                v.sem_inc(vgn, 1)
                VC(v.tensor_copy(out=mrep2[:].rearrange("p (g t) -> p g t", g=16),
                                 in_=mrow[:].unsqueeze(2).broadcast_to([64, 16, 64])))
                v.wait_ge(sgn, c + 1)
                VC(v.tensor_copy(out=rsrep[:].rearrange("p (g t) -> p g t", g=16),
                                 in_=rsr[:].unsqueeze(2).broadcast_to([64, 16, 64])))
                v.tensor_tensor(out=tmp1[:, :], in0=yf[:, :], in1=mrep2[:, :],
                                op=ALU.subtract)
                v.tensor_tensor(out=tmp2[:, :], in0=tmp1[:, :], in1=rsrep[:, :],
                                op=ALU.mult)
                v.tensor_tensor(out=tmp1[:, :], in0=tmp2[:, :], in1=lnwr[:, :],
                                op=ALU.mult)
                if c >= 2:
                    v.wait_ge(stoa if c % 2 == 0 else stob, 16 * (c // 2))
                v.tensor_tensor(out=ynb[:, c % 2, :], in0=tmp1[:, :], in1=lnbr[:, :],
                                op=ALU.add).then_inc(vgn, 4)

        @block.tensor
        def _(t):
            t.wait_ge(vset, 4)
            for c in range(NC_):
                cb = c % 2
                t.wait_ge(vPM, 2 * c + 1)
                t.wait_ge(vpre, 6 * c + 6)
                for g2 in range(16):
                    t.matmul(pP[:, g2 * 64:(g2 + 1) * 64], ktil[:, g2 * 64:(g2 + 1) * 64],
                             ystk[0:64, g2 * 64:(g2 + 1) * 64],
                             start=True, stop=True).then_inc(mmP, 1)
                t.wait_ge(vd, 2 * c + 1)
                for g2 in range(16):
                    t.matmul(pd[:, g2:g2 + 1], rku[:, g2 * 64:(g2 + 1) * 64],
                             onesj[:, 0:1], start=True, stop=True).then_inc(mmd, 1)
                t.wait_ge(vPM, 2 * c + 2)
                t.wait_ge(ldca if c % 2 == 0 else ldcb, 16 * (5 * (c // 2) + 5))
                t.wait_ge(vS, c)
                for g2 in range(16):
                    t.matmul(pY[:, g2 * 64:(g2 + 1) * 64], ystk[:, g2 * 64:(g2 + 1) * 64],
                             sv[:, cb, g2 * 64:(g2 + 1) * 64],
                             start=True, stop=True).then_inc(mmY, 1)
                t.wait_ge(vT, 3 * c + 1)
                for q2 in range(8):
                    t.transpose(pT[:, q2 * 64:(q2 + 1) * 64],
                                khat[:, q2 * 128:(q2 + 1) * 128],
                                identb[:, :]).then_inc(mmT, 1)
                t.wait_ge(vT, 3 * c + 3)
                for g2 in range(16):
                    src_t = khT0 if g2 % 2 == 0 else khT1
                    t.matmul(pSK[:, g2 * 64:(g2 + 1) * 64],
                             src_t[:, (g2 // 2) * 64:(g2 // 2) * 64 + 64],
                             vb0[:, cb, g2 * 64:(g2 + 1) * 64],
                             start=True, stop=True).then_inc(mmS, 1)
    al.close()
    return nc



def build_F1():
    nc = _new_nc()
    yn_i = nc.dram_tensor("yn_i", [HID, TOK], BF16, kind="ExternalInput").ap()
    g_i = nc.dram_tensor("g_i", [HID, TOK], BF16, kind="ExternalInput").ap()
    woT_i = nc.dram_tensor("woT_i", [HID, HID], F32, kind="ExternalInput").ap()
    x1_i = nc.dram_tensor("x1_i", [HID, TOK], F32, kind="ExternalInput").ap()
    o1_o = nc.dram_tensor("o1_o", [HID, TOK], F32, kind="ExternalOutput").ap()

    al = Alloc(nc)
    ygsb = al.sb("ygsb", [128, 16 * TOK], BF16)
    ynst = al.sb("ynst", [128, 2, TOK], BF16)
    gst = al.sb("gst", [128, 2, TOK], BF16)
    wo2 = al.sb("wo2_f1", [128, 2, 16, 128], BF16)
    x1r = al.sb("x1r_f1", [128, 2, TOK], F32)
    o1s = al.sb("o1s", [128, 2, TOK], F32)
    pa0 = al.ps("pa0", [128, 512], F32)
    pa1 = al.ps("pa1", [128, 512], F32)

    ld0a = al.sem("ld0a"); ld0b = al.sem("ld0b")
    ldwa = al.sem("ldwa"); ldwb = al.sem("ldwb")
    ldxa = al.sem("ldxa"); ldxb = al.sem("ldxb")
    mm1 = al.sem("mm1"); vv = al.sem("vv")
    stxa = al.sem("stxa"); stxb = al.sem("stxb")

    block = al.block()
    if True:
        @block.sync
        def _(sync):
            for ic in range(16):
                if ic >= 2:
                    sync.wait_ge(vv, ic - 1)
                l0 = ld0a if ic % 2 == 0 else ld0b
                sync.dma_start(out=ynst[:, ic % 2, :],
                               in_=yn_i[ic * 128:(ic + 1) * 128, :]).then_inc(l0, 16)
                sync.dma_start(out=gst[:, ic % 2, :],
                               in_=g_i[ic * 128:(ic + 1) * 128, :]).then_inc(l0, 16)
            for oc in range(16):
                if oc >= 2:
                    sync.wait_ge(vv, 16 + 2 * (oc - 1))
                sync.dma_start(out=x1r[:, oc % 2, :],
                               in_=x1_i[oc * 128:(oc + 1) * 128, :]).then_inc(
                                   ldxa if oc % 2 == 0 else ldxb, 16)
                sync.wait_ge(vv, 16 + 2 * (oc + 1))
                sync.dma_start(out=o1_o[oc * 128:(oc + 1) * 128, :],
                               in_=o1s[:, oc % 2, :]).then_inc(stxa if oc % 2 == 0 else stxb, 16)
            sync.wait_ge(stxa, 16 * 8)
            sync.wait_ge(stxb, 16 * 8)

        @block.gpsimd
        def _(g):
            for oc in range(16):
                if oc >= 2:
                    g.wait_ge(mm1, 2 * (oc - 1))
                g.dma_start(out=wo2[:, oc % 2, :, :],
                            in_=AP(woT_i.tensor, oc * 128,
                                   [[HID, 128], [HID * 128, 16], [1, 128]])).then_inc(
                                       ldwa if oc % 2 == 0 else ldwb, 16)

        @block.vector
        def _(v):
            for ic in range(16):
                v.wait_ge(ld0a if ic % 2 == 0 else ld0b, 32 * (ic // 2 + 1))
                v.tensor_tensor(out=ygsb[:, ic * TOK:(ic + 1) * TOK],
                                in0=ynst[:, ic % 2, :], in1=gst[:, ic % 2, :],
                                op=ALU.mult).then_inc(vv, 1)
            for oc in range(16):
                v.wait_ge(mm1, 2 * (oc + 1))
                v.wait_ge(ldxa if oc % 2 == 0 else ldxb, 16 * (oc // 2 + 1))
                if oc >= 2:
                    v.wait_ge(stxa if oc % 2 == 0 else stxb, 16 * (oc // 2))
                v.tensor_tensor(out=o1s[:, oc % 2, 0:512], in0=pa0[:, :],
                                in1=x1r[:, oc % 2, 0:512], op=ALU.add).then_inc(vv, 1)
                v.tensor_tensor(out=o1s[:, oc % 2, 512:1024], in0=pa1[:, :],
                                in1=x1r[:, oc % 2, 512:1024], op=ALU.add).then_inc(vv, 1)

        @block.tensor
        def _(t):
            t.wait_ge(vv, 16)
            for oc in range(16):
                t.wait_ge(ldwa if oc % 2 == 0 else ldwb, 16 * (oc // 2 + 1))
                if oc >= 1:
                    t.wait_ge(vv, 16 + 2 * oc)
                inst = None
                for dc in range(16):
                    inst = t.matmul(pa0[:, :], wo2[:, oc % 2, dc, :],
                                    ygsb[:, dc * TOK:dc * TOK + 512],
                                    start=(dc == 0), stop=(dc == 15))
                inst.then_inc(mm1, 1)
                inst = None
                for dc in range(16):
                    inst = t.matmul(pa1[:, :], wo2[:, oc % 2, dc, :],
                                    ygsb[:, dc * TOK + 512:dc * TOK + 1024],
                                    start=(dc == 0), stop=(dc == 15))
                inst.then_inc(mm1, 1)
    al.close()
    return nc



def build_F2():
    nc = _new_nc()
    xn1b = nc.dram_tensor("xn1b", [HID, TOK], BF16, kind="ExternalInput").ap()
    o1_i = nc.dram_tensor("o1_i", [HID, TOK], F32, kind="ExternalInput").ap()
    smy_i = nc.dram_tensor("smy_i", [8, TOK], F32, kind="ExternalInput").ap()
    smf_i = nc.dram_tensor("smf_i", [8, 8192], F32, kind="ExternalInput").ap()
    wg_i = nc.dram_tensor("wg_i", [HID, DFF1P], F32, kind="ExternalInput").ap()
    wu_i = nc.dram_tensor("wu_i", [HID, DFF1P], F32, kind="ExternalInput").ap()
    wd_i = nc.dram_tensor("wd_i", [DFF1P, HID], F32, kind="ExternalInput").ap()
    bgc_i = nc.dram_tensor("bgc_i", [128, 43], F32, kind="ExternalInput").ap()
    buc_i = nc.dram_tensor("buc_i", [128, 43], F32, kind="ExternalInput").ap()
    bdc_i = nc.dram_tensor("bdc_i", [128, 16], F32, kind="ExternalInput").ap()
    out_o = nc.dram_tensor("out_o", [HID, TOK], F32, kind="ExternalOutput").ap()

    al = Alloc(nc)
    smfs = al.sb("smfs", [8, 8192], F32)
    mskb = al.sb("mskb", [8, 8192], BF16)
    smys = al.sb("smys", [8, TOK], F32)
    lo = al.sb("lo", [8, 2], F32)
    hi = al.sb("hi", [8, 2], F32)
    midc = al.sb("midc", [8, 1], F32)
    cnt = al.sb("cnt", [8, 1], F32)
    sel = al.sb("sel", [8, 1], F32)
    tt1 = al.sb("tt1", [8, 1], F32)
    tt2 = al.sb("tt2", [8, 1], F32)
    kcol = al.sb("kcol", [8, 1], F32)
    ones8 = al.sb("ones8", [8, 1], F32)
    onec = al.sb("onec_f2", [1, 128], F32)
    msky = al.sb("msky", [8, TOK], F32)
    smm = al.sb("smm", [8, TOK], F32)
    srow = al.sb("srow", [1, TOK], F32)
    srep = al.sb("srep_f2", [128, TOK], F32)
    bgcs = al.sb("bgcs", [128, 43], F32)
    bucs = al.sb("bucs", [128, 43], F32)
    bdcs = al.sb("bdcs", [128, 16], F32)
    xnp = al.sb("xnp", [128, 16, 512], BF16)
    hsb = al.sb("hsb_f2", [128, 43 * 512], BF16)
    wg2 = al.sb("wg2_f2", [128, 2, 16, 128], BF16)
    wu2 = al.sb("wu2_f2", [128, 2, 16, 128], BF16)
    wd2 = al.sb("wd2_f2", [128, 2, 43, 128], BF16)
    pgb = al.sb("pgb", [128, 2, 512], F32)
    pub = al.sb("pub", [128, 2, 512], F32)
    thsb = al.sb("thsb_f2", [128, 2, 512], F32)
    v1sb = al.sb("v1sb_f2", [128, 2, 512], F32)
    dcb = al.sb("dcb", [128, 2, 512], F32)
    tsc = al.sb("tsc", [128, 2, 512], F32)
    o1r = al.sb("o1r", [128, 2, 512], F32)
    outs = al.sb("outs", [128, 2, 512], F32)

    pg0 = al.ps("pg0_f2", [128, 512], F32)
    pg1 = al.ps("pg1_f2", [128, 512], F32)
    pu0 = al.ps("pu0_f2", [128, 512], F32)
    pu1 = al.ps("pu1_f2", [128, 512], F32)
    pdn = al.ps("pdn_f2", [128, 512], F32)

    ldp = al.sem("ldp"); ldx = al.sem("ldx"); ldo = al.sem("ldo")
    ldw0 = al.sem("ldw0"); ldw1a = al.sem("ldw1a"); ldw1b = al.sem("ldw1b")
    vchf = al.sem("vchf")
    gset = al.sem("gset")
    vpro = al.sem("vpro"); mmpro = al.sem("mmpro")
    mmg = al.sem("mmg"); mmu = al.sem("mmu"); mmc = al.sem("mmc")
    vh = al.sem("vh"); vh2 = al.sem("vh2"); sh = al.sem("sh")
    vc = al.sem("vc"); vo = al.sem("vo")
    stoc = al.sem("stoc"); stod = al.sem("stod")

    NIC = 43
    NBIS = 42
    block = al.block()
    if True:
        @block.sync
        def _(sync):
            sync.dma_start(out=smfs[:, :], in_=smf_i[:, :]).then_inc(ldp, 16)
            sync.dma_start(out=smys[:, :], in_=smy_i[:, :]).then_inc(ldp, 16)
            sync.dma_start(out=bgcs[:, :], in_=bgc_i[:, :]).then_inc(ldp, 16)
            sync.dma_start(out=bucs[:, :], in_=buc_i[:, :]).then_inc(ldp, 16)
            sync.dma_start(out=bdcs[:, :], in_=bdc_i[:, :]).then_inc(ldp, 16)
            for tp in range(2):
                if tp == 1:
                    sync.wait_ge(mmu, NIC)
                for dc in range(16):
                    sync.dma_start(out=xnp[:, dc, :],
                                   in_=xn1b[dc * 128:(dc + 1) * 128,
                                            tp * 512:(tp + 1) * 512]).then_inc(ldx, 16)
                for oc in range(16):
                    rr = tp * 16 + oc
                    if rr >= 2:
                        sync.wait_ge(vo, rr - 1)
                    sync.dma_start(out=o1r[:, rr % 2, :],
                                   in_=o1_i[oc * 128:(oc + 1) * 128,
                                            tp * 512:(tp + 1) * 512]).then_inc(ldo, 16)
                    if rr >= 1:
                        sync.wait_ge(vo, rr)
                        po, pt = (rr - 1) % 16, (rr - 1) // 16
                        sync.dma_start(out=out_o[po * 128:(po + 1) * 128,
                                                 pt * 512:(pt + 1) * 512],
                                       in_=outs[:, (rr - 1) % 2, :]).then_inc(
                                           stoc if (rr - 1) % 2 == 0 else stod, 16)
            sync.wait_ge(vo, 32)
            sync.dma_start(out=out_o[15 * 128:16 * 128, 512:1024],
                           in_=outs[:, 31 % 2, :]).then_inc(stod, 16)
            sync.wait_ge(stoc, 16 * 16)
            sync.wait_ge(stod, 16 * 16)

        @block.gpsimd
        def _(g):
            g.memset(kcol[:], float(CAP_K)).then_inc(gset, 1)
            g.memset(ones8[:], 1.0).then_inc(gset, 1)
            g.memset(onec[:], 1.0).then_inc(gset, 1)
            g.memset(lo[:], 0.0).then_inc(gset, 1)
            g.memset(hi[:], 1.0).then_inc(gset, 1)
            for tp in range(2):
                for f in range(NIC):
                    r = tp * NIC + f
                    if r >= 2:
                        g.wait_ge(vh, r - 1)
                    g.dma_start(out=wg2[:, r % 2, :, :],
                                in_=AP(wg_i.tensor, f * 128,
                                       [[DFF1P, 128], [DFF1P * 128, 16], [1, 128]])).then_inc(ldw0, 16)
                    g.dma_start(out=wu2[:, r % 2, :, :],
                                in_=AP(wu_i.tensor, f * 128,
                                       [[DFF1P, 128], [DFF1P * 128, 16], [1, 128]])).then_inc(ldw0, 16)
                for oc in range(16):
                    rr = tp * 16 + oc
                    if rr >= 2:
                        g.wait_ge(mmc, rr - 1)
                    g.dma_start(out=wd2[:, rr % 2, :, :],
                                in_=AP(wd_i.tensor, oc * 128,
                                       [[HID, 128], [HID * 128, 43], [1, 128]])).then_inc(
                                           ldw1a if rr % 2 == 0 else ldw1b, 16)

        @block.scalar
        def _(s):
            for tp in range(2):
                for f in range(NIC):
                    r = tp * NIC + f
                    s.wait_ge(vh2, 2 * r + 1)
                    s.activation(out=thsb[:, r % 2, :], in_=pgb[:, r % 2, :],
                                 func=AF.Tanh, bias=0.0, scale=0.5).then_inc(sh, 1)

        @block.vector
        def _(v):
            VC = Chain(v, vchf)
            v.wait_ge(ldp, 80)
            v.wait_ge(gset, 5)
            for it in range(NBIS):
                i2, o2 = it % 2, (it + 1) % 2
                VC(v.tensor_tensor(out=tt1[:, :], in0=lo[:, i2:i2 + 1],
                                   in1=hi[:, i2:i2 + 1], op=ALU.add))
                VC(v.tensor_scalar(out=midc[:, :], in0=tt1[:, :], scalar1=0.5,
                                   scalar2=None, op0=ALU.mult))
                VC(v.tensor_scalar(out=mskb[:, :], in0=smfs[:, :],
                                   scalar1=midc[:, 0:1], scalar2=None, op0=ALU.is_ge))
                VC(v.reduce_sum(out=cnt[:, :], in_=mskb[:, :], axis=AX))
                VC(v.tensor_scalar(out=sel[:, :], in0=cnt[:, :],
                                   scalar1=kcol[:, 0:1], scalar2=None, op0=ALU.is_ge))
                VC(v.tensor_tensor(out=tt1[:, :], in0=midc[:, :], in1=lo[:, i2:i2 + 1],
                                   op=ALU.subtract))
                VC(v.tensor_tensor(out=tt2[:, :], in0=sel[:, :], in1=tt1[:, :], op=ALU.mult))
                VC(v.tensor_tensor(out=lo[:, o2:o2 + 1], in0=lo[:, i2:i2 + 1],
                                   in1=tt2[:, :], op=ALU.add))
                VC(v.tensor_tensor(out=tt1[:, :], in0=hi[:, i2:i2 + 1], in1=midc[:, :],
                                   op=ALU.subtract))
                VC(v.tensor_tensor(out=tt2[:, :], in0=sel[:, :], in1=tt1[:, :], op=ALU.mult))
                VC(v.tensor_tensor(out=hi[:, o2:o2 + 1], in0=midc[:, :], in1=tt2[:, :],
                                   op=ALU.add))
            tf = NBIS % 2
            VC(v.tensor_scalar(out=msky[:, :], in0=smys[:, :],
                               scalar1=lo[:, tf:tf + 1], scalar2=None, op0=ALU.is_ge))
            v.tensor_tensor(out=smm[:, :], in0=smys[:, :], in1=msky[:, :],
                            op=ALU.mult).then_inc(vpro, 1)
            v.wait_ge(mmpro, 2)
            v.tensor_copy(out=srow[0:1, 0:512], in_=pg0[0:1, :]).then_inc(vpro, 1)
            v.tensor_copy(out=srow[0:1, 512:1024], in_=pg1[0:1, :]).then_inc(vpro, 1)
            v.wait_ge(mmpro, 4)
            v.tensor_copy(out=srep[:, 0:512], in_=pu0[:, :])
            v.tensor_copy(out=srep[:, 512:1024], in_=pu1[:, :]).then_inc(vpro, 1)
            for tp in range(2):
                for f in range(NIC):
                    r = tp * NIC + f
                    if tp == 1 and f == 0:
                        v.wait_ge(mmc, 16)
                    v.wait_ge(mmg, r + 1)
                    pg = pg0 if r % 2 == 0 else pg1
                    pu = pu0 if r % 2 == 0 else pu1
                    v.tensor_scalar(out=pgb[:, r % 2, :], in0=pg[:, :],
                                    scalar1=bgcs[:, f:f + 1], scalar2=None,
                                    op0=ALU.add).then_inc(vh2, 1)
                    v.wait_ge(mmu, r + 1)
                    v.tensor_scalar(out=pub[:, r % 2, :], in0=pu[:, :],
                                    scalar1=bucs[:, f:f + 1], scalar2=None,
                                    op0=ALU.add).then_inc(vh2, 1)
                    v.wait_ge(sh, r + 1)
                    v.scalar_tensor_tensor(out=v1sb[:, r % 2, :], in0=thsb[:, r % 2, :],
                                           scalar=1.0, in1=pgb[:, r % 2, :],
                                           op0=ALU.add, op1=ALU.mult)
                    v.scalar_tensor_tensor(out=hsb[:, f * 512:(f + 1) * 512],
                                           in0=v1sb[:, r % 2, :], scalar=0.5,
                                           in1=pub[:, r % 2, :],
                                           op0=ALU.mult, op1=ALU.mult).then_inc(vh, 1)
                for oc in range(16):
                    rr = tp * 16 + oc
                    v.wait_ge(mmc, rr + 1)
                    v.tensor_scalar(out=dcb[:, rr % 2, :], in0=pdn[:, :],
                                    scalar1=bdcs[:, oc:oc + 1], scalar2=None,
                                    op0=ALU.add).then_inc(vc, 1)
                    v.tensor_tensor(out=tsc[:, rr % 2, :], in0=dcb[:, rr % 2, :],
                                    in1=srep[:, tp * 512:(tp + 1) * 512], op=ALU.mult)
                    v.wait_ge(ldo, 16 * (rr + 1))
                    if rr >= 2:
                        v.wait_ge(stoc if rr % 2 == 0 else stod, 16 * (rr // 2))
                    v.tensor_tensor(out=outs[:, rr % 2, :], in0=tsc[:, rr % 2, :],
                                    in1=o1r[:, rr % 2, :], op=ALU.add).then_inc(vo, 1)

        @block.tensor
        def _(t):
            t.wait_ge(vpro, 1)
            t.wait_ge(gset, 3)
            t.matmul(pg0[0:1, :], ones8[:, 0:1], smm[:, 0:512],
                     start=True, stop=True).then_inc(mmpro, 1)
            t.matmul(pg1[0:1, :], ones8[:, 0:1], smm[:, 512:1024],
                     start=True, stop=True).then_inc(mmpro, 1)
            t.wait_ge(vpro, 3)
            t.matmul(pu0[:, :], onec[0:1, :], srow[0:1, 0:512],
                     start=True, stop=True).then_inc(mmpro, 1)
            t.matmul(pu1[:, :], onec[0:1, :], srow[0:1, 512:1024],
                     start=True, stop=True).then_inc(mmpro, 1)
            t.wait_ge(vpro, 4)
            for tp in range(2):
                for f in range(NIC):
                    r = tp * NIC + f
                    t.wait_ge(ldx, 16 * 16 * (tp + 1))
                    t.wait_ge(ldw0, 32 * (r + 1))
                    if r >= 2:
                        t.wait_ge(vh2, 2 * (r - 1))
                    pg = pg0 if r % 2 == 0 else pg1
                    pu = pu0 if r % 2 == 0 else pu1
                    inst = None
                    for dc in range(16):
                        inst = t.matmul(pg[:, :], wg2[:, r % 2, dc, :], xnp[:, dc, :],
                                        start=(dc == 0), stop=(dc == 15))
                    inst.then_inc(mmg, 1)
                    inst = None
                    for dc in range(16):
                        inst = t.matmul(pu[:, :], wu2[:, r % 2, dc, :], xnp[:, dc, :],
                                        start=(dc == 0), stop=(dc == 15))
                    inst.then_inc(mmu, 1)
                for oc in range(16):
                    rr = tp * 16 + oc
                    t.wait_ge(ldw1a if rr % 2 == 0 else ldw1b, 16 * (rr // 2 + 1))
                    t.wait_ge(vh, NIC * (tp + 1))
                    t.wait_ge(vc, rr)
                    inst = None
                    for ic in range(NIC):
                        inst = t.matmul(pdn[:, :], wd2[:, rr % 2, ic, :],
                                        hsb[:, ic * 512:(ic + 1) * 512],
                                        start=(ic == 0), stop=(ic == NIC - 1))
                    inst.then_inc(mmc, 1)
    al.close()
    return nc


_NC_CACHE = {}


def _get_nc(name, builder):
    if name not in _NC_CACHE:
        _NC_CACHE[name] = builder()
    return _NC_CACHE[name]


_WARMED = set()


def _run(name, builder, in_maps):
    nc = _get_nc(name, builder)
    if name not in _WARMED:
        run_bass_kernel_spmd(nc, in_maps, core_ids=list(range(NCORE)))
        _WARMED.add(name)
    res = run_bass_kernel_spmd(nc, in_maps, core_ids=list(range(NCORE)))
    return res.results


def kernel(**inputs):
    f32 = np.float32
    ca = np.ascontiguousarray
    x = inputs["L0_attn_hidden_states"]
    xf = ca(np.asarray(x, f32).reshape(B * SEQ, HID))
    MT = _rel_bucket_table()
    GMB = _gumbel_const()

    gT = ca(np.asarray(inputs["L0_ffn_gate_idx"]).T).astype(np.int32)
    uT = ca(np.asarray(inputs["L0_ffn_up_idx"]).T).astype(np.int32)
    dT = ca(np.asarray(inputs["L0_ffn_down_idx"]).T).astype(np.int32)
    lut = np.asarray(inputs["L0_ffn_lut"], f32).reshape(1, 16)
    mapsA = []
    for c in range(NCORE):
        sl = slice(1024 * c, 1024 * c + 1024)
        mapsA.append({"x0": ca(xf[sl]),
                      "gT_idx": ca(gT[256 * c:256 * c + 256]),
                      "uT_idx": ca(uT[256 * c:256 * c + 256]),
                      "d_idx": ca(dT[688 * c:688 * c + 688]),
                      "lut": lut})
    resA = _run("A", build_A, mapsA)
    xn_cm = [np.asarray(resA[c]["xn_cm"]) for c in range(NCORE)]
    WgT = np.concatenate([np.asarray(resA[c]["WgT"]) for c in range(NCORE)], 0)
    WuT = np.concatenate([np.asarray(resA[c]["WuT"]) for c in range(NCORE)], 0)
    Wd = np.concatenate([np.asarray(resA[c]["Wd"]) for c in range(NCORE)], 0)
    xn_full = [ca(np.concatenate([xn_cm[2 * b], xn_cm[2 * b + 1]], 1))
               for b in range(B)]

    Wq = np.asarray(inputs["L0_attn_Wq"], f32)
    Wk = np.asarray(inputs["L0_attn_Wk"], f32)
    Wv = np.asarray(inputs["L0_attn_Wv"], f32)
    relb = np.asarray(inputs["L0_attn_rel_bias"], f32)
    mapsB = []
    for c in range(NCORE):
        b, p = c // 2, c % 2
        cs = slice(1024 * p, 1024 * p + 1024)
        mapsB.append({"xn": xn_full[b], "wq": ca(Wq[:, cs]), "wk": ca(Wk[:, cs]),
                      "wv": ca(Wv[:, cs]), "rb": ca(relb[:, 16 * p:16 * p + 16]),
                      "mt": MT})
    resB = _run("B", build_B, mapsB)
    o_full = [np.concatenate([np.asarray(resB[2 * b]["o_cm"]),
                              np.asarray(resB[2 * b + 1]["o_cm"])], 0)
              for b in range(B)]

    Wo0 = np.asarray(inputs["L0_attn_Wo"], f32)
    rw = np.asarray(inputs["L1_ffn_router_w"], f32)
    rbv = np.asarray(inputs["L1_ffn_router_b"], f32).reshape(1, 8)
    mapsC = []
    for c in range(NCORE):
        b, p = c // 2, c % 2
        ts_ = slice(1024 * p, 1024 * p + 1024)
        mapsC.append({"o_cm": ca(o_full[b][:, ts_]), "wo": Wo0,
                      "xn0": xn_cm[c], "WgT": WgT, "WuT": WuT, "Wd": Wd,
                      "x0": ca(xf[1024 * c:1024 * c + 1024]),
                      "rw": rw, "rbv": rbv,
                      "gmb": ca(GMB[1024 * c:1024 * c + 1024])})
    resC = _run("C", build_C, mapsC)

    maas = np.concatenate(
        [ca(np.asarray(inputs["L1_attn_time_maa_" + nm], f32).reshape(16, 128).T)
         for nm in ("x", "w", "k", "v", "r", "g")]
        + [ca(np.asarray(inputs["L1_attn_time_decay"], f32).reshape(16, 128).T)],
        axis=1)
    w1 = np.asarray(inputs["L1_attn_time_maa_w1"], f32)
    w2 = np.asarray(inputs["L1_attn_time_maa_w2"], f32)
    dw1 = np.asarray(inputs["L1_attn_time_decay_w1"], f32)
    dw2 = np.asarray(inputs["L1_attn_time_decay_w2"], f32)
    wrT = ca(np.asarray(inputs["L1_attn_W_r"], f32).T)
    wkT = ca(np.asarray(inputs["L1_attn_W_k"], f32).T)
    wvT = ca(np.asarray(inputs["L1_attn_W_v"], f32).T)
    wgT = ca(np.asarray(inputs["L1_attn_W_g"], f32).T)
    mapsD = []
    for c in range(NCORE):
        b, p = c // 2, c % 2
        if p == 0:
            pcol = np.zeros((128, 16), f32)
        else:
            prev = np.asarray(resC[2 * b]["xn1o"])[:, 1023]
            pcol = ca(prev.reshape(16, 128).T)
        mapsD.append({"xn1b": np.asarray(resC[c]["xn1bo"]), "pcol": pcol,
                      "maas": maas, "w1": w1, "w2": w2, "dw1": dw1, "dw2": dw2,
                      "wrT": wrT, "wkT": wkT, "wvT": wvT, "wgT": wgT})
    resD = _run("D", build_D, mapsD)

    st = np.asarray(inputs["L1_attn_time_state"], f32)
    fa = np.asarray(inputs["L1_attn_time_faaaa"], f32)
    lnw = np.asarray(inputs["L1_attn_ln_w"], f32)
    lnb = np.asarray(inputs["L1_attn_ln_b"], f32)
    mapsE = []
    for c in range(NCORE):
        b, p = c // 2, c % 2
        rows = slice(1024 * p, 1024 * p + 1024)
        mapsE.append({
            "r_i": ca(np.concatenate([np.asarray(resD[2 * b]["r_o"])[rows],
                                      np.asarray(resD[2 * b + 1]["r_o"])[rows]], 1)),
            "k_i": ca(np.concatenate([np.asarray(resD[2 * b]["k_o"])[rows],
                                      np.asarray(resD[2 * b + 1]["k_o"])[rows]], 1)),
            "w_i": ca(np.concatenate([np.asarray(resD[2 * b]["w_o"])[rows],
                                      np.asarray(resD[2 * b + 1]["w_o"])[rows]], 1)),
            "v_i": ca(np.concatenate([np.asarray(resD[2 * b]["v_o"])[:, rows],
                                      np.asarray(resD[2 * b + 1]["v_o"])[:, rows]], 0)),
            "st_i": ca(st[16 * p:16 * p + 16].reshape(1024, 64)),
            "faT_i": ca(fa[16 * p:16 * p + 16].T),
            "lnw_i": ca(lnw[rows].reshape(1, 1024)),
            "lnb_i": ca(lnb[rows].reshape(1, 1024))})
    resE = _run("E", build_E, mapsE)

    woT1 = ca(np.asarray(inputs["L1_attn_W_o"], f32).T)
    yn_cm = [ca(np.concatenate([np.asarray(resE[2 * b]["yn_o"]).T,
                                np.asarray(resE[2 * b + 1]["yn_o"]).T], 0))
             for b in range(B)]
    mapsF1 = []
    for c in range(NCORE):
        b, p = c // 2, c % 2
        ts_ = slice(1024 * p, 1024 * p + 1024)
        mapsF1.append({"yn_i": ca(yn_cm[b][:, ts_]),
                       "g_i": np.asarray(resD[c]["g_o"]),
                       "woT_i": woT1, "x1_i": np.asarray(resC[c]["x1o"])})
    resF1 = _run("F1", build_F1, mapsF1)

    smf = ca(np.concatenate([np.asarray(resC[c]["smo"]) for c in range(NCORE)],
                            0).T)
    wgp = np.zeros((HID, DFF1P), f32)
    wgp[:, :DFF1] = np.asarray(inputs["L1_ffn_w_gate"], f32)
    wup = np.zeros((HID, DFF1P), f32)
    wup[:, :DFF1] = np.asarray(inputs["L1_ffn_w_up"], f32)
    wdp = np.zeros((DFF1P, HID), f32)
    wdp[:DFF1, :] = np.asarray(inputs["L1_ffn_w_down"], f32)
    bgp = np.zeros(DFF1P, f32); bgp[:DFF1] = np.asarray(inputs["L1_ffn_b_gate"], f32)
    bup = np.zeros(DFF1P, f32); bup[:DFF1] = np.asarray(inputs["L1_ffn_b_up"], f32)
    bgc = ca(bgp.reshape(43, 128).T)
    buc = ca(bup.reshape(43, 128).T)
    bdc = ca(np.asarray(inputs["L1_ffn_b_down"], f32).reshape(16, 128).T)
    mapsF2 = []
    for c in range(NCORE):
        mapsF2.append({"xn1b": np.asarray(resC[c]["xn1bo"]),
                       "o1_i": np.asarray(resF1[c]["o1_o"]),
                       "smy_i": ca(smf[:, 1024 * c:1024 * c + 1024]),
                       "smf_i": smf, "wg_i": wgp, "wu_i": wup, "wd_i": wdp,
                       "bgc_i": bgc, "buc_i": buc, "bdc_i": bdc})
    resF2 = _run("F2", build_F2, mapsF2)

    out = np.empty((B * SEQ, HID), f32)
    for c in range(NCORE):
        out[1024 * c:1024 * c + 1024, :] = np.asarray(resF2[c]["out_o"]).T
    return out.reshape(B, SEQ, HID)
